# version 04
# baseline (speedup 1.0000x reference)
import numpy as np
import ml_dtypes
from contextlib import ExitStack

import concourse.bass as bass
import concourse.mybir as mybir
from concourse.bass_utils import run_bass_kernel_spmd

F32 = mybir.dt.float32
BF16 = mybir.dt.bfloat16
F16 = mybir.dt.float16
I8 = mybir.dt.int8
AF = mybir.ActivationFunctionType
ALU = mybir.AluOpType
AX = mybir.AxisListType

D = 4096
T = 1024
H = 32
S = 4096
NT = T // 128
ND = D // 128
NEG = 8
EGW = 512
NC = 8
GRP = 4

SCALE = float(1.0 / np.sqrt(128.0))
LN15 = float(np.log(15.0))

DEBUG = False
PHASES = 4
NO_CC = False


def build_nc():
    nc = bass.Bass("TRN2", target_bir_lowering=False)

    x_in = nc.dram_tensor("x_in", [T, D], F32, kind="ExternalInput")
    wqT = nc.dram_tensor("wqT", [D, D], F32, kind="ExternalInput")
    wkT = nc.dram_tensor("wkT", [D, D], F32, kind="ExternalInput")
    wvT = nc.dram_tensor("wvT", [D, D], F32, kind="ExternalInput")
    woT = nc.dram_tensor("woT", [D, D], F32, kind="ExternalInput")
    wfT = nc.dram_tensor("wfT", [D, D], I8, kind="ExternalInput")
    normrep = nc.dram_tensor("normrep", [D, D], F16, kind="ExternalInput")
    bias_rep = nc.dram_tensor("bias_rep", [128, D], F32, kind="ExternalInput")
    ones_1x128_bf = nc.dram_tensor("ones_1x128_bf", [1, 128], BF16, kind="ExternalInput")
    ones_128x1_bf = nc.dram_tensor("ones_128x1_bf", [128, 1], BF16, kind="ExternalInput")
    ones_128x1_f = nc.dram_tensor("ones_128x1_f", [128, 1], F32, kind="ExternalInput")
    ones_1x128_f = nc.dram_tensor("ones_1x128_f", [1, 128], F32, kind="ExternalInput")
    eye_bf = nc.dram_tensor("eye_bf", [128, 128], BF16, kind="ExternalInput")
    sc128 = nc.dram_tensor("sc128", [128, 128], BF16, kind="ExternalInput")
    eye_f = nc.dram_tensor("eye_f", [128, 128], F32, kind="ExternalInput")
    consts = nc.dram_tensor("consts", [128, 4], F32, kind="ExternalInput")

    out_ext = nc.dram_tensor("out", [T, D], F32, kind="ExternalOutput")

    qT_loc = nc.dram_tensor("qT_loc", [D, T], BF16, kind="Internal")
    kT_loc = nc.dram_tensor("kT_loc", [D, T], BF16, kind="Internal")
    v_loc = nc.dram_tensor("v_loc", [T, D], BF16, kind="Internal")
    knloc = nc.dram_tensor("knloc", [H, T], BF16, kind="Internal")
    qnloc = nc.dram_tensor("qnloc", [H, T], BF16, kind="Internal")
    kg = nc.dram_tensor("kg", [GRP * D, T], BF16, kind="Internal")
    vg = nc.dram_tensor("vg", [S, D], BF16, kind="Internal")
    kng = nc.dram_tensor("kng", [GRP * H, T], BF16, kind="Internal")
    mkd = nc.dram_tensor("mkd", [1, H], F32, kind="Internal")
    x2loc = nc.dram_tensor("x2loc", [T, D], F32, kind="Internal")

    dbg = {}
    if DEBUG:
        dbg["d_qT"] = nc.dram_tensor("d_qT", [D, T], BF16, kind="ExternalOutput")
        dbg["d_kg"] = nc.dram_tensor("d_kg", [GRP * D, T], BF16, kind="ExternalOutput")
        dbg["d_vg"] = nc.dram_tensor("d_vg", [S, D], BF16, kind="ExternalOutput")
        dbg["d_kng"] = nc.dram_tensor("d_kng", [GRP * H, T], BF16, kind="ExternalOutput")
        dbg["d_qn"] = nc.dram_tensor("d_qn", [H, T], BF16, kind="ExternalOutput")
        if PHASES >= 2:
            dbg["d_at"] = nc.dram_tensor("d_at", [128, H * T], BF16, kind="ExternalOutput")
        if PHASES >= 3:
            dbg["d_x2"] = nc.dram_tensor("d_x2", [T, D], F32, kind="ExternalOutput")

    RG = [[0, 1, 2, 3], [4, 5, 6, 7]]

    with ExitStack() as g_ctx:
        def g_sb(name, shape, dt):
            return g_ctx.enter_context(nc.sbuf_tensor(name, shape, dt))

        def g_sem(name):
            return g_ctx.enter_context(nc.semaphore(name))

        panel = [g_sb(f"panel{i}", [128, ND * EGW], BF16) for i in range(2)]
        onec_bf = g_sb("onec_bf", [128, 1], BF16)
        oner_bf = g_sb("oner_bf", [1, 128], BF16)
        onec_f = g_sb("onec_f", [128, 1], F32)
        oner_f = g_sb("oner_f", [1, 128], F32)
        eyeb = g_sb("eyeb", [128, 128], BF16)
        sc128b = g_sb("sc128b", [128, 128], BF16)
        eyef = g_sb("eyef", [128, 128], F32)
        csb = g_sb("csb", [128, 4], F32)

        names = ("c_ld bar x_ld ln_bn ln_mv ln_rstd ln_apply tp_pe tp_ev "
                 "pan0 pan1 x0 x1 pfs0 pfs1 pcv sg0 sg1 sg2 sg3 sg4 sg5 sg6 sg7 pj_mm pj_ev sq_tt n_mm n_ev knst qnst kst vst qst kdone vdone kndone cc_s "
                 "pfs pfv mk_ld mk_dve mk_pe ng sp ex dn av pd rc pr rb ev2 "
                 "x3a x3b x2a x2b o_mm o_ev nrm dq f_mm f_ev x4 xg4a xg4b bl4a bl4b o4a o4b dbg_st").split()
        sm = {n: g_sem(n) for n in names}

        PASSES = [("k", wkT, kT_loc), ("v", wvT, v_loc), ("q", wqT, qT_loc)]

        SWP = [0, 1] + list(range(24, 40))

        def pan_w(eng, k):
            if 2 <= k < 24:
                eng.wait_ge(sm["pcv"], 32 * (k - 1))
            else:
                i = SWP.index(k)
                par = [j for j in SWP if j % 2 == k % 2].index(k)
                eng.wait_ge(sm["pan0" if k % 2 == 0 else "pan1"], 16 * (par + 1))

        def pjg_cnt(pi, eg, gi):
            return pi * NEG * 8 + eg * 8 + gi + 1

        def slab_guard(eng, k):
            j = k - 1
            if k - 2 < 8:
                eng.wait_ge(sm["kst"], 64 * min(j + 1, 8))
            elif k - 2 < 16:
                eng.wait_ge(sm["vst"], 16 * (min(j, 15) - 7))
            else:
                eng.wait_ge(sm["qst"], 64 * (j - 15))

        with ExitStack() as ctx:
            def sb(name, shape, dt):
                return ctx.enter_context(nc.sbuf_tensor(name, shape, dt))

            def ps(name, shape, dt):
                return ctx.enter_context(nc.psum_tensor(name, shape, dt))

            xt = [sb(f"xt{i}", [128, D], F32) for i in range(2)]
            lnbf = [sb(f"lnbf{i}", [128, D], BF16) for i in range(2)]
            lnT = sb("lnT", [128, ND * 1024], BF16)
            stats = [sb(f"stats{i}", [128, 48], F32) for i in range(2)]
            mv = [sb(f"mv{i}", [128, 2], F32) for i in range(2)]
            lnv = [sb(f"lnv{i}", [128, 1], F32) for i in range(2)]
            rstd = [sb(f"rstd{i}", [128, 1], F32) for i in range(2)]
            slab = [sb(f"slab{i}", [128, 4096], BF16) for i in range(2)]
            sqt = [sb(f"sqt{i}", [128, 1024], BF16) for i in range(2)]
            nst = [sb(f"nst{i}", [1, 1024], BF16) for i in range(4)]
            stg = [lnbf[i // 4].ap().bitcast(F32)[:, (i % 4) * 512:(i % 4) * 512 + 512]
                   for i in range(8)]
            pj_ps = [ps(f"pj_ps{i}", [128, 512], F32) for i in range(2)]
            kn_ps = ps("kn_ps", [1, 1024], F32)
            tp_ps = [ps(f"tp_ps{i}", [128, 128], BF16) for i in range(2)]

            blk = ctx.enter_context(nc.Block("p1"))

            @blk.sync
            def _(s):
                s.dma_start(out=onec_bf[:, :], in_=ones_128x1_bf[:, :]).then_inc(sm["c_ld"], 16)
                s.dma_start(out=oner_bf[:, :], in_=ones_1x128_bf[:, :]).then_inc(sm["c_ld"], 16)
                s.dma_start(out=onec_f[:, :], in_=ones_128x1_f[:, :]).then_inc(sm["c_ld"], 16)
                s.dma_start(out=oner_f[:, :], in_=ones_1x128_f[:, :]).then_inc(sm["c_ld"], 16)
                s.dma_start(out=eyeb[:, :], in_=eye_bf[:, :]).then_inc(sm["c_ld"], 16)
                s.dma_start(out=sc128b[:, :], in_=sc128[:, :]).then_inc(sm["c_ld"], 16)
                s.dma_start(out=eyef[:, :], in_=eye_f[:, :]).then_inc(sm["c_ld"], 16)
                s.dma_start(out=csb[:, :], in_=consts[:, :]).then_inc(sm["c_ld"], 16)
                for i in range(NT):
                    if i >= 2:
                        s.wait_ge(sm["ln_apply"], i - 1)
                    s.dma_start(out=xt[i % 2][:, :],
                                in_=x_in[128 * i:128 * (i + 1), :]).then_inc(
                                    sm["x0" if i % 2 == 0 else "x1"], 16)
                for pi, (pname, _w, locT) in enumerate(PASSES):
                    stsem = {"k": "kst", "v": "vst", "q": "qst"}[pname]
                    for eg in range(NEG):
                        k = pi * NEG + eg
                        s.wait_ge(sm["pj_ev"], pjg_cnt(pi, eg, 7))
                        if pname in ("k", "q"):
                            for ec in range(4):
                                r0 = eg * 512 + ec * 128
                                s.dma_start(out=locT[r0:r0 + 128, :],
                                            in_=slab[eg % 2][:, ec * 1024:(ec + 1) * 1024]
                                            ).then_inc(sm[stsem], 16)
                        else:
                            dst = v_loc.ap().rearrange("(ts p) e -> p ts e", p=128)[
                                :, :, eg * EGW:(eg + 1) * EGW]
                            src = slab[eg % 2].ap().rearrange("p (ts e) -> p ts e", e=EGW)
                            s.dma_start(out=dst, in_=src).then_inc(sm[stsem], 16)
                    if pname == "k":
                        s.wait_ge(sm["kst"], 64 * 8)
                        s.nop().then_inc(sm["kdone"], 1)
                    elif pname == "v":
                        s.wait_ge(sm["vst"], 16 * 8)
                        s.nop().then_inc(sm["vdone"], 1)
                s.wait_ge(sm["qst"], 64 * 8)
                s.nop().then_inc(sm["bar"], 1)

            @blk.gpsimd
            def _(g):
                for k in range(24):
                    pi, eg = divmod(k, NEG)
                    wT = PASSES[pi][1]
                    if k < 2:
                        src = wT.ap().rearrange("(d p) e -> p d e", p=128)[
                            :, :, eg * EGW:(eg + 1) * EGW]
                        dst = panel[k % 2].ap().rearrange("p (d e) -> p d e", e=EGW)
                        g.dma_start(out=dst, in_=src).then_inc(
                            sm["pan0" if k % 2 == 0 else "pan1"], 16)
                    else:
                        g.wait_ge(sm["pj_mm"], pjg_cnt(*divmod(k - 2, NEG), 7))
                        for d in range(ND):
                            j = (k - 2) * ND + d
                            g.wait_ge(sm[f"sg{j % 8}"], 16 * (j // 8 + 1))
                            g.tensor_copy(panel[k % 2][:, d * EGW:(d + 1) * EGW],
                                          stg[j % 8]).then_inc(sm["pcv"], 1)
                    if k == 9:
                        g.wait_ge(sm["kdone"], 1)
                        g.wait_ge(sm["kndone"], 1)
                        if NO_CC:
                            g.dma_start(out=kg[0:D, :], in_=kT_loc[:, :]).then_inc(sm["cc_s"], 16)
                            g.dma_start(out=kng[0:H, :], in_=knloc[:, :]).then_inc(sm["cc_s"], 16)
                            g.wait_ge(sm["cc_s"], 32)
                        else:
                            g.collective_compute(
                                "AllGather", ALU.bypass, replica_groups=RG,
                                ins=[kT_loc.ap().opt()], outs=[kg.ap().opt()],
                            ).then_inc(sm["cc_s"], 1)
                            g.collective_compute(
                                "AllGather", ALU.bypass, replica_groups=RG,
                                ins=[knloc.ap().opt()], outs=[kng.ap().opt()],
                            ).then_inc(sm["cc_s"], 1)
                    if k == 17:
                        g.wait_ge(sm["vdone"], 1)
                        if NO_CC:
                            g.dma_start(out=vg[0:T, :], in_=v_loc[:, :]).then_inc(sm["cc_s"], 16)
                            g.wait_ge(sm["cc_s"], 48)
                        else:
                            g.collective_compute(
                                "AllGather", ALU.bypass, replica_groups=RG,
                                ins=[v_loc.ap().opt()], outs=[vg.ap().opt()],
                            ).then_inc(sm["cc_s"], 1)
                if DEBUG:
                    g.wait_ge(sm["cc_s"], 48 if NO_CC else 3)
                    g.wait_ge(sm["qst"], 64 * 8)
                    g.wait_ge(sm["qnst"], 16 * H)
                    g.dma_start(out=dbg["d_qT"][:, :], in_=qT_loc[:, :]).then_inc(sm["dbg_st"], 16)
                    g.dma_start(out=dbg["d_kg"][:, :], in_=kg[:, :]).then_inc(sm["dbg_st"], 16)
                    g.dma_start(out=dbg["d_vg"][:, :], in_=vg[:, :]).then_inc(sm["dbg_st"], 16)
                    g.dma_start(out=dbg["d_kng"][:, :], in_=kng[:, :]).then_inc(sm["dbg_st"], 16)
                    g.dma_start(out=dbg["d_qn"][:, :], in_=qnloc[:, :]).then_inc(sm["dbg_st"], 16)
                g.nop().then_inc(sm["bar"], 1)

            @blk.tensor
            def _(t):
                t.wait_ge(sm["c_ld"], 16 * 8)
                for i in range(NT):
                    t.wait_ge(sm["ln_apply"], i + 1)
                    for d in range(ND):
                        j = i * ND + d
                        if j >= 2:
                            t.wait_ge(sm["tp_ev"], j - 1)
                        t.transpose(tp_ps[j % 2][:, :],
                                    lnbf[i % 2][:, 128 * d:128 * (d + 1)],
                                    eyeb[:, :]).then_inc(sm["tp_pe"], 1)
                t.wait_ge(sm["tp_ev"], NT * ND)
                for pi, (pname, _w, _l) in enumerate(PASSES):
                    pend = []
                    for eg in range(NEG):
                        k = pi * NEG + eg
                        pan_w(t, k)
                        for gi in range(8):
                            gidx = pi * NEG * 8 + eg * 8 + gi
                            if gidx >= 2:
                                t.wait_ge(sm["pj_ev"], gidx - 1)
                            if pname in ("k", "q"):
                                ec, th = divmod(gi, 2)
                                lastm = None
                                for d in range(ND):
                                    lastm = t.matmul(
                                        pj_ps[gidx % 2][:, :],
                                        panel[k % 2][:, d * EGW + ec * 128:d * EGW + (ec + 1) * 128],
                                        lnT[:, d * 1024 + th * 512:d * 1024 + (th + 1) * 512],
                                        start=(d == 0), stop=(d == ND - 1))
                            else:
                                ts_ = gi
                                for d in range(ND):
                                    lastm = t.matmul(
                                        pj_ps[gidx % 2][:, :],
                                        lnT[:, d * 1024 + ts_ * 128:d * 1024 + (ts_ + 1) * 128],
                                        panel[k % 2][:, d * EGW:(d + 1) * EGW],
                                        start=(d == 0), stop=(d == ND - 1))
                            lastm.then_inc(sm["pj_mm"], 1)
                            if pname in ("k", "q") and pend:
                                h = pend.pop(0)
                                gh = (0 if pname == "k" else H) + h
                                t.wait_ge(sm["sq_tt"], (gh % H) * 2 + 2 + (0 if pname == "k" else 2 * H))
                                if gh >= 1:
                                    t.wait_ge(sm["n_ev"], gh)
                                lm = None
                                for th2 in range(2):
                                    lm = t.matmul(kn_ps[:, th2 * 512:(th2 + 1) * 512],
                                                  onec_bf[:, :],
                                                  sqt[h % 2][:, th2 * 512:(th2 + 1) * 512],
                                                  start=True, stop=True)
                                lm.then_inc(sm["n_mm"], 1)
                            if pname in ("k", "q") and gi % 2 == 1:
                                pend.append(eg * 4 + gi // 2)
                    for h in pend:
                        gh = (0 if pname == "k" else H) + h
                        t.wait_ge(sm["sq_tt"], (gh % H) * 2 + 2 + (0 if pname == "k" else 2 * H))
                        if gh >= 1:
                            t.wait_ge(sm["n_ev"], gh)
                        lm = None
                        for th2 in range(2):
                            lm = t.matmul(kn_ps[:, th2 * 512:(th2 + 1) * 512],
                                          onec_bf[:, :],
                                          sqt[h % 2][:, th2 * 512:(th2 + 1) * 512],
                                          start=True, stop=True)
                        lm.then_inc(sm["n_mm"], 1)
                    pend.clear()
                t.nop().then_inc(sm["bar"], 1)

            @blk.scalar
            def _(sc):
                sc.wait_ge(sm["c_ld"], 16 * 8)
                for i in range(NT):
                    sc.wait_ge(sm["ln_mv"], i + 1)
                    sc.activation(lnv[i % 2][:, :], mv[i % 2][:, 1:2], AF.Ln,
                                  scale=1.0, bias=csb[:, 0:1]).then_inc(sm["ln_rstd"], 1)
                    sc.wait_ge(sm["ln_rstd"], 2 * i + 1)
                    sc.activation(rstd[i % 2][:, :], lnv[i % 2][:, :], AF.Exp,
                                  scale=-0.5).then_inc(sm["ln_rstd"], 1)
                def norm_store(gh):
                    pnameh = "k" if gh < H else "q"
                    h = gh % H
                    nloc = knloc if pnameh == "k" else qnloc
                    nsem = "knst" if pnameh == "k" else "qnst"
                    sc.wait_ge(sm["n_ev"], gh + 1)
                    sc.dma_start(out=nloc[h:h + 1, :],
                                 in_=nst[gh % 4][:, :]).then_inc(sm[nsem], 16)
                    if gh == H - 1:
                        sc.wait_ge(sm["knst"], 16 * H)
                        sc.nop().then_inc(sm["kndone"], 1)

                sc.wait_ge(sm["tp_ev"], NT * ND)
                for k in range(2, 24):
                    pi, eg = divmod(k, NEG)
                    wT = PASSES[pi][1]
                    kk = k - 3
                    if 0 <= kk < 8:
                        for ec in range(4):
                            norm_store(kk * 4 + ec)
                    elif kk >= 16:
                        for ec in range(4):
                            norm_store(H + (kk - 16) * 4 + ec)
                    for d in range(ND):
                        j = (k - 2) * ND + d
                        if j >= 8:
                            sc.wait_ge(sm["pcv"], j - 7)
                        srcp = wT.ap().rearrange("(d p) e -> p d e", p=128)[
                            :, d:d + 1, eg * EGW:(eg + 1) * EGW]
                        sc.dma_start(out=stg[j % 8].rearrange("p e -> p () e"),
                                     in_=srcp).then_inc(sm[f"sg{j % 8}"], 16)
                for gh in range(H + 5 * 4, 2 * H):
                    norm_store(gh)
                sc.wait_ge(sm["qnst"], 16 * H)
                sc.nop().then_inc(sm["bar"], 1)

            @blk.vector
            def _(v):
                for i in range(NT + 1):
                    if i < NT:
                        v.wait_ge(sm["x0" if i % 2 == 0 else "x1"], 16 * (i // 2 + 1))
                        ins = None
                        for j in range(8):
                            ins = v.bn_stats(stats[i % 2][:, 6 * j:6 * j + 6],
                                             xt[i % 2][:, 512 * j:512 * (j + 1)])
                        ins.then_inc(sm["ln_bn"], 1)
                        v.wait_ge(sm["ln_bn"], i + 1)
                        v.bn_aggr(mv[i % 2][:, :], stats[i % 2][:, :]).then_inc(sm["ln_mv"], 1)
                        v.wait_ge(sm["ln_rstd"], 2 * i + 2)
                        if i >= 2:
                            v.wait_ge(sm["tp_pe"], 32 * (i - 1))
                        v.tensor_scalar(lnbf[i % 2][:, :], xt[i % 2][:, :],
                                        mv[i % 2][:, 0:1], rstd[i % 2][:, :],
                                        ALU.subtract, ALU.mult).then_inc(sm["ln_apply"], 1)
                    if i >= 1:
                        ti = i - 1
                        for d in range(ND):
                            j = ti * ND + d
                            v.wait_ge(sm["tp_pe"], j + 1)
                            v.tensor_copy(lnT[:, d * 1024 + ti * 128:d * 1024 + (ti + 1) * 128],
                                          tp_ps[j % 2][:, :]).then_inc(sm["tp_ev"], 1)
                def nevac(h, gh):
                    v.wait_ge(sm["n_mm"], gh + 1)
                    if gh >= 4:
                        if gh - 4 < H:
                            v.wait_ge(sm["knst"], 64 * min(gh // 4, 8))
                        else:
                            v.wait_ge(sm["qnst"], 64 * ((gh - H) // 4))
                    v.tensor_copy(nst[gh % 4][:, :], kn_ps[:, :]).then_inc(sm["n_ev"], 1)

                for pi, (pname, _w, _l) in enumerate(PASSES):
                    noff = 0 if pname == "k" else H
                    for eg in range(NEG):
                        k = pi * NEG + eg
                        for gi in range(8):
                            gidx = pi * NEG * 8 + eg * 8 + gi
                            v.wait_ge(sm["pj_mm"], gidx + 1)
                            if gi == 0 and k >= 2:
                                slab_guard(v, k)
                            if pname in ("k", "q"):
                                ec, th = divmod(gi, 2)
                                dstc = ec * 1024 + th * 512
                            else:
                                dstc = gi * 512
                            v.tensor_copy(slab[eg % 2][:, dstc:dstc + 512],
                                          pj_ps[gidx % 2][:, :]).then_inc(sm["pj_ev"], 1)
                            if pname in ("k", "q") and gi % 2 == 1:
                                ec = gi // 2
                                h = eg * 4 + ec
                                gh = noff + h
                                v.wait_ge(sm["pj_ev"], gidx + 1)
                                if gh >= 2:
                                    v.wait_ge(sm["n_mm"], gh - 1)
                                v.tensor_tensor(sqt[h % 2][:, 0:512],
                                                slab[eg % 2][:, ec * 1024:ec * 1024 + 512],
                                                slab[eg % 2][:, ec * 1024:ec * 1024 + 512],
                                                ALU.mult).then_inc(sm["sq_tt"], 1)
                                v.tensor_tensor(sqt[h % 2][:, 512:1024],
                                                slab[eg % 2][:, ec * 1024 + 512:ec * 1024 + 1024],
                                                slab[eg % 2][:, ec * 1024 + 512:ec * 1024 + 1024],
                                                ALU.mult).then_inc(sm["sq_tt"], 1)
                                if ec >= 1:
                                    nevac(eg * 4 + ec - 1, noff + eg * 4 + ec - 1)
                        if pname in ("k", "q"):
                            nevac(eg * 4 + 3, noff + eg * 4 + 3)
                v.nop().then_inc(sm["bar"], 1)


        CC1, CC2, CC3 = (16, 32, 48) if NO_CC else (1, 2, 3)
        if PHASES >= 2:
            with ExitStack() as p23:
                attnT = p23.enter_context(
                    nc.sbuf_tensor("attnT", [128, H * 1024], BF16))
                with ExitStack() as ctx:
                    def sb(name, shape, dt):
                        return ctx.enter_context(nc.sbuf_tensor(name, shape, dt))

                    def ps(name, shape, dt):
                        return ctx.enter_context(nc.psum_tensor(name, shape, dt))

                    khb = [sb(f"khb{i}", [128, 4 * 1024], BF16) for i in range(2)]
                    vhb = [sb(f"vhb{i}", [128, H * 128], BF16) for i in range(2)]
                    qhb = [sb(f"qhb{i}", [128, 1024], BF16) for i in range(2)]
                    qnb = [sb(f"qnb{i}", [128, 1024], BF16) for i in range(2)]
                    negm = [sb(f"negm{i}", [128, 1024], BF16) for i in range(2)]
                    ptb = [sb(f"ptb{i}", [128, 1024], BF16) for i in range(2)]
                    denv = [sb(f"denv{i}", [128, 1024], F32) for i in range(2)]
                    denvb = [sb(f"denvb{i}", [128, 1024], BF16) for i in range(2)]
                    kn4 = sb("kn4", [H, 2 * 1024], BF16)
                    mk2 = sb("mk2", [H, 1], F32)
                    mk2b = sb("mk2b", [H, 1], F32)
                    mk2n = sb("mk2n", [128, H], F32)
                    r_sb = sb("r_sb", [1, 512], F32)
                    rbc_sb = sb("rbc_sb", [128, 512], F32)
                    s_ps = [ps(f"s_ps{i}", [128, 1024], F32) for i in range(2)]
                    acc_ps = [ps(f"acc_ps{i}", [128, 512], F32) for i in range(3)]
                    dr_ps = ps("dr_ps", [128, 512], F32)
                    den_ps = dr_ps[0:1, :]
                    rbc_ps = dr_ps

                    blk = ctx.enter_context(nc.Block("p2"))

                    @blk.sync
                    def _(s):
                        s.wait_ge(sm["bar"], 5)
                        s.wait_ge(sm["cc_s"], CC2)
                        kngv = kng.ap().rearrange("(r h) t -> h r t", r=GRP)
                        kn4v = kn4.ap().rearrange("h (r t) -> h r t", t=T)
                        s.dma_start(out=kn4v, in_=kngv[:, 0:2, :]).then_inc(sm["mk_ld"], 16)
                        s.wait_ge(sm["mk_dve"], 1)
                        s.dma_start(out=kn4v, in_=kngv[:, 2:4, :]).then_inc(sm["mk_ld"], 16)
                        s.wait_ge(sm["mk_dve"], 3)
                        s.dma_start(out=mkd.ap().rearrange("a b -> b a"),
                                    in_=mk2[:, :]).then_inc(sm["mk_ld"], 16)
                        s.wait_ge(sm["mk_ld"], 48)
                        s.dma_start(out=mk2n.ap().rearrange("p h -> p () h"),
                                    in_=mkd.ap().partition_broadcast(128)
                                    ).then_inc(sm["mk_ld"], 16)
                        s.wait_ge(sm["qst"], 64 * 8)
                        s.wait_ge(sm["qnst"], 16 * H)
                        s.wait_ge(sm["cc_s"], CC3)
                        for h in range(H):
                            if h >= 2:
                                s.wait_ge(sm["sp"], 32 * (h - 1))
                                s.wait_ge(sm["av"], 32 * (h - 1))
                                s.wait_ge(sm["ng"], h - 1)
                            pf = sm["pfs0" if h % 2 == 0 else "pfs1"]
                            s.dma_start(
                                out=khb[h % 2].ap().rearrange("p (r t) -> p r t", t=T),
                                in_=kg.ap().rearrange("(r hh p) t -> p r hh t", r=GRP, hh=H)[:, :, h, :],
                            ).then_inc(pf, 16)
                            s.dma_start(out=qhb[h % 2][:, :],
                                        in_=qT_loc[h * 128:(h + 1) * 128, :]).then_inc(pf, 16)
                            s.dma_start(out=qnb[h % 2].ap().rearrange("p t -> p () t"),
                                        in_=qnloc[h:h + 1, :].partition_broadcast(128)
                                        ).then_inc(pf, 16)
                            s.dma_start(
                                out=vhb[h % 2].ap().rearrange("p (c e) -> p c e", e=128),
                                in_=vg.ap().rearrange("(c p) e -> p c e", p=128)[:, :, h * 128:(h + 1) * 128],
                            ).then_inc(pf, 16)
                        s.wait_ge(sm["pfs0"], 64 * 16)
                        s.wait_ge(sm["pfs1"], 64 * 16)
                        s.nop().then_inc(sm["bar"], 1)

                    def eidx(h, qg, p):
                        return h * 32 + 2 * p + qg

                    @blk.scalar
                    def _(sc):
                        sc.wait_ge(sm["bar"], 5)
                        for h in range(H):
                            for p in range(16):
                                for qg in range(2):
                                    g = eidx(h, qg, p)
                                    sc.wait_ge(sm["sp"], g + 1)
                                    if g >= 2:
                                        sc.wait_ge(sm["av"], g - 1)
                                        sc.wait_ge(sm["dn"], g - 1)
                                    sc.activation(ptb[qg][:, :], s_ps[qg][:, :],
                                                  AF.Exp, scale=SCALE).then_inc(sm["ex"], 1)
                        sc.nop().then_inc(sm["bar"], 1)

                    @blk.tensor
                    def _(t):
                        t.wait_ge(sm["bar"], 5)

                        def accb(h, qg):
                            return acc_ps[0] if qg == 0 else acc_ps[1 + h % 2]

                        def av_pair(h, qg, p):
                            g = eidx(h, qg, p)
                            t.wait_ge(sm["ex"], g + 1)
                            if p == 0 and h >= 1:
                                t.wait_ge(sm["ev2"], 2 * h - 1 if qg == 0 else max(2 * h - 2, 0))
                            lm = None
                            for c in (2 * p, 2 * p + 1):
                                lm = t.matmul(accb(h, qg)[:, :],
                                              vhb[h % 2][:, c * 128:(c + 1) * 128],
                                              ptb[qg][:, (c % 2) * 512:(c % 2) * 512 + 512],
                                              start=(c == 0), stop=(c == 31))
                            lm.then_inc(sm["av"], 1)

                        def den_rbc(h, qg, part):
                            hq = 2 * h + qg
                            if part == 0:
                                t.wait_ge(sm["dn"], h * 32 + 31 + qg)
                                t.wait_ge(sm["mk_pe"], hq + 1)
                                if hq >= 1:
                                    t.wait_ge(sm["rb"], hq)
                                t.matmul(den_ps[:, :], onec_bf[:, :], denvb[qg][:, 0:512],
                                         start=True, stop=False)
                                t.matmul(den_ps[:, :], onec_bf[:, :], denvb[qg][:, 512:1024],
                                         start=False, stop=True).then_inc(sm["pd"], 1)
                            else:
                                t.wait_ge(sm["rc"], hq + 1)
                                if hq >= 1:
                                    t.wait_ge(sm["rb"], hq)
                                t.matmul(rbc_ps[:, :], oner_f[:, :], r_sb[:, :],
                                         start=True, stop=True).then_inc(sm["pr"], 1)

                        pend = []
                        for h in range(H):
                            t.wait_ge(sm["ng"], h + 1)
                            t.wait_ge(sm["pfs0" if h % 2 == 0 else "pfs1"],
                                      64 * (h // 2 + 1))
                            for p in range(16):
                                for qg in range(2):
                                    g = eidx(h, qg, p)
                                    if p >= 1:
                                        t.wait_ge(sm["ex"], eidx(h, qg, p - 1) + 1)
                                    elif h >= 1:
                                        t.wait_ge(sm["ex"], eidx(h - 1, qg, 15) + 1)
                                    lm = None
                                    for c in (2 * p, 2 * p + 1):
                                        t.matmul(s_ps[qg][:, (c % 2) * 512:(c % 2) * 512 + 512],
                                                 khb[h % 2][:, c * 128:(c + 1) * 128],
                                                 qhb[h % 2][:, qg * 512:(qg + 1) * 512],
                                                 start=True, stop=False)
                                        lm = t.matmul(s_ps[qg][:, (c % 2) * 512:(c % 2) * 512 + 512],
                                                      sc128b[:, :],
                                                      negm[h % 2][:, qg * 512:(qg + 1) * 512],
                                                      start=False, stop=True)
                                    lm.then_inc(sm["sp"], 1)
                                for _ in range(2):
                                    if pend:
                                        den_rbc(*pend.pop(0))
                                for qg in range(2):
                                    if p >= 1:
                                        av_pair(h, qg, p - 1)
                            for qg in range(2):
                                av_pair(h, qg, 15)
                            pend += [(h, 0, 0), (h, 0, 1), (h, 1, 0), (h, 1, 1)]
                        for job in pend:
                            den_rbc(*job)
                        t.nop().then_inc(sm["bar"], 1)

                    @blk.vector
                    def _(v):
                        v.wait_ge(sm["bar"], 5)
                        v.wait_ge(sm["mk_ld"], 16)
                        v.reduce_max(mk2[:, :], kn4[:, :], axis=AX.X).then_inc(sm["mk_dve"], 1)
                        v.wait_ge(sm["mk_ld"], 32)
                        v.reduce_max(mk2b[:, :], kn4[:, :], axis=AX.X).then_inc(sm["mk_dve"], 1)
                        v.wait_ge(sm["mk_dve"], 2)
                        v.tensor_tensor(mk2[:, :], mk2[:, :], mk2b[:, :],
                                        ALU.max).then_inc(sm["mk_dve"], 1)
                        v.wait_ge(sm["mk_ld"], 64)
                        v.tensor_scalar(mk2n[:, :], mk2n[:, :], -0.5, None,
                                        ALU.mult).then_inc(sm["mk_dve"], 1)

                        def fin(h, qg):
                            hq = 2 * h + qg
                            v.wait_ge(sm["pd"], hq + 1)
                            if hq >= 1:
                                v.wait_ge(sm["pr"], hq)
                            v.reciprocal(r_sb[:, :], den_ps[:, :]).then_inc(sm["rc"], 1)
                            v.wait_ge(sm["pr"], hq + 1)
                            v.tensor_copy(rbc_sb[:, :], rbc_ps[:, :]).then_inc(sm["rb"], 1)
                            v.wait_ge(sm["rb"], hq + 1)
                            c0 = h * 1024 + qg * 512
                            accq = acc_ps[0] if qg == 0 else acc_ps[1 + h % 2]
                            v.tensor_tensor(attnT[:, c0:c0 + 512],
                                            accq[:, :], rbc_sb[:, :],
                                            ALU.mult).then_inc(sm["ev2"], 1)

                        for h in range(H):
                            v.wait_ge(sm["pfs0" if h % 2 == 0 else "pfs1"],
                                      64 * (h // 2 + 1))
                            if h >= 2:
                                v.wait_ge(sm["sp"], 32 * (h - 1))
                            if h == 0:
                                v.wait_ge(sm["mk_dve"], 4)
                            v.tensor_scalar(negm[h % 2][:, :], qnb[h % 2][:, :],
                                            -0.5, mk2n[:, h:h + 1],
                                            ALU.mult, ALU.add).then_inc(sm["ng"], 1)
                            if h >= 1:
                                fin(h - 1, 0)
                                fin(h - 1, 1)
                            for p in range(16):
                                for qg in range(2):
                                    g = eidx(h, qg, p)
                                    v.wait_ge(sm["ex"], g + 1)
                                    if p == 0:
                                        hq = 2 * h + qg
                                        if hq >= 2:
                                            v.wait_ge(sm["pd"], hq - 1)
                                        v.tensor_copy(denv[qg][:, :], ptb[qg][:, :]
                                                      ).then_inc(sm["dn"], 1)
                                    else:
                                        v.wait_ge(sm["dn"], g)
                                        lastd = v.tensor_tensor(denv[qg][:, :], denv[qg][:, :],
                                                                ptb[qg][:, :], ALU.add)
                                        lastd.then_inc(sm["dn"], 1)
                                        if p == 15:
                                            v.wait_ge(sm["dn"], g + 1)
                                            v.tensor_copy(denvb[qg][:, :], denv[qg][:, :]
                                                          ).then_inc(sm["mk_pe"], 1)
                        fin(H - 1, 0)
                        fin(H - 1, 1)
                        v.nop().then_inc(sm["bar"], 1)

                    @blk.gpsimd
                    def _(g):
                        g.wait_ge(sm["bar"], 5)
                        if PHASES >= 3:
                            for og in range(2):
                                k = 24 + og
                                g.wait_ge(sm["pj_mm"], pjg_cnt(2, 6 + og, 7))
                                srcp = woT.ap().rearrange("(d p) e -> p d e", p=128)[
                                    :, :, og * EGW:(og + 1) * EGW]
                                dstp = panel[k % 2].ap().rearrange("p (d e) -> p d e", e=EGW)
                                g.dma_start(out=dstp, in_=srcp).then_inc(
                                    sm["pan0" if k % 2 == 0 else "pan1"], 16)
                        if DEBUG:
                            g.wait_ge(sm["ev2"], 2 * H)
                            g.dma_start(out=dbg["d_at"][:, :], in_=attnT[:, :]
                                        ).then_inc(sm["dbg_st"], 16)
                            g.wait_ge(sm["dbg_st"], 16 * 6)
                        g.nop().then_inc(sm["bar"], 1)


                if PHASES >= 3:
                    with ExitStack() as ctx:
                        def sb(name, shape, dt):
                            return ctx.enter_context(nc.sbuf_tensor(name, shape, dt))

                        def ps(name, shape, dt):
                            return ctx.enter_context(nc.psum_tensor(name, shape, dt))

                        xog = [sb(f"xog{i}", [128, 8 * 512], F32) for i in range(2)]
                        piece = [sb(f"piece{i}", [128, 512], F32) for i in range(2)]
                        o_ps = [ps(f"o_ps{i}", [128, 512], F32) for i in range(2)]

                        blk = ctx.enter_context(nc.Block("p3"))

                        @blk.sync
                        def _(s):
                            s.wait_ge(sm["bar"], 10)
                            for og in range(NEG):
                                if og >= 2:
                                    s.wait_ge(sm["o_ev"], 8 * (og - 1))
                                s.dma_start(
                                    out=xog[og % 2].ap().rearrange("p (ts e) -> p ts e", e=EGW),
                                    in_=x_in.ap().rearrange("(ts p) e -> p ts e", p=128)[
                                        :, :, og * EGW:(og + 1) * EGW],
                                ).then_inc(sm["x3a" if og % 2 == 0 else "x3b"], 16)
                                for ts_ in range(8):
                                    idx = og * 8 + ts_
                                    s.wait_ge(sm["o_ev"], idx + 1)
                                    dst = x2loc.ap().rearrange("(ts p) e -> p ts e", p=128)[
                                        :, ts_:ts_ + 1, og * EGW:(og + 1) * EGW]
                                    s.dma_start(out=dst,
                                                in_=piece[idx % 2].ap().rearrange(
                                                    "p (one e) -> p one e", one=1)
                                                ).then_inc(sm["x2a" if idx % 2 == 0 else "x2b"], 16)
                            s.wait_ge(sm["x2a"], 16 * 32)
                            s.wait_ge(sm["x2b"], 16 * 32)
                            s.wait_ge(sm["x3a"], 16 * 4)
                            s.wait_ge(sm["x3b"], 16 * 4)
                            s.nop().then_inc(sm["bar"], 1)

                        @blk.tensor
                        def _(t):
                            t.wait_ge(sm["bar"], 10)
                            for og in range(NEG):
                                pan_w(t, 24 + og)
                                for ts_ in range(8):
                                    gi = og * 8 + ts_
                                    if gi >= 2:
                                        t.wait_ge(sm["o_ev"], gi - 1)
                                    lm = None
                                    for e in range(ND):
                                        lm = t.matmul(
                                            o_ps[gi % 2][:, :],
                                            attnT[:, e * 1024 + ts_ * 128:e * 1024 + (ts_ + 1) * 128],
                                            panel[(24 + og) % 2][:, e * EGW:(e + 1) * EGW],
                                            start=(e == 0), stop=(e == ND - 1))
                                    lm.then_inc(sm["o_mm"], 1)
                            t.nop().then_inc(sm["bar"], 1)

                        @blk.vector
                        def _(v):
                            v.wait_ge(sm["bar"], 10)
                            for og in range(NEG):
                                v.wait_ge(sm["x3a" if og % 2 == 0 else "x3b"],
                                          16 * (og // 2 + 1))
                                for ts_ in range(8):
                                    idx = og * 8 + ts_
                                    v.wait_ge(sm["o_mm"], idx + 1)
                                    if idx >= 2:
                                        v.wait_ge(sm["x2a" if idx % 2 == 0 else "x2b"],
                                                  16 * (idx // 2))
                                    v.tensor_tensor(piece[idx % 2][:, :],
                                                    o_ps[idx % 2][:, :],
                                                    xog[og % 2][:, ts_ * 512:(ts_ + 1) * 512],
                                                    ALU.add).then_inc(sm["o_ev"], 1)
                            v.nop().then_inc(sm["bar"], 1)

                        @blk.scalar
                        def _(sc):
                            sc.wait_ge(sm["bar"], 10)
                            sc.nop().then_inc(sm["bar"], 1)

                        @blk.gpsimd
                        def _(g):
                            g.wait_ge(sm["bar"], 10)
                            for og in range(2, NEG):
                                k = 24 + og
                                g.wait_ge(sm["o_mm"], 8 * (og - 1))
                                srcp = woT.ap().rearrange("(d p) e -> p d e", p=128)[
                                    :, :, og * EGW:(og + 1) * EGW]
                                dstp = panel[k % 2].ap().rearrange("p (d e) -> p d e", e=EGW)
                                g.dma_start(out=dstp, in_=srcp).then_inc(
                                    sm["pan0" if k % 2 == 0 else "pan1"], 16)
                            if PHASES >= 4:
                                for og in range(2):
                                    k = 32 + og
                                    g.wait_ge(sm["o_mm"], 8 * (og + 7))
                                    srcp = wfT.ap().rearrange("(d p) e -> p d e", p=128)[
                                        :, :, og * EGW:(og + 1) * EGW]
                                    dstp = panel[k % 2].ap().rearrange("p (d e) -> p d e", e=EGW)
                                    g.dma_start(out=dstp, in_=srcp).then_inc(
                                        sm["pan0" if k % 2 == 0 else "pan1"], 16)
                            if DEBUG:
                                g.wait_ge(sm["x2a"], 16 * 32)
                                g.wait_ge(sm["x2b"], 16 * 32)
                                g.dma_start(out=dbg["d_x2"][:, :], in_=x2loc[:, :]
                                            ).then_inc(sm["dbg_st"], 16)
                            g.nop().then_inc(sm["bar"], 1)

        if PHASES >= 4:
            with ExitStack() as p4:
                lnT2 = p4.enter_context(nc.sbuf_tensor("lnT2", [128, ND * 1024], BF16))
                normb = p4.enter_context(nc.sbuf_tensor("normb", [128, ND * EGW], BF16))

                with ExitStack() as ctx:
                    def sb(name, shape, dt):
                        return ctx.enter_context(nc.sbuf_tensor(name, shape, dt))

                    def ps(name, shape, dt):
                        return ctx.enter_context(nc.psum_tensor(name, shape, dt))

                    xt2 = sb("xt2", [128, D], F32)
                    lnbf2 = [sb(f"lnbf2_{i}", [128, D], BF16) for i in range(2)]
                    stats2 = [sb(f"stats2_{i}", [128, 48], F32) for i in range(2)]
                    mv2 = [sb(f"mv2_{i}", [128, 2], F32) for i in range(2)]
                    lnv2 = [sb(f"lnv2_{i}", [128, 1], F32) for i in range(2)]
                    rstd2 = [sb(f"rstd2_{i}", [128, 1], F32) for i in range(2)]
                    tp2_ps = [ps(f"tp2_ps{i}", [128, 128], BF16) for i in range(2)]

                    blk = ctx.enter_context(nc.Block("p4a"))

                    @blk.sync
                    def _(s):
                        s.wait_ge(sm["bar"], 15)
                        for i in range(NT):
                            if i >= 1:
                                s.wait_ge(sm["ln_apply"], NT + i)
                            s.dma_start(out=xt2[:, :],
                                        in_=x2loc[128 * i:128 * (i + 1), :]).then_inc(sm["x4"], 16)
                        s.wait_ge(sm["x4"], 16 * NT)
                        s.nop().then_inc(sm["bar"], 1)

                    @blk.vector
                    def _(v):
                        v.wait_ge(sm["bar"], 15)
                        for i in range(NT + 1):
                            if i < NT:
                                v.wait_ge(sm["x4"], 16 * (i + 1))
                                ins = None
                                for j in range(8):
                                    ins = v.bn_stats(stats2[i % 2][:, 6 * j:6 * j + 6],
                                                     xt2[:, 512 * j:512 * (j + 1)])
                                ins.then_inc(sm["ln_bn"], 1)
                                v.wait_ge(sm["ln_bn"], NT + i + 1)
                                v.bn_aggr(mv2[i % 2][:, :], stats2[i % 2][:, :]).then_inc(sm["ln_mv"], 1)
                                v.wait_ge(sm["ln_rstd"], 2 * NT + 2 * i + 2)
                                if i >= 2:
                                    v.wait_ge(sm["tp_pe"], NT * ND + 32 * (i - 1))
                                v.tensor_scalar(lnbf2[i % 2][:, :], xt2[:, :],
                                                mv2[i % 2][:, 0:1], rstd2[i % 2][:, :],
                                                ALU.subtract, ALU.mult).then_inc(sm["ln_apply"], 1)
                            if i >= 1:
                                ti = i - 1
                                for d in range(ND):
                                    j = ti * ND + d
                                    v.wait_ge(sm["tp_pe"], NT * ND + j + 1)
                                    v.tensor_copy(lnT2[:, d * 1024 + ti * 128:d * 1024 + (ti + 1) * 128],
                                                  tp2_ps[j % 2][:, :]).then_inc(sm["tp_ev"], 1)
                        v.nop().then_inc(sm["bar"], 1)

                    @blk.scalar
                    def _(sc):
                        sc.wait_ge(sm["bar"], 15)
                        for i in range(NT):
                            sc.wait_ge(sm["ln_mv"], NT + i + 1)
                            sc.activation(lnv2[i % 2][:, :], mv2[i % 2][:, 1:2], AF.Ln,
                                          scale=1.0, bias=csb[:, 0:1]).then_inc(sm["ln_rstd"], 1)
                            sc.wait_ge(sm["ln_rstd"], 2 * NT + 2 * i + 1)
                            sc.activation(rstd2[i % 2][:, :], lnv2[i % 2][:, :], AF.Exp,
                                          scale=-0.5, bias=csb[:, 1:2]).then_inc(sm["ln_rstd"], 1)
                        sc.nop().then_inc(sm["bar"], 1)

                    @blk.tensor
                    def _(t):
                        t.wait_ge(sm["bar"], 15)
                        for i in range(NT):
                            t.wait_ge(sm["ln_apply"], NT + i + 1)
                            for d in range(ND):
                                j = i * ND + d
                                if j >= 2:
                                    t.wait_ge(sm["tp_ev"], NT * ND + j - 1)
                                t.transpose(tp2_ps[j % 2][:, :],
                                            lnbf2[i % 2][:, 128 * d:128 * (d + 1)],
                                            eyeb[:, :]).then_inc(sm["tp_pe"], 1)
                        t.nop().then_inc(sm["bar"], 1)

                    @blk.gpsimd
                    def _(g):
                        g.wait_ge(sm["bar"], 15)
                        srcn = normrep.ap().rearrange("(d p) e -> p d e", p=128)[:, :, 0:EGW]
                        dstn = normb.ap().rearrange("p (d e) -> p d e", e=EGW)
                        g.dma_start(out=dstn, in_=srcn).then_inc(sm["nrm"], 16)
                        g.nop().then_inc(sm["bar"], 1)

                with ExitStack() as ctx:
                    def sb(name, shape, dt):
                        return ctx.enter_context(nc.sbuf_tensor(name, shape, dt))

                    def ps(name, shape, dt):
                        return ctx.enter_context(nc.psum_tensor(name, shape, dt))

                    x2og = [sb(f"x2og{i}", [128, 8 * 512], F32) for i in range(2)]
                    biasog = [sb(f"biasog{i}", [128, 512], F32) for i in range(2)]
                    piece = [sb(f"fpiece{i}", [128, 512], F32) for i in range(2)]
                    f_ps = [ps(f"f_ps{i}", [128, 512], F32) for i in range(2)]

                    blk = ctx.enter_context(nc.Block("p4b"))

                    @blk.sync
                    def _(s):
                        s.wait_ge(sm["bar"], 20)
                        for og in range(NEG):
                            if og >= 2:
                                s.wait_ge(sm["f_ev"], 16 * (og - 1))
                            s.dma_start(
                                out=x2og[og % 2].ap().rearrange("p (ts e) -> p ts e", e=EGW),
                                in_=x2loc.ap().rearrange("(ts p) e -> p ts e", p=128)[
                                    :, :, og * EGW:(og + 1) * EGW],
                            ).then_inc(sm["xg4a" if og % 2 == 0 else "xg4b"], 16)
                            s.dma_start(out=biasog[og % 2][:, :],
                                        in_=bias_rep[:, og * EGW:(og + 1) * EGW]
                                        ).then_inc(sm["bl4a" if og % 2 == 0 else "bl4b"], 16)
                            for ts_ in range(8):
                                idx = og * 8 + ts_
                                s.wait_ge(sm["f_ev"], 2 * (idx + 1))
                                dst = out_ext.ap().rearrange("(ts p) e -> p ts e", p=128)[
                                    :, ts_:ts_ + 1, og * EGW:(og + 1) * EGW]
                                s.dma_start(out=dst,
                                            in_=piece[idx % 2].ap().rearrange(
                                                "p (one e) -> p one e", one=1)
                                            ).then_inc(sm["o4a" if idx % 2 == 0 else "o4b"], 16)
                        s.nop().then_inc(sm["bar"], 1)

                    @blk.gpsimd
                    def _(g):
                        g.wait_ge(sm["bar"], 20)
                        for og in range(NEG):
                            if og >= 2:
                                k = 32 + og
                                g.wait_ge(sm["f_mm"], 8 * (og - 1))
                                srcp = wfT.ap().rearrange("(d p) e -> p d e", p=128)[
                                    :, :, og * EGW:(og + 1) * EGW]
                                dstp = panel[k % 2].ap().rearrange("p (d e) -> p d e", e=EGW)
                                g.dma_start(out=dstp, in_=srcp).then_inc(
                                    sm["pan0" if k % 2 == 0 else "pan1"], 16)
                            if og >= 1:
                                g.wait_ge(sm["dq"], og)
                                srcn = normrep.ap().rearrange("(d p) e -> p d e", p=128)[
                                    :, :, og * EGW:(og + 1) * EGW]
                                dstn = normb.ap().rearrange("p (d e) -> p d e", e=EGW)
                                g.dma_start(out=dstn, in_=srcn).then_inc(sm["nrm"], 16)
                        g.wait_ge(sm["o4a"], 16 * 32)
                        g.wait_ge(sm["o4b"], 16 * 32)
                        g.wait_ge(sm["bar"], 24)
                        g.nop().then_inc(sm["bar"], 1)

                    @blk.vector
                    def _(v):
                        v.wait_ge(sm["bar"], 20)
                        for og in range(NEG):
                            k = 32 + og
                            pan_w(v, k)
                            v.wait_ge(sm["nrm"], 16 * (og + 1))
                            if og >= 2:
                                v.wait_ge(sm["f_mm"], 8 * (og - 1))
                            v.tensor_tensor(panel[k % 2][:, :], panel[k % 2][:, :],
                                            normb[:, :], ALU.mult).then_inc(sm["dq"], 1)
                            v.wait_ge(sm["xg4a" if og % 2 == 0 else "xg4b"],
                                      16 * (og // 2 + 1))
                            v.wait_ge(sm["bl4a" if og % 2 == 0 else "bl4b"],
                                      16 * (og // 2 + 1))
                            for ts_ in range(8):
                                idx = og * 8 + ts_
                                v.wait_ge(sm["f_mm"], idx + 1)
                                if idx >= 2:
                                    v.wait_ge(sm["o4a" if idx % 2 == 0 else "o4b"],
                                              16 * (idx // 2))
                                v.tensor_tensor(piece[idx % 2][:, :], f_ps[idx % 2][:, :],
                                                biasog[og % 2][:, :], ALU.add
                                                ).then_inc(sm["f_ev"], 1)
                                v.wait_ge(sm["f_ev"], 2 * idx + 1)
                                v.tensor_tensor(piece[idx % 2][:, :], piece[idx % 2][:, :],
                                                x2og[og % 2][:, ts_ * 512:(ts_ + 1) * 512],
                                                ALU.add).then_inc(sm["f_ev"], 1)
                        v.nop().then_inc(sm["bar"], 1)

                    @blk.tensor
                    def _(t):
                        t.wait_ge(sm["bar"], 20)
                        for og in range(NEG):
                            t.wait_ge(sm["dq"], og + 1)
                            k = 32 + og
                            for ts_ in range(8):
                                gi = og * 8 + ts_
                                if gi >= 2:
                                    t.wait_ge(sm["f_ev"], 2 * (gi - 1))
                                lm = None
                                for d in range(ND):
                                    lm = t.matmul(
                                        f_ps[gi % 2][:, :],
                                        lnT2[:, d * 1024 + ts_ * 128:d * 1024 + (ts_ + 1) * 128],
                                        panel[k % 2][:, d * EGW:(d + 1) * EGW],
                                        start=(d == 0), stop=(d == ND - 1))
                                lm.then_inc(sm["f_mm"], 1)
                            t.nop().then_inc(sm["bar"], 1) if og == NEG - 1 else None

                    @blk.scalar
                    def _(sc):
                        sc.wait_ge(sm["bar"], 20)
                        sc.nop().then_inc(sm["bar"], 1)
        if PHASES < 4:
            ndbg = 5 + (1 if PHASES >= 2 else 0) + (1 if PHASES >= 3 else 0)
            with ExitStack() as ctx:
                blk = ctx.enter_context(nc.Block("fin"))

                @blk.gpsimd
                def _(g):
                    if DEBUG:
                        g.wait_ge(sm["dbg_st"], 16 * ndbg)
                    g.wait_ge(sm["bar"], 5 * PHASES)

    return nc, dbg


def _host_layout(L0_attn_x, L0_attn_w_q, L0_attn_w_k, L0_attn_w_v, L0_attn_w_o,
                 L0_ffn_weight_q, L0_ffn_weight_norm, L0_ffn_bias):
    x = np.ascontiguousarray(np.asarray(L0_attn_x).astype(np.float32, copy=False).reshape(NC * T, D))
    wqT_ = np.ascontiguousarray(np.asarray(L0_attn_w_q).astype(np.float32, copy=False).T)
    wkT_ = np.ascontiguousarray(np.asarray(L0_attn_w_k).astype(np.float32, copy=False).T)
    wvT_ = np.ascontiguousarray(np.asarray(L0_attn_w_v).astype(np.float32, copy=False).T)
    woT_ = np.ascontiguousarray(np.asarray(L0_attn_w_o).astype(np.float32, copy=False).T)
    wfT_ = np.ascontiguousarray(np.asarray(L0_ffn_weight_q).astype(np.int8, copy=False).reshape(D, D).T)
    nrm = np.asarray(L0_ffn_weight_norm).astype(np.float16, copy=False).reshape(D, D // 16)
    normrep_ = np.ascontiguousarray(np.repeat(nrm.T, 16, axis=0))
    bias_rep_ = np.ascontiguousarray(
        np.tile(np.asarray(L0_ffn_bias).astype(np.float32, copy=False)[None, :], (128, 1)))
    base = {
        "wqT": wqT_, "wkT": wkT_, "wvT": wvT_, "woT": woT_,
        "wfT": wfT_, "normrep": normrep_, "bias_rep": bias_rep_,
        "ones_1x128_bf": np.ones((1, 128), ml_dtypes.bfloat16),
        "ones_128x1_bf": np.ones((128, 1), ml_dtypes.bfloat16),
        "ones_128x1_f": np.ones((128, 1), np.float32),
        "ones_1x128_f": np.ones((1, 128), np.float32),
        "eye_bf": np.eye(128).astype(ml_dtypes.bfloat16),
        "sc128": np.full((128, 128), 1.0 / 128.0, ml_dtypes.bfloat16),
        "eye_f": np.eye(128, dtype=np.float32),
        "consts": np.tile(np.array([[1e-5, -np.log(15.0), 0.0, 0.0]], np.float32), (128, 1)),
    }
    in_maps = []
    for c in range(NC):
        m = dict(base)
        m["x_in"] = np.ascontiguousarray(x[c * T:(c + 1) * T])
        in_maps.append(m)
    return in_maps


def kernel(**inputs):
    in_maps = _host_layout(**inputs)
    nc, _ = build_nc()
    res = run_bass_kernel_spmd(nc, in_maps, core_ids=list(range(NC)))
    out = np.concatenate([np.asarray(r["out"]) for r in res.results], axis=0)
    return out.reshape(2, S, D).astype(np.float32)


# version 05
# speedup vs baseline: 1.0124x; 1.0124x over previous
import numpy as np
import ml_dtypes
from contextlib import ExitStack

import concourse.bass as bass
import concourse.mybir as mybir
from concourse.bass_utils import run_bass_kernel_spmd

F32 = mybir.dt.float32
BF16 = mybir.dt.bfloat16
F16 = mybir.dt.float16
I8 = mybir.dt.int8
AF = mybir.ActivationFunctionType
ALU = mybir.AluOpType
AX = mybir.AxisListType

D = 4096
T = 1024
H = 32
S = 4096
NT = T // 128
ND = D // 128
NEG = 8
EGW = 512
NC = 8
GRP = 4

SCALE = float(1.0 / np.sqrt(128.0))
LN15 = float(np.log(15.0))

DEBUG = False
PHASES = 4
NO_CC = False


def build_nc():
    nc = bass.Bass("TRN2", target_bir_lowering=False)

    x_in = nc.dram_tensor("x_in", [T, D], F32, kind="ExternalInput")
    wqT = nc.dram_tensor("wqT", [D, D], F32, kind="ExternalInput")
    wkT = nc.dram_tensor("wkT", [D, D], F32, kind="ExternalInput")
    wvT = nc.dram_tensor("wvT", [D, D], F32, kind="ExternalInput")
    woT = nc.dram_tensor("woT", [D, D], F32, kind="ExternalInput")
    wfT = nc.dram_tensor("wfT", [D, D], I8, kind="ExternalInput")
    normrep = nc.dram_tensor("normrep", [D, D], F16, kind="ExternalInput")
    bias_rep = nc.dram_tensor("bias_rep", [128, D], F32, kind="ExternalInput")
    ones_1x128_bf = nc.dram_tensor("ones_1x128_bf", [1, 128], BF16, kind="ExternalInput")
    ones_128x1_bf = nc.dram_tensor("ones_128x1_bf", [128, 1], BF16, kind="ExternalInput")
    ones_128x1_f = nc.dram_tensor("ones_128x1_f", [128, 1], F32, kind="ExternalInput")
    ones_1x128_f = nc.dram_tensor("ones_1x128_f", [1, 128], F32, kind="ExternalInput")
    eye_bf = nc.dram_tensor("eye_bf", [128, 128], BF16, kind="ExternalInput")
    sc128 = nc.dram_tensor("sc128", [128, 128], BF16, kind="ExternalInput")
    eye_f = nc.dram_tensor("eye_f", [128, 128], F32, kind="ExternalInput")
    consts = nc.dram_tensor("consts", [128, 4], F32, kind="ExternalInput")

    out_ext = nc.dram_tensor("out", [T, D], F32, kind="ExternalOutput")

    qT_loc = nc.dram_tensor("qT_loc", [D, T], BF16, kind="Internal")
    kT_loc = nc.dram_tensor("kT_loc", [D, T], BF16, kind="Internal")
    v_loc = nc.dram_tensor("v_loc", [T, D], BF16, kind="Internal")
    knloc = nc.dram_tensor("knloc", [H, T], BF16, kind="Internal")
    qnloc = nc.dram_tensor("qnloc", [H, T], BF16, kind="Internal")
    kg = nc.dram_tensor("kg", [GRP * D, T], BF16, kind="Internal")
    vg = nc.dram_tensor("vg", [S, D], BF16, kind="Internal")
    kng = nc.dram_tensor("kng", [GRP * H, T], BF16, kind="Internal")
    mkd = nc.dram_tensor("mkd", [1, H], F32, kind="Internal")
    x2loc = nc.dram_tensor("x2loc", [T, D], F32, kind="Internal")

    dbg = {}
    if DEBUG:
        dbg["d_qT"] = nc.dram_tensor("d_qT", [D, T], BF16, kind="ExternalOutput")
        dbg["d_kg"] = nc.dram_tensor("d_kg", [GRP * D, T], BF16, kind="ExternalOutput")
        dbg["d_vg"] = nc.dram_tensor("d_vg", [S, D], BF16, kind="ExternalOutput")
        dbg["d_kng"] = nc.dram_tensor("d_kng", [GRP * H, T], BF16, kind="ExternalOutput")
        dbg["d_qn"] = nc.dram_tensor("d_qn", [H, T], BF16, kind="ExternalOutput")
        if PHASES >= 2:
            dbg["d_at"] = nc.dram_tensor("d_at", [128, H * T], BF16, kind="ExternalOutput")
        if PHASES >= 3:
            dbg["d_x2"] = nc.dram_tensor("d_x2", [T, D], F32, kind="ExternalOutput")

    RG = [[0, 1, 2, 3], [4, 5, 6, 7]]

    with ExitStack() as g_ctx:
        def g_sb(name, shape, dt):
            return g_ctx.enter_context(nc.sbuf_tensor(name, shape, dt))

        def g_sem(name):
            return g_ctx.enter_context(nc.semaphore(name))

        panel = [g_sb(f"panel{i}", [128, ND * EGW], BF16) for i in range(2)]
        onec_bf = g_sb("onec_bf", [128, 1], BF16)
        oner_bf = g_sb("oner_bf", [1, 128], BF16)
        onec_f = g_sb("onec_f", [128, 1], F32)
        oner_f = g_sb("oner_f", [1, 128], F32)
        eyeb = g_sb("eyeb", [128, 128], BF16)
        sc128b = g_sb("sc128b", [128, 128], BF16)
        eyef = g_sb("eyef", [128, 128], F32)
        csb = g_sb("csb", [128, 4], F32)

        names = ("c_ld bar x_ld ln_bn ln_mv ln_rstd ln_apply tp_pe tp_ev "
                 "pan0 pan1 x0 x1 pfs0 pfs1 pcv sg0 sg1 sg2 sg3 sg4 sg5 sg6 sg7 pj_mm pj_ev sq_tt n_mm n_ev knst qnst kst vst qst kdone vdone kndone cc_s "
                 "pfs pfv mk_ld mk_dve mk_pe ng sp ex dn av pd rc pr rb ev2 "
                 "x3a x3b x2a x2b o_mm o_ev nrm dq f_mm f_ev x4 xg4a xg4b bl4a bl4b o4a o4b dbg_st").split()
        sm = {n: g_sem(n) for n in names}

        PASSES = [("k", wkT, kT_loc), ("v", wvT, v_loc), ("q", wqT, qT_loc)]

        SWP = [0, 1] + list(range(24, 40))

        def pan_w(eng, k):
            if 2 <= k < 24:
                eng.wait_ge(sm["pcv"], 32 * (k - 1))
            else:
                i = SWP.index(k)
                par = [j for j in SWP if j % 2 == k % 2].index(k)
                eng.wait_ge(sm["pan0" if k % 2 == 0 else "pan1"], 16 * (par + 1))

        def pjg_cnt(pi, eg, gi):
            return pi * NEG * 8 + eg * 8 + gi + 1

        def slab_guard(eng, k):
            j = k - 1
            if k - 2 < 8:
                eng.wait_ge(sm["kst"], 64 * min(j + 1, 8))
            elif k - 2 < 16:
                eng.wait_ge(sm["vst"], 16 * (min(j, 15) - 7))
            else:
                eng.wait_ge(sm["qst"], 64 * (j - 15))

        with ExitStack() as ctx:
            def sb(name, shape, dt):
                return ctx.enter_context(nc.sbuf_tensor(name, shape, dt))

            def ps(name, shape, dt):
                return ctx.enter_context(nc.psum_tensor(name, shape, dt))

            xt = [sb(f"xt{i}", [128, D], F32) for i in range(2)]
            lnbf = [sb(f"lnbf{i}", [128, D], BF16) for i in range(2)]
            lnT = sb("lnT", [128, ND * 1024], BF16)
            stats = [sb(f"stats{i}", [128, 48], F32) for i in range(2)]
            mv = [sb(f"mv{i}", [128, 2], F32) for i in range(2)]
            lnv = [sb(f"lnv{i}", [128, 1], F32) for i in range(2)]
            rstd = [sb(f"rstd{i}", [128, 1], F32) for i in range(2)]
            slab = [sb(f"slab{i}", [128, 4096], BF16) for i in range(2)]
            sqt = [sb(f"sqt{i}", [128, 1024], BF16) for i in range(2)]
            nst = [sb(f"nst{i}", [1, 1024], BF16) for i in range(4)]
            stg = [lnbf[i // 4].ap().bitcast(F32)[:, (i % 4) * 512:(i % 4) * 512 + 512]
                   for i in range(8)]
            pj_ps = [ps(f"pj_ps{i}", [128, 512], F32) for i in range(2)]
            kn_ps = ps("kn_ps", [1, 1024], F32)
            tp_ps = [ps(f"tp_ps{i}", [128, 128], BF16) for i in range(2)]

            blk = ctx.enter_context(nc.Block("p1"))

            @blk.sync
            def _(s):
                s.dma_start(out=onec_bf[:, :], in_=ones_128x1_bf[:, :]).then_inc(sm["c_ld"], 16)
                s.dma_start(out=oner_bf[:, :], in_=ones_1x128_bf[:, :]).then_inc(sm["c_ld"], 16)
                s.dma_start(out=onec_f[:, :], in_=ones_128x1_f[:, :]).then_inc(sm["c_ld"], 16)
                s.dma_start(out=oner_f[:, :], in_=ones_1x128_f[:, :]).then_inc(sm["c_ld"], 16)
                s.dma_start(out=eyeb[:, :], in_=eye_bf[:, :]).then_inc(sm["c_ld"], 16)
                s.dma_start(out=sc128b[:, :], in_=sc128[:, :]).then_inc(sm["c_ld"], 16)
                s.dma_start(out=eyef[:, :], in_=eye_f[:, :]).then_inc(sm["c_ld"], 16)
                s.dma_start(out=csb[:, :], in_=consts[:, :]).then_inc(sm["c_ld"], 16)
                for i in range(NT):
                    if i >= 2:
                        s.wait_ge(sm["ln_apply"], i - 1)
                    s.dma_start(out=xt[i % 2][:, :],
                                in_=x_in[128 * i:128 * (i + 1), :]).then_inc(
                                    sm["x0" if i % 2 == 0 else "x1"], 16)
                for pi, (pname, _w, locT) in enumerate(PASSES):
                    stsem = {"k": "kst", "v": "vst", "q": "qst"}[pname]
                    for eg in range(NEG):
                        k = pi * NEG + eg
                        s.wait_ge(sm["pj_ev"], pjg_cnt(pi, eg, 7))
                        if pname in ("k", "q"):
                            for ec in range(4):
                                r0 = eg * 512 + ec * 128
                                s.dma_start(out=locT[r0:r0 + 128, :],
                                            in_=slab[eg % 2][:, ec * 1024:(ec + 1) * 1024]
                                            ).then_inc(sm[stsem], 16)
                        else:
                            dst = v_loc.ap().rearrange("(ts p) e -> p ts e", p=128)[
                                :, :, eg * EGW:(eg + 1) * EGW]
                            src = slab[eg % 2].ap().rearrange("p (ts e) -> p ts e", e=EGW)
                            s.dma_start(out=dst, in_=src).then_inc(sm[stsem], 16)
                    if pname == "k":
                        s.wait_ge(sm["kst"], 64 * 8)
                        s.nop().then_inc(sm["kdone"], 1)
                    elif pname == "v":
                        s.wait_ge(sm["vst"], 16 * 8)
                        s.nop().then_inc(sm["vdone"], 1)
                s.wait_ge(sm["qst"], 64 * 8)
                s.nop().then_inc(sm["bar"], 1)

            @blk.gpsimd
            def _(g):
                for k in range(24):
                    pi, eg = divmod(k, NEG)
                    wT = PASSES[pi][1]
                    if k < 2:
                        src = wT.ap().rearrange("(d p) e -> p d e", p=128)[
                            :, :, eg * EGW:(eg + 1) * EGW]
                        dst = panel[k % 2].ap().rearrange("p (d e) -> p d e", e=EGW)
                        g.dma_start(out=dst, in_=src).then_inc(
                            sm["pan0" if k % 2 == 0 else "pan1"], 16)
                    else:
                        g.wait_ge(sm["pj_mm"], pjg_cnt(*divmod(k - 2, NEG), 7))
                        for d in range(ND):
                            j = (k - 2) * ND + d
                            g.wait_ge(sm[f"sg{j % 8}"], 16 * (j // 8 + 1))
                            g.tensor_copy(panel[k % 2][:, d * EGW:(d + 1) * EGW],
                                          stg[j % 8]).then_inc(sm["pcv"], 1)
                    if k == 13:
                        g.wait_ge(sm["kdone"], 1)
                        g.wait_ge(sm["kndone"], 1)
                        if NO_CC:
                            g.dma_start(out=kg[0:D, :], in_=kT_loc[:, :]).then_inc(sm["cc_s"], 16)
                            g.dma_start(out=kng[0:H, :], in_=knloc[:, :]).then_inc(sm["cc_s"], 16)
                            g.wait_ge(sm["cc_s"], 32)
                        else:
                            g.collective_compute(
                                "AllGather", ALU.bypass, replica_groups=RG,
                                ins=[kT_loc.ap().opt()], outs=[kg.ap().opt()],
                            ).then_inc(sm["cc_s"], 1)
                            g.collective_compute(
                                "AllGather", ALU.bypass, replica_groups=RG,
                                ins=[knloc.ap().opt()], outs=[kng.ap().opt()],
                            ).then_inc(sm["cc_s"], 1)
                    if k == 21:
                        g.wait_ge(sm["vdone"], 1)
                        if NO_CC:
                            g.dma_start(out=vg[0:T, :], in_=v_loc[:, :]).then_inc(sm["cc_s"], 16)
                            g.wait_ge(sm["cc_s"], 48)
                        else:
                            g.collective_compute(
                                "AllGather", ALU.bypass, replica_groups=RG,
                                ins=[v_loc.ap().opt()], outs=[vg.ap().opt()],
                            ).then_inc(sm["cc_s"], 1)
                if DEBUG:
                    g.wait_ge(sm["cc_s"], 48 if NO_CC else 3)
                    g.wait_ge(sm["qst"], 64 * 8)
                    g.wait_ge(sm["qnst"], 16 * H)
                    g.dma_start(out=dbg["d_qT"][:, :], in_=qT_loc[:, :]).then_inc(sm["dbg_st"], 16)
                    g.dma_start(out=dbg["d_kg"][:, :], in_=kg[:, :]).then_inc(sm["dbg_st"], 16)
                    g.dma_start(out=dbg["d_vg"][:, :], in_=vg[:, :]).then_inc(sm["dbg_st"], 16)
                    g.dma_start(out=dbg["d_kng"][:, :], in_=kng[:, :]).then_inc(sm["dbg_st"], 16)
                    g.dma_start(out=dbg["d_qn"][:, :], in_=qnloc[:, :]).then_inc(sm["dbg_st"], 16)
                g.nop().then_inc(sm["bar"], 1)

            @blk.tensor
            def _(t):
                t.wait_ge(sm["c_ld"], 16 * 8)
                for i in range(NT):
                    t.wait_ge(sm["ln_apply"], i + 1)
                    for d in range(ND):
                        j = i * ND + d
                        if j >= 2:
                            t.wait_ge(sm["tp_ev"], j - 1)
                        t.transpose(tp_ps[j % 2][:, :],
                                    lnbf[i % 2][:, 128 * d:128 * (d + 1)],
                                    eyeb[:, :]).then_inc(sm["tp_pe"], 1)
                t.wait_ge(sm["tp_ev"], NT * ND)
                for pi, (pname, _w, _l) in enumerate(PASSES):
                    pend = []
                    for eg in range(NEG):
                        k = pi * NEG + eg
                        pan_w(t, k)
                        for gi in range(8):
                            gidx = pi * NEG * 8 + eg * 8 + gi
                            if gidx >= 2:
                                t.wait_ge(sm["pj_ev"], gidx - 1)
                            if pname in ("k", "q"):
                                ec, th = divmod(gi, 2)
                                lastm = None
                                for d in range(ND):
                                    lastm = t.matmul(
                                        pj_ps[gidx % 2][:, :],
                                        panel[k % 2][:, d * EGW + ec * 128:d * EGW + (ec + 1) * 128],
                                        lnT[:, d * 1024 + th * 512:d * 1024 + (th + 1) * 512],
                                        start=(d == 0), stop=(d == ND - 1))
                            else:
                                ts_ = gi
                                for d in range(ND):
                                    lastm = t.matmul(
                                        pj_ps[gidx % 2][:, :],
                                        lnT[:, d * 1024 + ts_ * 128:d * 1024 + (ts_ + 1) * 128],
                                        panel[k % 2][:, d * EGW:(d + 1) * EGW],
                                        start=(d == 0), stop=(d == ND - 1))
                            lastm.then_inc(sm["pj_mm"], 1)
                            if pname in ("k", "q") and pend:
                                h = pend.pop(0)
                                gh = (0 if pname == "k" else H) + h
                                t.wait_ge(sm["sq_tt"], (gh % H) * 2 + 2 + (0 if pname == "k" else 2 * H))
                                if gh >= 1:
                                    t.wait_ge(sm["n_ev"], gh)
                                lm = None
                                for th2 in range(2):
                                    lm = t.matmul(kn_ps[:, th2 * 512:(th2 + 1) * 512],
                                                  onec_bf[:, :],
                                                  sqt[h % 2][:, th2 * 512:(th2 + 1) * 512],
                                                  start=True, stop=True)
                                lm.then_inc(sm["n_mm"], 1)
                            if pname in ("k", "q") and gi % 2 == 1:
                                pend.append(eg * 4 + gi // 2)
                    for h in pend:
                        gh = (0 if pname == "k" else H) + h
                        t.wait_ge(sm["sq_tt"], (gh % H) * 2 + 2 + (0 if pname == "k" else 2 * H))
                        if gh >= 1:
                            t.wait_ge(sm["n_ev"], gh)
                        lm = None
                        for th2 in range(2):
                            lm = t.matmul(kn_ps[:, th2 * 512:(th2 + 1) * 512],
                                          onec_bf[:, :],
                                          sqt[h % 2][:, th2 * 512:(th2 + 1) * 512],
                                          start=True, stop=True)
                        lm.then_inc(sm["n_mm"], 1)
                    pend.clear()
                t.nop().then_inc(sm["bar"], 1)

            @blk.scalar
            def _(sc):
                sc.wait_ge(sm["c_ld"], 16 * 8)
                for i in range(NT):
                    sc.wait_ge(sm["ln_mv"], i + 1)
                    sc.activation(lnv[i % 2][:, :], mv[i % 2][:, 1:2], AF.Ln,
                                  scale=1.0, bias=csb[:, 0:1]).then_inc(sm["ln_rstd"], 1)
                    sc.wait_ge(sm["ln_rstd"], 2 * i + 1)
                    sc.activation(rstd[i % 2][:, :], lnv[i % 2][:, :], AF.Exp,
                                  scale=-0.5).then_inc(sm["ln_rstd"], 1)
                def norm_store(gh):
                    pnameh = "k" if gh < H else "q"
                    h = gh % H
                    nloc = knloc if pnameh == "k" else qnloc
                    nsem = "knst" if pnameh == "k" else "qnst"
                    sc.wait_ge(sm["n_ev"], gh + 1)
                    sc.dma_start(out=nloc[h:h + 1, :],
                                 in_=nst[gh % 4][:, :]).then_inc(sm[nsem], 16)
                    if gh == H - 1:
                        sc.wait_ge(sm["knst"], 16 * H)
                        sc.nop().then_inc(sm["kndone"], 1)

                sc.wait_ge(sm["tp_ev"], NT * ND)
                for k in range(2, 24):
                    pi, eg = divmod(k, NEG)
                    wT = PASSES[pi][1]
                    kk = k - 3
                    if 0 <= kk < 8:
                        for ec in range(4):
                            norm_store(kk * 4 + ec)
                    elif kk >= 16:
                        for ec in range(4):
                            norm_store(H + (kk - 16) * 4 + ec)
                    for d in range(ND):
                        j = (k - 2) * ND + d
                        if j >= 8:
                            sc.wait_ge(sm["pcv"], j - 7)
                        srcp = wT.ap().rearrange("(d p) e -> p d e", p=128)[
                            :, d:d + 1, eg * EGW:(eg + 1) * EGW]
                        sc.dma_start(out=stg[j % 8].rearrange("p e -> p () e"),
                                     in_=srcp).then_inc(sm[f"sg{j % 8}"], 16)
                for gh in range(H + 5 * 4, 2 * H):
                    norm_store(gh)
                sc.wait_ge(sm["qnst"], 16 * H)
                sc.nop().then_inc(sm["bar"], 1)

            @blk.vector
            def _(v):
                for i in range(NT + 1):
                    if i < NT:
                        v.wait_ge(sm["x0" if i % 2 == 0 else "x1"], 16 * (i // 2 + 1))
                        ins = None
                        for j in range(8):
                            ins = v.bn_stats(stats[i % 2][:, 6 * j:6 * j + 6],
                                             xt[i % 2][:, 512 * j:512 * (j + 1)])
                        ins.then_inc(sm["ln_bn"], 1)
                        v.wait_ge(sm["ln_bn"], i + 1)
                        v.bn_aggr(mv[i % 2][:, :], stats[i % 2][:, :]).then_inc(sm["ln_mv"], 1)
                        v.wait_ge(sm["ln_rstd"], 2 * i + 2)
                        if i >= 2:
                            v.wait_ge(sm["tp_pe"], 32 * (i - 1))
                        v.tensor_scalar(lnbf[i % 2][:, :], xt[i % 2][:, :],
                                        mv[i % 2][:, 0:1], rstd[i % 2][:, :],
                                        ALU.subtract, ALU.mult).then_inc(sm["ln_apply"], 1)
                    if i >= 1:
                        ti = i - 1
                        for d in range(ND):
                            j = ti * ND + d
                            v.wait_ge(sm["tp_pe"], j + 1)
                            v.tensor_copy(lnT[:, d * 1024 + ti * 128:d * 1024 + (ti + 1) * 128],
                                          tp_ps[j % 2][:, :]).then_inc(sm["tp_ev"], 1)
                def nevac(h, gh):
                    v.wait_ge(sm["n_mm"], gh + 1)
                    if gh >= 4:
                        if gh - 4 < H:
                            v.wait_ge(sm["knst"], 64 * min(gh // 4, 8))
                        else:
                            v.wait_ge(sm["qnst"], 64 * ((gh - H) // 4))
                    v.tensor_copy(nst[gh % 4][:, :], kn_ps[:, :]).then_inc(sm["n_ev"], 1)

                for pi, (pname, _w, _l) in enumerate(PASSES):
                    noff = 0 if pname == "k" else H
                    for eg in range(NEG):
                        k = pi * NEG + eg
                        for gi in range(8):
                            gidx = pi * NEG * 8 + eg * 8 + gi
                            v.wait_ge(sm["pj_mm"], gidx + 1)
                            if gi == 0 and k >= 2:
                                slab_guard(v, k)
                            if pname in ("k", "q"):
                                ec, th = divmod(gi, 2)
                                dstc = ec * 1024 + th * 512
                            else:
                                dstc = gi * 512
                            v.tensor_copy(slab[eg % 2][:, dstc:dstc + 512],
                                          pj_ps[gidx % 2][:, :]).then_inc(sm["pj_ev"], 1)
                            if pname in ("k", "q") and gi % 2 == 1:
                                ec = gi // 2
                                h = eg * 4 + ec
                                gh = noff + h
                                v.wait_ge(sm["pj_ev"], gidx + 1)
                                if gh >= 2:
                                    v.wait_ge(sm["n_mm"], gh - 1)
                                v.tensor_tensor(sqt[h % 2][:, 0:512],
                                                slab[eg % 2][:, ec * 1024:ec * 1024 + 512],
                                                slab[eg % 2][:, ec * 1024:ec * 1024 + 512],
                                                ALU.mult).then_inc(sm["sq_tt"], 1)
                                v.tensor_tensor(sqt[h % 2][:, 512:1024],
                                                slab[eg % 2][:, ec * 1024 + 512:ec * 1024 + 1024],
                                                slab[eg % 2][:, ec * 1024 + 512:ec * 1024 + 1024],
                                                ALU.mult).then_inc(sm["sq_tt"], 1)
                                if ec >= 1:
                                    nevac(eg * 4 + ec - 1, noff + eg * 4 + ec - 1)
                        if pname in ("k", "q"):
                            nevac(eg * 4 + 3, noff + eg * 4 + 3)
                v.nop().then_inc(sm["bar"], 1)


        CC1, CC2, CC3 = (16, 32, 48) if NO_CC else (1, 2, 3)
        if PHASES >= 2:
            with ExitStack() as p23:
                attnT = p23.enter_context(
                    nc.sbuf_tensor("attnT", [128, H * 1024], BF16))
                with ExitStack() as ctx:
                    def sb(name, shape, dt):
                        return ctx.enter_context(nc.sbuf_tensor(name, shape, dt))

                    def ps(name, shape, dt):
                        return ctx.enter_context(nc.psum_tensor(name, shape, dt))

                    khb = [sb(f"khb{i}", [128, 4 * 1024], BF16) for i in range(2)]
                    vhb = [sb(f"vhb{i}", [128, H * 128], BF16) for i in range(2)]
                    qhb = [sb(f"qhb{i}", [128, 1024], BF16) for i in range(2)]
                    qnb = [sb(f"qnb{i}", [128, 1024], BF16) for i in range(2)]
                    negm = [sb(f"negm{i}", [128, 1024], BF16) for i in range(2)]
                    ptb = [[sb(f"ptb{i}_{j}", [128, 1024], BF16) for j in range(2)] for i in range(2)]
                    denv = [sb(f"denv{i}", [128, 1024], F32) for i in range(2)]
                    denvb = [sb(f"denvb{i}", [128, 1024], BF16) for i in range(2)]
                    kn4 = sb("kn4", [H, 1024], BF16)
                    mk2 = sb("mk2", [H, 1], F32)
                    mk2b = sb("mk2b", [H, 1], F32)
                    mk2n = sb("mk2n", [128, H], F32)
                    r_sb = sb("r_sb", [1, 512], F32)
                    rbc_sb = sb("rbc_sb", [128, 512], F32)
                    s_ps = [ps(f"s_ps{i}", [128, 1024], F32) for i in range(2)]
                    acc_ps = [ps(f"acc_ps{i}", [128, 512], F32) for i in range(3)]
                    dr_ps = ps("dr_ps", [128, 512], F32)
                    den_ps = dr_ps[0:1, :]
                    rbc_ps = dr_ps

                    blk = ctx.enter_context(nc.Block("p2"))

                    @blk.sync
                    def _(s):
                        s.wait_ge(sm["bar"], 5)
                        s.wait_ge(sm["cc_s"], CC2)
                        kngv = kng.ap().rearrange("(r h) t -> h r t", r=GRP)
                        s.dma_start(out=kn4[:, :], in_=kngv[:, 0, :]).then_inc(sm["mk_ld"], 16)
                        s.wait_ge(sm["mk_dve"], 1)
                        s.dma_start(out=kn4[:, :], in_=kngv[:, 1, :]).then_inc(sm["mk_ld"], 16)
                        s.wait_ge(sm["mk_dve"], 3)
                        s.dma_start(out=kn4[:, :], in_=kngv[:, 2, :]).then_inc(sm["mk_ld"], 16)
                        s.wait_ge(sm["mk_dve"], 5)
                        s.dma_start(out=kn4[:, :], in_=kngv[:, 3, :]).then_inc(sm["mk_ld"], 16)
                        s.wait_ge(sm["mk_dve"], 7)
                        s.dma_start(out=mkd.ap().rearrange("a b -> b a"),
                                    in_=mk2[:, :]).then_inc(sm["mk_ld"], 16)
                        s.wait_ge(sm["mk_ld"], 80)
                        s.dma_start(out=mk2n.ap().rearrange("p h -> p () h"),
                                    in_=mkd.ap().partition_broadcast(128)
                                    ).then_inc(sm["mk_ld"], 16)
                        s.wait_ge(sm["qst"], 64 * 8)
                        s.wait_ge(sm["qnst"], 16 * H)
                        s.wait_ge(sm["cc_s"], CC3)
                        for h in range(H):
                            if h >= 2:
                                s.wait_ge(sm["sp"], 32 * (h - 1))
                                s.wait_ge(sm["av"], 32 * (h - 1))
                                s.wait_ge(sm["ng"], h - 1)
                            pf = sm["pfs0" if h % 2 == 0 else "pfs1"]
                            s.dma_start(
                                out=khb[h % 2].ap().rearrange("p (r t) -> p r t", t=T),
                                in_=kg.ap().rearrange("(r hh p) t -> p r hh t", r=GRP, hh=H)[:, :, h, :],
                            ).then_inc(pf, 16)
                            s.dma_start(out=qhb[h % 2][:, :],
                                        in_=qT_loc[h * 128:(h + 1) * 128, :]).then_inc(pf, 16)
                            s.dma_start(out=qnb[h % 2].ap().rearrange("p t -> p () t"),
                                        in_=qnloc[h:h + 1, :].partition_broadcast(128)
                                        ).then_inc(pf, 16)
                            s.dma_start(
                                out=vhb[h % 2].ap().rearrange("p (c e) -> p c e", e=128),
                                in_=vg.ap().rearrange("(c p) e -> p c e", p=128)[:, :, h * 128:(h + 1) * 128],
                            ).then_inc(pf, 16)
                        s.wait_ge(sm["pfs0"], 64 * 16)
                        s.wait_ge(sm["pfs1"], 64 * 16)
                        s.nop().then_inc(sm["bar"], 1)

                    def eidx(h, qg, p):
                        return h * 32 + 2 * p + qg

                    @blk.scalar
                    def _(sc):
                        sc.wait_ge(sm["bar"], 5)
                        for h in range(H):
                            for p in range(16):
                                for qg in range(2):
                                    g = eidx(h, qg, p)
                                    sc.wait_ge(sm["sp"], g + 1)
                                    if g >= 2:
                                        sc.wait_ge(sm["av"], g - 1)
                                        sc.wait_ge(sm["dn"], g - 1)
                                    sc.activation(ptb[qg][p % 2][:, :], s_ps[qg][:, :],
                                                  AF.Exp, scale=SCALE).then_inc(sm["ex"], 1)
                        sc.nop().then_inc(sm["bar"], 1)

                    @blk.tensor
                    def _(t):
                        t.wait_ge(sm["bar"], 5)

                        def accb(h, qg):
                            return acc_ps[0] if qg == 0 else acc_ps[1 + h % 2]

                        def av_pair(h, qg, p):
                            g = eidx(h, qg, p)
                            t.wait_ge(sm["ex"], g + 1)
                            if p == 0 and h >= 1:
                                t.wait_ge(sm["ev2"], 2 * h - 1 if qg == 0 else max(2 * h - 2, 0))
                            lm = None
                            for c in (2 * p, 2 * p + 1):
                                lm = t.matmul(accb(h, qg)[:, :],
                                              vhb[h % 2][:, c * 128:(c + 1) * 128],
                                              ptb[qg][p % 2][:, (c % 2) * 512:(c % 2) * 512 + 512],
                                              start=(c == 0), stop=(c == 31))
                            lm.then_inc(sm["av"], 1)

                        def den_rbc(h, qg, part):
                            hq = 2 * h + qg
                            if part == 0:
                                t.wait_ge(sm["dn"], h * 32 + 31 + qg)
                                t.wait_ge(sm["mk_pe"], hq + 1)
                                if hq >= 1:
                                    t.wait_ge(sm["rb"], hq)
                                t.matmul(den_ps[:, :], onec_bf[:, :], denvb[qg][:, 0:512],
                                         start=True, stop=False)
                                t.matmul(den_ps[:, :], onec_bf[:, :], denvb[qg][:, 512:1024],
                                         start=False, stop=True).then_inc(sm["pd"], 1)
                            else:
                                t.wait_ge(sm["rc"], hq + 1)
                                if hq >= 1:
                                    t.wait_ge(sm["rb"], hq)
                                t.matmul(rbc_ps[:, :], oner_f[:, :], r_sb[:, :],
                                         start=True, stop=True).then_inc(sm["pr"], 1)

                        pend = []
                        for h in range(H):
                            t.wait_ge(sm["ng"], h + 1)
                            t.wait_ge(sm["pfs0" if h % 2 == 0 else "pfs1"],
                                      64 * (h // 2 + 1))
                            for p in range(16):
                                for qg in range(2):
                                    g = eidx(h, qg, p)
                                    if p >= 1:
                                        t.wait_ge(sm["ex"], eidx(h, qg, p - 1) + 1)
                                    elif h >= 1:
                                        t.wait_ge(sm["ex"], eidx(h - 1, qg, 15) + 1)
                                    lm = None
                                    for c in (2 * p, 2 * p + 1):
                                        t.matmul(s_ps[qg][:, (c % 2) * 512:(c % 2) * 512 + 512],
                                                 khb[h % 2][:, c * 128:(c + 1) * 128],
                                                 qhb[h % 2][:, qg * 512:(qg + 1) * 512],
                                                 start=True, stop=False)
                                        lm = t.matmul(s_ps[qg][:, (c % 2) * 512:(c % 2) * 512 + 512],
                                                      sc128b[:, :],
                                                      negm[h % 2][:, qg * 512:(qg + 1) * 512],
                                                      start=False, stop=True)
                                    lm.then_inc(sm["sp"], 1)
                                for _ in range(2):
                                    if pend:
                                        den_rbc(*pend.pop(0))
                                for qg in range(2):
                                    if p >= 1:
                                        av_pair(h, qg, p - 1)
                            for qg in range(2):
                                av_pair(h, qg, 15)
                            pend += [(h, 0, 0), (h, 0, 1), (h, 1, 0), (h, 1, 1)]
                        for job in pend:
                            den_rbc(*job)
                        t.nop().then_inc(sm["bar"], 1)

                    @blk.vector
                    def _(v):
                        v.wait_ge(sm["bar"], 5)
                        v.wait_ge(sm["mk_ld"], 16)
                        v.reduce_max(mk2[:, :], kn4[:, :], axis=AX.X).then_inc(sm["mk_dve"], 1)
                        for rr in range(1, 4):
                            v.wait_ge(sm["mk_ld"], 16 * (rr + 1))
                            v.reduce_max(mk2b[:, :], kn4[:, :], axis=AX.X).then_inc(sm["mk_dve"], 1)
                            v.wait_ge(sm["mk_dve"], 2 * rr)
                            v.tensor_tensor(mk2[:, :], mk2[:, :], mk2b[:, :],
                                            ALU.max).then_inc(sm["mk_dve"], 1)
                        v.wait_ge(sm["mk_ld"], 96)
                        v.tensor_scalar(mk2n[:, :], mk2n[:, :], -0.5, None,
                                        ALU.mult).then_inc(sm["mk_dve"], 1)

                        def fin(h, qg):
                            hq = 2 * h + qg
                            v.wait_ge(sm["pd"], hq + 1)
                            if hq >= 1:
                                v.wait_ge(sm["pr"], hq)
                            v.reciprocal(r_sb[:, :], den_ps[:, :]).then_inc(sm["rc"], 1)
                            v.wait_ge(sm["pr"], hq + 1)
                            v.tensor_copy(rbc_sb[:, :], rbc_ps[:, :]).then_inc(sm["rb"], 1)
                            v.wait_ge(sm["rb"], hq + 1)
                            c0 = h * 1024 + qg * 512
                            accq = acc_ps[0] if qg == 0 else acc_ps[1 + h % 2]
                            v.tensor_tensor(attnT[:, c0:c0 + 512],
                                            accq[:, :], rbc_sb[:, :],
                                            ALU.mult).then_inc(sm["ev2"], 1)

                        for h in range(H):
                            v.wait_ge(sm["pfs0" if h % 2 == 0 else "pfs1"],
                                      64 * (h // 2 + 1))
                            if h >= 2:
                                v.wait_ge(sm["sp"], 32 * (h - 1))
                            if h == 0:
                                v.wait_ge(sm["mk_dve"], 8)
                            v.tensor_scalar(negm[h % 2][:, :], qnb[h % 2][:, :],
                                            -0.5, mk2n[:, h:h + 1],
                                            ALU.mult, ALU.add).then_inc(sm["ng"], 1)
                            if h >= 1:
                                fin(h - 1, 0)
                                fin(h - 1, 1)
                            for p in range(16):
                                for qg in range(2):
                                    g = eidx(h, qg, p)
                                    v.wait_ge(sm["ex"], g + 1)
                                    if p == 0:
                                        hq = 2 * h + qg
                                        if hq >= 2:
                                            v.wait_ge(sm["pd"], hq - 1)
                                        v.tensor_copy(denv[qg][:, :], ptb[qg][p % 2][:, :]
                                                      ).then_inc(sm["dn"], 1)
                                    else:
                                        v.wait_ge(sm["dn"], g)
                                        lastd = v.tensor_tensor(denv[qg][:, :], denv[qg][:, :],
                                                                ptb[qg][p % 2][:, :], ALU.add)
                                        lastd.then_inc(sm["dn"], 1)
                                        if p == 15:
                                            v.wait_ge(sm["dn"], g + 1)
                                            v.tensor_copy(denvb[qg][:, :], denv[qg][:, :]
                                                          ).then_inc(sm["mk_pe"], 1)
                        fin(H - 1, 0)
                        fin(H - 1, 1)
                        v.nop().then_inc(sm["bar"], 1)

                    @blk.gpsimd
                    def _(g):
                        g.wait_ge(sm["bar"], 5)
                        if PHASES >= 3:
                            for og in range(2):
                                k = 24 + og
                                g.wait_ge(sm["pj_mm"], pjg_cnt(2, 6 + og, 7))
                                srcp = woT.ap().rearrange("(d p) e -> p d e", p=128)[
                                    :, :, og * EGW:(og + 1) * EGW]
                                dstp = panel[k % 2].ap().rearrange("p (d e) -> p d e", e=EGW)
                                g.dma_start(out=dstp, in_=srcp).then_inc(
                                    sm["pan0" if k % 2 == 0 else "pan1"], 16)
                        if DEBUG:
                            g.wait_ge(sm["ev2"], 2 * H)
                            g.dma_start(out=dbg["d_at"][:, :], in_=attnT[:, :]
                                        ).then_inc(sm["dbg_st"], 16)
                            g.wait_ge(sm["dbg_st"], 16 * 6)
                        g.nop().then_inc(sm["bar"], 1)


                if PHASES >= 3:
                    with ExitStack() as ctx:
                        def sb(name, shape, dt):
                            return ctx.enter_context(nc.sbuf_tensor(name, shape, dt))

                        def ps(name, shape, dt):
                            return ctx.enter_context(nc.psum_tensor(name, shape, dt))

                        xog = [sb(f"xog{i}", [128, 8 * 512], F32) for i in range(2)]
                        piece = [sb(f"piece{i}", [128, 512], F32) for i in range(2)]
                        o_ps = [ps(f"o_ps{i}", [128, 512], F32) for i in range(2)]

                        blk = ctx.enter_context(nc.Block("p3"))

                        @blk.sync
                        def _(s):
                            s.wait_ge(sm["bar"], 10)
                            for og in range(NEG):
                                if og >= 2:
                                    s.wait_ge(sm["o_ev"], 8 * (og - 1))
                                s.dma_start(
                                    out=xog[og % 2].ap().rearrange("p (ts e) -> p ts e", e=EGW),
                                    in_=x_in.ap().rearrange("(ts p) e -> p ts e", p=128)[
                                        :, :, og * EGW:(og + 1) * EGW],
                                ).then_inc(sm["x3a" if og % 2 == 0 else "x3b"], 16)
                                for ts_ in range(8):
                                    idx = og * 8 + ts_
                                    s.wait_ge(sm["o_ev"], idx + 1)
                                    dst = x2loc.ap().rearrange("(ts p) e -> p ts e", p=128)[
                                        :, ts_:ts_ + 1, og * EGW:(og + 1) * EGW]
                                    s.dma_start(out=dst,
                                                in_=piece[idx % 2].ap().rearrange(
                                                    "p (one e) -> p one e", one=1)
                                                ).then_inc(sm["x2a" if idx % 2 == 0 else "x2b"], 16)
                            s.wait_ge(sm["x2a"], 16 * 32)
                            s.wait_ge(sm["x2b"], 16 * 32)
                            s.wait_ge(sm["x3a"], 16 * 4)
                            s.wait_ge(sm["x3b"], 16 * 4)
                            s.nop().then_inc(sm["bar"], 1)

                        @blk.tensor
                        def _(t):
                            t.wait_ge(sm["bar"], 10)
                            for og in range(NEG):
                                pan_w(t, 24 + og)
                                for ts_ in range(8):
                                    gi = og * 8 + ts_
                                    if gi >= 2:
                                        t.wait_ge(sm["o_ev"], gi - 1)
                                    lm = None
                                    for e in range(ND):
                                        lm = t.matmul(
                                            o_ps[gi % 2][:, :],
                                            attnT[:, e * 1024 + ts_ * 128:e * 1024 + (ts_ + 1) * 128],
                                            panel[(24 + og) % 2][:, e * EGW:(e + 1) * EGW],
                                            start=(e == 0), stop=(e == ND - 1))
                                    lm.then_inc(sm["o_mm"], 1)
                            t.nop().then_inc(sm["bar"], 1)

                        @blk.vector
                        def _(v):
                            v.wait_ge(sm["bar"], 10)
                            for og in range(NEG):
                                v.wait_ge(sm["x3a" if og % 2 == 0 else "x3b"],
                                          16 * (og // 2 + 1))
                                for ts_ in range(8):
                                    idx = og * 8 + ts_
                                    v.wait_ge(sm["o_mm"], idx + 1)
                                    if idx >= 2:
                                        v.wait_ge(sm["x2a" if idx % 2 == 0 else "x2b"],
                                                  16 * (idx // 2))
                                    v.tensor_tensor(piece[idx % 2][:, :],
                                                    o_ps[idx % 2][:, :],
                                                    xog[og % 2][:, ts_ * 512:(ts_ + 1) * 512],
                                                    ALU.add).then_inc(sm["o_ev"], 1)
                            v.nop().then_inc(sm["bar"], 1)

                        @blk.scalar
                        def _(sc):
                            sc.wait_ge(sm["bar"], 10)
                            sc.nop().then_inc(sm["bar"], 1)

                        @blk.gpsimd
                        def _(g):
                            g.wait_ge(sm["bar"], 10)
                            for og in range(2, NEG):
                                k = 24 + og
                                g.wait_ge(sm["o_mm"], 8 * (og - 1))
                                srcp = woT.ap().rearrange("(d p) e -> p d e", p=128)[
                                    :, :, og * EGW:(og + 1) * EGW]
                                dstp = panel[k % 2].ap().rearrange("p (d e) -> p d e", e=EGW)
                                g.dma_start(out=dstp, in_=srcp).then_inc(
                                    sm["pan0" if k % 2 == 0 else "pan1"], 16)
                            if PHASES >= 4:
                                for og in range(2):
                                    k = 32 + og
                                    g.wait_ge(sm["o_mm"], 8 * (og + 7))
                                    srcp = wfT.ap().rearrange("(d p) e -> p d e", p=128)[
                                        :, :, og * EGW:(og + 1) * EGW]
                                    dstp = panel[k % 2].ap().rearrange("p (d e) -> p d e", e=EGW)
                                    g.dma_start(out=dstp, in_=srcp).then_inc(
                                        sm["pan0" if k % 2 == 0 else "pan1"], 16)
                            if DEBUG:
                                g.wait_ge(sm["x2a"], 16 * 32)
                                g.wait_ge(sm["x2b"], 16 * 32)
                                g.dma_start(out=dbg["d_x2"][:, :], in_=x2loc[:, :]
                                            ).then_inc(sm["dbg_st"], 16)
                            g.nop().then_inc(sm["bar"], 1)

        if PHASES >= 4:
            with ExitStack() as p4:
                lnT2 = p4.enter_context(nc.sbuf_tensor("lnT2", [128, ND * 1024], BF16))
                normb = p4.enter_context(nc.sbuf_tensor("normb", [128, ND * EGW], BF16))

                with ExitStack() as ctx:
                    def sb(name, shape, dt):
                        return ctx.enter_context(nc.sbuf_tensor(name, shape, dt))

                    def ps(name, shape, dt):
                        return ctx.enter_context(nc.psum_tensor(name, shape, dt))

                    xt2 = sb("xt2", [128, D], F32)
                    lnbf2 = [sb(f"lnbf2_{i}", [128, D], BF16) for i in range(2)]
                    stats2 = [sb(f"stats2_{i}", [128, 48], F32) for i in range(2)]
                    mv2 = [sb(f"mv2_{i}", [128, 2], F32) for i in range(2)]
                    lnv2 = [sb(f"lnv2_{i}", [128, 1], F32) for i in range(2)]
                    rstd2 = [sb(f"rstd2_{i}", [128, 1], F32) for i in range(2)]
                    tp2_ps = [ps(f"tp2_ps{i}", [128, 128], BF16) for i in range(2)]

                    blk = ctx.enter_context(nc.Block("p4a"))

                    @blk.sync
                    def _(s):
                        s.wait_ge(sm["bar"], 15)
                        for i in range(NT):
                            if i >= 1:
                                s.wait_ge(sm["ln_apply"], NT + i)
                            s.dma_start(out=xt2[:, :],
                                        in_=x2loc[128 * i:128 * (i + 1), :]).then_inc(sm["x4"], 16)
                        s.wait_ge(sm["x4"], 16 * NT)
                        s.nop().then_inc(sm["bar"], 1)

                    @blk.vector
                    def _(v):
                        v.wait_ge(sm["bar"], 15)
                        for i in range(NT + 1):
                            if i < NT:
                                v.wait_ge(sm["x4"], 16 * (i + 1))
                                ins = None
                                for j in range(8):
                                    ins = v.bn_stats(stats2[i % 2][:, 6 * j:6 * j + 6],
                                                     xt2[:, 512 * j:512 * (j + 1)])
                                ins.then_inc(sm["ln_bn"], 1)
                                v.wait_ge(sm["ln_bn"], NT + i + 1)
                                v.bn_aggr(mv2[i % 2][:, :], stats2[i % 2][:, :]).then_inc(sm["ln_mv"], 1)
                                v.wait_ge(sm["ln_rstd"], 2 * NT + 2 * i + 2)
                                if i >= 2:
                                    v.wait_ge(sm["tp_pe"], NT * ND + 32 * (i - 1))
                                v.tensor_scalar(lnbf2[i % 2][:, :], xt2[:, :],
                                                mv2[i % 2][:, 0:1], rstd2[i % 2][:, :],
                                                ALU.subtract, ALU.mult).then_inc(sm["ln_apply"], 1)
                            if i >= 1:
                                ti = i - 1
                                for d in range(ND):
                                    j = ti * ND + d
                                    v.wait_ge(sm["tp_pe"], NT * ND + j + 1)
                                    v.tensor_copy(lnT2[:, d * 1024 + ti * 128:d * 1024 + (ti + 1) * 128],
                                                  tp2_ps[j % 2][:, :]).then_inc(sm["tp_ev"], 1)
                        v.nop().then_inc(sm["bar"], 1)

                    @blk.scalar
                    def _(sc):
                        sc.wait_ge(sm["bar"], 15)
                        for i in range(NT):
                            sc.wait_ge(sm["ln_mv"], NT + i + 1)
                            sc.activation(lnv2[i % 2][:, :], mv2[i % 2][:, 1:2], AF.Ln,
                                          scale=1.0, bias=csb[:, 0:1]).then_inc(sm["ln_rstd"], 1)
                            sc.wait_ge(sm["ln_rstd"], 2 * NT + 2 * i + 1)
                            sc.activation(rstd2[i % 2][:, :], lnv2[i % 2][:, :], AF.Exp,
                                          scale=-0.5, bias=csb[:, 1:2]).then_inc(sm["ln_rstd"], 1)
                        sc.nop().then_inc(sm["bar"], 1)

                    @blk.tensor
                    def _(t):
                        t.wait_ge(sm["bar"], 15)
                        for i in range(NT):
                            t.wait_ge(sm["ln_apply"], NT + i + 1)
                            for d in range(ND):
                                j = i * ND + d
                                if j >= 2:
                                    t.wait_ge(sm["tp_ev"], NT * ND + j - 1)
                                t.transpose(tp2_ps[j % 2][:, :],
                                            lnbf2[i % 2][:, 128 * d:128 * (d + 1)],
                                            eyeb[:, :]).then_inc(sm["tp_pe"], 1)
                        t.nop().then_inc(sm["bar"], 1)

                    @blk.gpsimd
                    def _(g):
                        g.wait_ge(sm["bar"], 15)
                        srcn = normrep.ap().rearrange("(d p) e -> p d e", p=128)[:, :, 0:EGW]
                        dstn = normb.ap().rearrange("p (d e) -> p d e", e=EGW)
                        g.dma_start(out=dstn, in_=srcn).then_inc(sm["nrm"], 16)
                        g.nop().then_inc(sm["bar"], 1)

                with ExitStack() as ctx:
                    def sb(name, shape, dt):
                        return ctx.enter_context(nc.sbuf_tensor(name, shape, dt))

                    def ps(name, shape, dt):
                        return ctx.enter_context(nc.psum_tensor(name, shape, dt))

                    x2og = [sb(f"x2og{i}", [128, 8 * 512], F32) for i in range(2)]
                    biasog = [sb(f"biasog{i}", [128, 512], F32) for i in range(2)]
                    piece = [sb(f"fpiece{i}", [128, 512], F32) for i in range(2)]
                    f_ps = [ps(f"f_ps{i}", [128, 512], F32) for i in range(2)]

                    blk = ctx.enter_context(nc.Block("p4b"))

                    @blk.sync
                    def _(s):
                        s.wait_ge(sm["bar"], 20)
                        for og in range(NEG):
                            if og >= 2:
                                s.wait_ge(sm["f_ev"], 16 * (og - 1))
                            s.dma_start(
                                out=x2og[og % 2].ap().rearrange("p (ts e) -> p ts e", e=EGW),
                                in_=x2loc.ap().rearrange("(ts p) e -> p ts e", p=128)[
                                    :, :, og * EGW:(og + 1) * EGW],
                            ).then_inc(sm["xg4a" if og % 2 == 0 else "xg4b"], 16)
                            s.dma_start(out=biasog[og % 2][:, :],
                                        in_=bias_rep[:, og * EGW:(og + 1) * EGW]
                                        ).then_inc(sm["bl4a" if og % 2 == 0 else "bl4b"], 16)
                            for ts_ in range(8):
                                idx = og * 8 + ts_
                                s.wait_ge(sm["f_ev"], 2 * (idx + 1))
                                dst = out_ext.ap().rearrange("(ts p) e -> p ts e", p=128)[
                                    :, ts_:ts_ + 1, og * EGW:(og + 1) * EGW]
                                s.dma_start(out=dst,
                                            in_=piece[idx % 2].ap().rearrange(
                                                "p (one e) -> p one e", one=1)
                                            ).then_inc(sm["o4a" if idx % 2 == 0 else "o4b"], 16)
                        s.nop().then_inc(sm["bar"], 1)

                    @blk.gpsimd
                    def _(g):
                        g.wait_ge(sm["bar"], 20)
                        for og in range(NEG):
                            if og >= 2:
                                k = 32 + og
                                g.wait_ge(sm["f_mm"], 8 * (og - 1))
                                srcp = wfT.ap().rearrange("(d p) e -> p d e", p=128)[
                                    :, :, og * EGW:(og + 1) * EGW]
                                dstp = panel[k % 2].ap().rearrange("p (d e) -> p d e", e=EGW)
                                g.dma_start(out=dstp, in_=srcp).then_inc(
                                    sm["pan0" if k % 2 == 0 else "pan1"], 16)
                            if og >= 1:
                                g.wait_ge(sm["dq"], og)
                                srcn = normrep.ap().rearrange("(d p) e -> p d e", p=128)[
                                    :, :, og * EGW:(og + 1) * EGW]
                                dstn = normb.ap().rearrange("p (d e) -> p d e", e=EGW)
                                g.dma_start(out=dstn, in_=srcn).then_inc(sm["nrm"], 16)
                        g.wait_ge(sm["o4a"], 16 * 32)
                        g.wait_ge(sm["o4b"], 16 * 32)
                        g.wait_ge(sm["bar"], 24)
                        g.nop().then_inc(sm["bar"], 1)

                    @blk.vector
                    def _(v):
                        v.wait_ge(sm["bar"], 20)
                        for og in range(NEG):
                            k = 32 + og
                            pan_w(v, k)
                            v.wait_ge(sm["nrm"], 16 * (og + 1))
                            if og >= 2:
                                v.wait_ge(sm["f_mm"], 8 * (og - 1))
                            v.tensor_tensor(panel[k % 2][:, :], panel[k % 2][:, :],
                                            normb[:, :], ALU.mult).then_inc(sm["dq"], 1)
                            v.wait_ge(sm["xg4a" if og % 2 == 0 else "xg4b"],
                                      16 * (og // 2 + 1))
                            v.wait_ge(sm["bl4a" if og % 2 == 0 else "bl4b"],
                                      16 * (og // 2 + 1))
                            for ts_ in range(8):
                                idx = og * 8 + ts_
                                v.wait_ge(sm["f_mm"], idx + 1)
                                if idx >= 2:
                                    v.wait_ge(sm["o4a" if idx % 2 == 0 else "o4b"],
                                              16 * (idx // 2))
                                v.tensor_tensor(piece[idx % 2][:, :], f_ps[idx % 2][:, :],
                                                biasog[og % 2][:, :], ALU.add
                                                ).then_inc(sm["f_ev"], 1)
                                v.wait_ge(sm["f_ev"], 2 * idx + 1)
                                v.tensor_tensor(piece[idx % 2][:, :], piece[idx % 2][:, :],
                                                x2og[og % 2][:, ts_ * 512:(ts_ + 1) * 512],
                                                ALU.add).then_inc(sm["f_ev"], 1)
                        v.nop().then_inc(sm["bar"], 1)

                    @blk.tensor
                    def _(t):
                        t.wait_ge(sm["bar"], 20)
                        for og in range(NEG):
                            t.wait_ge(sm["dq"], og + 1)
                            k = 32 + og
                            for ts_ in range(8):
                                gi = og * 8 + ts_
                                if gi >= 2:
                                    t.wait_ge(sm["f_ev"], 2 * (gi - 1))
                                lm = None
                                for d in range(ND):
                                    lm = t.matmul(
                                        f_ps[gi % 2][:, :],
                                        lnT2[:, d * 1024 + ts_ * 128:d * 1024 + (ts_ + 1) * 128],
                                        panel[k % 2][:, d * EGW:(d + 1) * EGW],
                                        start=(d == 0), stop=(d == ND - 1))
                                lm.then_inc(sm["f_mm"], 1)
                            t.nop().then_inc(sm["bar"], 1) if og == NEG - 1 else None

                    @blk.scalar
                    def _(sc):
                        sc.wait_ge(sm["bar"], 20)
                        sc.nop().then_inc(sm["bar"], 1)
        if PHASES < 4:
            ndbg = 5 + (1 if PHASES >= 2 else 0) + (1 if PHASES >= 3 else 0)
            with ExitStack() as ctx:
                blk = ctx.enter_context(nc.Block("fin"))

                @blk.gpsimd
                def _(g):
                    if DEBUG:
                        g.wait_ge(sm["dbg_st"], 16 * ndbg)
                    g.wait_ge(sm["bar"], 5 * PHASES)

    return nc, dbg


def _host_layout(L0_attn_x, L0_attn_w_q, L0_attn_w_k, L0_attn_w_v, L0_attn_w_o,
                 L0_ffn_weight_q, L0_ffn_weight_norm, L0_ffn_bias):
    x = np.ascontiguousarray(np.asarray(L0_attn_x).astype(np.float32, copy=False).reshape(NC * T, D))
    wqT_ = np.ascontiguousarray(np.asarray(L0_attn_w_q).astype(np.float32, copy=False).T)
    wkT_ = np.ascontiguousarray(np.asarray(L0_attn_w_k).astype(np.float32, copy=False).T)
    wvT_ = np.ascontiguousarray(np.asarray(L0_attn_w_v).astype(np.float32, copy=False).T)
    woT_ = np.ascontiguousarray(np.asarray(L0_attn_w_o).astype(np.float32, copy=False).T)
    wfT_ = np.ascontiguousarray(np.asarray(L0_ffn_weight_q).astype(np.int8, copy=False).reshape(D, D).T)
    nrm = np.asarray(L0_ffn_weight_norm).astype(np.float16, copy=False).reshape(D, D // 16)
    normrep_ = np.ascontiguousarray(np.repeat(nrm.T, 16, axis=0))
    bias_rep_ = np.ascontiguousarray(
        np.tile(np.asarray(L0_ffn_bias).astype(np.float32, copy=False)[None, :], (128, 1)))
    base = {
        "wqT": wqT_, "wkT": wkT_, "wvT": wvT_, "woT": woT_,
        "wfT": wfT_, "normrep": normrep_, "bias_rep": bias_rep_,
        "ones_1x128_bf": np.ones((1, 128), ml_dtypes.bfloat16),
        "ones_128x1_bf": np.ones((128, 1), ml_dtypes.bfloat16),
        "ones_128x1_f": np.ones((128, 1), np.float32),
        "ones_1x128_f": np.ones((1, 128), np.float32),
        "eye_bf": np.eye(128).astype(ml_dtypes.bfloat16),
        "sc128": np.full((128, 128), 1.0 / 128.0, ml_dtypes.bfloat16),
        "eye_f": np.eye(128, dtype=np.float32),
        "consts": np.tile(np.array([[1e-5, -np.log(15.0), 0.0, 0.0]], np.float32), (128, 1)),
    }
    in_maps = []
    for c in range(NC):
        m = dict(base)
        m["x_in"] = np.ascontiguousarray(x[c * T:(c + 1) * T])
        in_maps.append(m)
    return in_maps


def kernel(**inputs):
    in_maps = _host_layout(**inputs)
    nc, _ = build_nc()
    res = run_bass_kernel_spmd(nc, in_maps, core_ids=list(range(NC)))
    out = np.concatenate([np.asarray(r["out"]) for r in res.results], axis=0)
    return out.reshape(2, S, D).astype(np.float32)


# version 06
# speedup vs baseline: 1.0819x; 1.0687x over previous
import numpy as np
import ml_dtypes
from contextlib import ExitStack

import concourse.bass as bass
import concourse.mybir as mybir
from concourse.bass_utils import run_bass_kernel_spmd

F32 = mybir.dt.float32
BF16 = mybir.dt.bfloat16
F16 = mybir.dt.float16
I8 = mybir.dt.int8
AF = mybir.ActivationFunctionType
ALU = mybir.AluOpType
AX = mybir.AxisListType

D = 4096
T = 1024
H = 32
S = 4096
NT = T // 128
ND = D // 128
NEG = 8
EGW = 512
NC = 8
GRP = 4

SCALE = float(1.0 / np.sqrt(128.0))
LN15 = float(np.log(15.0))

DEBUG = False
PHASES = 4
NO_CC = False


def build_nc():
    nc = bass.Bass("TRN2", target_bir_lowering=False)

    x_in = nc.dram_tensor("x_in", [T, D], F32, kind="ExternalInput")
    wqT = nc.dram_tensor("wqT", [D, D], F32, kind="ExternalInput")
    wkT = nc.dram_tensor("wkT", [D, D], F32, kind="ExternalInput")
    wvT = nc.dram_tensor("wvT", [D, D], F32, kind="ExternalInput")
    woT = nc.dram_tensor("woT", [D, D], F32, kind="ExternalInput")
    wfT = nc.dram_tensor("wfT", [D, D], I8, kind="ExternalInput")
    normrep = nc.dram_tensor("normrep", [D, D], F16, kind="ExternalInput")
    bias_rep = nc.dram_tensor("bias_rep", [128, D], F32, kind="ExternalInput")
    ones_1x128_bf = nc.dram_tensor("ones_1x128_bf", [1, 128], BF16, kind="ExternalInput")
    ones_128x1_bf = nc.dram_tensor("ones_128x1_bf", [128, 1], BF16, kind="ExternalInput")
    ones_128x1_f = nc.dram_tensor("ones_128x1_f", [128, 1], F32, kind="ExternalInput")
    ones_1x128_f = nc.dram_tensor("ones_1x128_f", [1, 128], F32, kind="ExternalInput")
    eye_bf = nc.dram_tensor("eye_bf", [128, 128], BF16, kind="ExternalInput")
    sc128 = nc.dram_tensor("sc128", [128, 128], BF16, kind="ExternalInput")
    eye_f = nc.dram_tensor("eye_f", [128, 128], F32, kind="ExternalInput")
    consts = nc.dram_tensor("consts", [128, 4], F32, kind="ExternalInput")

    out_ext = nc.dram_tensor("out", [T, D], F32, kind="ExternalOutput")

    qT_loc = nc.dram_tensor("qT_loc", [D, T], BF16, kind="Internal")
    kT_loc = nc.dram_tensor("kT_loc", [D, T], BF16, kind="Internal")
    v_loc = nc.dram_tensor("v_loc", [T, D], BF16, kind="Internal")
    knloc = nc.dram_tensor("knloc", [H, T], BF16, kind="Internal")
    qnloc = nc.dram_tensor("qnloc", [H, T], BF16, kind="Internal")
    kg = nc.dram_tensor("kg", [GRP * D, T], BF16, kind="Internal")
    vg = nc.dram_tensor("vg", [S, D], BF16, kind="Internal")
    kng = nc.dram_tensor("kng", [GRP * H, T], BF16, kind="Internal")
    mkd = nc.dram_tensor("mkd", [1, H], F32, kind="Internal")
    x2loc = nc.dram_tensor("x2loc", [T, D], F32, kind="Internal")

    dbg = {}
    if DEBUG:
        dbg["d_qT"] = nc.dram_tensor("d_qT", [D, T], BF16, kind="ExternalOutput")
        dbg["d_kg"] = nc.dram_tensor("d_kg", [GRP * D, T], BF16, kind="ExternalOutput")
        dbg["d_vg"] = nc.dram_tensor("d_vg", [S, D], BF16, kind="ExternalOutput")
        dbg["d_kng"] = nc.dram_tensor("d_kng", [GRP * H, T], BF16, kind="ExternalOutput")
        dbg["d_qn"] = nc.dram_tensor("d_qn", [H, T], BF16, kind="ExternalOutput")
        if PHASES >= 2:
            dbg["d_at"] = nc.dram_tensor("d_at", [128, H * T], BF16, kind="ExternalOutput")
        if PHASES >= 3:
            dbg["d_x2"] = nc.dram_tensor("d_x2", [T, D], F32, kind="ExternalOutput")

    RG = [[0, 1, 2, 3], [4, 5, 6, 7]]

    with ExitStack() as g_ctx:
        def g_sb(name, shape, dt):
            return g_ctx.enter_context(nc.sbuf_tensor(name, shape, dt))

        def g_sem(name):
            return g_ctx.enter_context(nc.semaphore(name))

        panel = [g_sb(f"panel{i}", [128, ND * EGW], BF16) for i in range(2)]
        onec_bf = g_sb("onec_bf", [128, 1], BF16)
        oner_bf = g_sb("oner_bf", [1, 128], BF16)
        onec_f = g_sb("onec_f", [128, 1], F32)
        oner_f = g_sb("oner_f", [1, 128], F32)
        eyeb = g_sb("eyeb", [128, 128], BF16)
        sc128b = g_sb("sc128b", [128, 128], BF16)
        eyef = g_sb("eyef", [128, 128], F32)
        csb = g_sb("csb", [128, 4], F32)

        names = ("c_ld bar x_ld ln_bn ln_mv ln_rstd ln_apply tp_pe tp_ev "
                 "pan0 pan1 x0 x1 pfs0 pfs1 pcv sg0 sg1 sg2 sg3 sg4 sg5 sg6 sg7 pj_mm pj_ev sq_tt n_mm n_ev knst qnst kst vst qst kdone vdone kndone cc_s "
                 "pfs pfv mk_ld mk_dve mk_pe ng sp ex dn av pd rc pr rb ev2 "
                 "x3a x3b x2a x2b o_mm o_ev nrm dq f_mm f_ev x4 xg4a xg4b bl4a bl4b o4a o4b dbg_st").split()
        sm = {n: g_sem(n) for n in names}

        PASSES = [("k", wkT, kT_loc), ("v", wvT, v_loc), ("q", wqT, qT_loc)]

        SWP = [0, 1] + list(range(24, 40))

        def pan_w(eng, k):
            if 2 <= k < 24:
                eng.wait_ge(sm["pcv"], 32 * (k - 1))
            else:
                i = SWP.index(k)
                par = [j for j in SWP if j % 2 == k % 2].index(k)
                eng.wait_ge(sm["pan0" if k % 2 == 0 else "pan1"], 16 * (par + 1))

        def pjg_cnt(pi, eg, gi):
            return pi * NEG * 8 + eg * 8 + gi + 1

        def slab_guard(eng, k):
            j = k - 1
            if k - 2 < 8:
                eng.wait_ge(sm["kst"], 64 * min(j + 1, 8))
            elif k - 2 < 16:
                eng.wait_ge(sm["vst"], 16 * (min(j, 15) - 7))
            else:
                eng.wait_ge(sm["qst"], 64 * (j - 15))

        with ExitStack() as ctx:
            def sb(name, shape, dt):
                return ctx.enter_context(nc.sbuf_tensor(name, shape, dt))

            def ps(name, shape, dt):
                return ctx.enter_context(nc.psum_tensor(name, shape, dt))

            xt = [sb(f"xt{i}", [128, D], F32) for i in range(2)]
            lnbf = [sb(f"lnbf{i}", [128, D], BF16) for i in range(2)]
            lnT = sb("lnT", [128, ND * 1024], BF16)
            stats = [sb(f"stats{i}", [128, 48], F32) for i in range(2)]
            mv = [sb(f"mv{i}", [128, 2], F32) for i in range(2)]
            lnv = [sb(f"lnv{i}", [128, 1], F32) for i in range(2)]
            rstd = [sb(f"rstd{i}", [128, 1], F32) for i in range(2)]
            slab = [sb(f"slab{i}", [128, 4096], BF16) for i in range(2)]
            sqt = [sb(f"sqt{i}", [128, 1024], BF16) for i in range(2)]
            nst = [sb(f"nst{i}", [1, 1024], BF16) for i in range(4)]
            stg = [lnbf[i // 4].ap().bitcast(F32)[:, (i % 4) * 512:(i % 4) * 512 + 512]
                   for i in range(8)]
            pj_ps = [ps(f"pj_ps{i}", [128, 512], F32) for i in range(2)]
            kn_ps = ps("kn_ps", [1, 1024], F32)
            tp_ps = [ps(f"tp_ps{i}", [128, 128], BF16) for i in range(2)]

            blk = ctx.enter_context(nc.Block("p1"))

            @blk.sync
            def _(s):
                s.dma_start(out=onec_bf[:, :], in_=ones_128x1_bf[:, :]).then_inc(sm["c_ld"], 16)
                s.dma_start(out=oner_bf[:, :], in_=ones_1x128_bf[:, :]).then_inc(sm["c_ld"], 16)
                s.dma_start(out=onec_f[:, :], in_=ones_128x1_f[:, :]).then_inc(sm["c_ld"], 16)
                s.dma_start(out=oner_f[:, :], in_=ones_1x128_f[:, :]).then_inc(sm["c_ld"], 16)
                s.dma_start(out=eyeb[:, :], in_=eye_bf[:, :]).then_inc(sm["c_ld"], 16)
                s.dma_start(out=sc128b[:, :], in_=sc128[:, :]).then_inc(sm["c_ld"], 16)
                s.dma_start(out=eyef[:, :], in_=eye_f[:, :]).then_inc(sm["c_ld"], 16)
                s.dma_start(out=csb[:, :], in_=consts[:, :]).then_inc(sm["c_ld"], 16)
                for i in range(NT):
                    if i >= 2:
                        s.wait_ge(sm["ln_apply"], i - 1)
                    s.dma_start(out=xt[i % 2][:, :],
                                in_=x_in[128 * i:128 * (i + 1), :]).then_inc(
                                    sm["x0" if i % 2 == 0 else "x1"], 16)
                for pi, (pname, _w, locT) in enumerate(PASSES):
                    stsem = {"k": "kst", "v": "vst", "q": "qst"}[pname]
                    for eg in range(NEG):
                        k = pi * NEG + eg
                        s.wait_ge(sm["pj_ev"], pjg_cnt(pi, eg, 7))
                        if pname in ("k", "q"):
                            for ec in range(4):
                                r0 = eg * 512 + ec * 128
                                s.dma_start(out=locT[r0:r0 + 128, :],
                                            in_=slab[eg % 2][:, ec * 1024:(ec + 1) * 1024]
                                            ).then_inc(sm[stsem], 16)
                        else:
                            dst = v_loc.ap().rearrange("(ts p) e -> p ts e", p=128)[
                                :, :, eg * EGW:(eg + 1) * EGW]
                            src = slab[eg % 2].ap().rearrange("p (ts e) -> p ts e", e=EGW)
                            s.dma_start(out=dst, in_=src).then_inc(sm[stsem], 16)
                    if pname == "k":
                        s.wait_ge(sm["kst"], 64 * 8)
                        s.nop().then_inc(sm["kdone"], 1)
                    elif pname == "v":
                        s.wait_ge(sm["vst"], 16 * 8)
                        s.nop().then_inc(sm["vdone"], 1)
                s.wait_ge(sm["qst"], 64 * 8)
                s.nop().then_inc(sm["bar"], 1)

            @blk.gpsimd
            def _(g):
                for k in range(24):
                    pi, eg = divmod(k, NEG)
                    wT = PASSES[pi][1]
                    if k < 2:
                        src = wT.ap().rearrange("(d p) e -> p d e", p=128)[
                            :, :, eg * EGW:(eg + 1) * EGW]
                        dst = panel[k % 2].ap().rearrange("p (d e) -> p d e", e=EGW)
                        g.dma_start(out=dst, in_=src).then_inc(
                            sm["pan0" if k % 2 == 0 else "pan1"], 16)
                    else:
                        g.wait_ge(sm["pj_mm"], pjg_cnt(*divmod(k - 2, NEG), 7))
                        for d in range(ND):
                            j = (k - 2) * ND + d
                            g.wait_ge(sm[f"sg{j % 8}"], 16 * (j // 8 + 1))
                            g.tensor_copy(panel[k % 2][:, d * EGW:(d + 1) * EGW],
                                          stg[j % 8]).then_inc(sm["pcv"], 1)
                    if k == 13:
                        g.wait_ge(sm["kdone"], 1)
                        g.wait_ge(sm["kndone"], 1)
                        if NO_CC:
                            g.dma_start(out=kg[0:D, :], in_=kT_loc[:, :]).then_inc(sm["cc_s"], 16)
                            g.dma_start(out=kng[0:H, :], in_=knloc[:, :]).then_inc(sm["cc_s"], 16)
                            g.wait_ge(sm["cc_s"], 32)
                        else:
                            g.collective_compute(
                                "AllGather", ALU.bypass, replica_groups=RG,
                                ins=[kT_loc.ap().opt()], outs=[kg.ap().opt()],
                            ).then_inc(sm["cc_s"], 1)
                            g.collective_compute(
                                "AllGather", ALU.bypass, replica_groups=RG,
                                ins=[knloc.ap().opt()], outs=[kng.ap().opt()],
                            ).then_inc(sm["cc_s"], 1)
                    if k == 21:
                        g.wait_ge(sm["vdone"], 1)
                        if NO_CC:
                            g.dma_start(out=vg[0:T, :], in_=v_loc[:, :]).then_inc(sm["cc_s"], 16)
                            g.wait_ge(sm["cc_s"], 48)
                        else:
                            g.collective_compute(
                                "AllGather", ALU.bypass, replica_groups=RG,
                                ins=[v_loc.ap().opt()], outs=[vg.ap().opt()],
                            ).then_inc(sm["cc_s"], 1)
                if DEBUG:
                    g.wait_ge(sm["cc_s"], 48 if NO_CC else 3)
                    g.wait_ge(sm["qst"], 64 * 8)
                    g.wait_ge(sm["qnst"], 16 * H)
                    g.dma_start(out=dbg["d_qT"][:, :], in_=qT_loc[:, :]).then_inc(sm["dbg_st"], 16)
                    g.dma_start(out=dbg["d_kg"][:, :], in_=kg[:, :]).then_inc(sm["dbg_st"], 16)
                    g.dma_start(out=dbg["d_vg"][:, :], in_=vg[:, :]).then_inc(sm["dbg_st"], 16)
                    g.dma_start(out=dbg["d_kng"][:, :], in_=kng[:, :]).then_inc(sm["dbg_st"], 16)
                    g.dma_start(out=dbg["d_qn"][:, :], in_=qnloc[:, :]).then_inc(sm["dbg_st"], 16)
                g.nop().then_inc(sm["bar"], 1)

            @blk.tensor
            def _(t):
                t.wait_ge(sm["c_ld"], 16 * 8)
                for i in range(NT):
                    t.wait_ge(sm["ln_apply"], i + 1)
                    for d in range(ND):
                        j = i * ND + d
                        if j >= 2:
                            t.wait_ge(sm["tp_ev"], j - 1)
                        t.transpose(tp_ps[j % 2][:, :],
                                    lnbf[i % 2][:, 128 * d:128 * (d + 1)],
                                    eyeb[:, :]).then_inc(sm["tp_pe"], 1)
                t.wait_ge(sm["tp_ev"], NT * ND)
                for pi, (pname, _w, _l) in enumerate(PASSES):
                    pend = []
                    for eg in range(NEG):
                        k = pi * NEG + eg
                        pan_w(t, k)
                        for gi in range(8):
                            gidx = pi * NEG * 8 + eg * 8 + gi
                            if gidx >= 2:
                                t.wait_ge(sm["pj_ev"], gidx - 1)
                            if pname in ("k", "q"):
                                ec, th = divmod(gi, 2)
                                lastm = None
                                for d in range(ND):
                                    lastm = t.matmul(
                                        pj_ps[gidx % 2][:, :],
                                        panel[k % 2][:, d * EGW + ec * 128:d * EGW + (ec + 1) * 128],
                                        lnT[:, d * 1024 + th * 512:d * 1024 + (th + 1) * 512],
                                        start=(d == 0), stop=(d == ND - 1))
                            else:
                                ts_ = gi
                                for d in range(ND):
                                    lastm = t.matmul(
                                        pj_ps[gidx % 2][:, :],
                                        lnT[:, d * 1024 + ts_ * 128:d * 1024 + (ts_ + 1) * 128],
                                        panel[k % 2][:, d * EGW:(d + 1) * EGW],
                                        start=(d == 0), stop=(d == ND - 1))
                            lastm.then_inc(sm["pj_mm"], 1)
                            if pname in ("k", "q") and pend:
                                h = pend.pop(0)
                                gh = (0 if pname == "k" else H) + h
                                t.wait_ge(sm["sq_tt"], (gh % H) * 2 + 2 + (0 if pname == "k" else 2 * H))
                                if gh >= 1:
                                    t.wait_ge(sm["n_ev"], gh)
                                lm = None
                                for th2 in range(2):
                                    lm = t.matmul(kn_ps[:, th2 * 512:(th2 + 1) * 512],
                                                  onec_bf[:, :],
                                                  sqt[h % 2][:, th2 * 512:(th2 + 1) * 512],
                                                  start=True, stop=True)
                                lm.then_inc(sm["n_mm"], 1)
                            if pname in ("k", "q") and gi % 2 == 1:
                                pend.append(eg * 4 + gi // 2)
                    for h in pend:
                        gh = (0 if pname == "k" else H) + h
                        t.wait_ge(sm["sq_tt"], (gh % H) * 2 + 2 + (0 if pname == "k" else 2 * H))
                        if gh >= 1:
                            t.wait_ge(sm["n_ev"], gh)
                        lm = None
                        for th2 in range(2):
                            lm = t.matmul(kn_ps[:, th2 * 512:(th2 + 1) * 512],
                                          onec_bf[:, :],
                                          sqt[h % 2][:, th2 * 512:(th2 + 1) * 512],
                                          start=True, stop=True)
                        lm.then_inc(sm["n_mm"], 1)
                    pend.clear()
                t.nop().then_inc(sm["bar"], 1)

            @blk.scalar
            def _(sc):
                sc.wait_ge(sm["c_ld"], 16 * 8)
                for i in range(NT):
                    sc.wait_ge(sm["ln_mv"], i + 1)
                    sc.activation(lnv[i % 2][:, :], mv[i % 2][:, 1:2], AF.Ln,
                                  scale=1.0, bias=csb[:, 0:1]).then_inc(sm["ln_rstd"], 1)
                    sc.wait_ge(sm["ln_rstd"], 2 * i + 1)
                    sc.activation(rstd[i % 2][:, :], lnv[i % 2][:, :], AF.Exp,
                                  scale=-0.5).then_inc(sm["ln_rstd"], 1)
                def norm_store(gh):
                    pnameh = "k" if gh < H else "q"
                    h = gh % H
                    nloc = knloc if pnameh == "k" else qnloc
                    nsem = "knst" if pnameh == "k" else "qnst"
                    sc.wait_ge(sm["n_ev"], gh + 1)
                    sc.dma_start(out=nloc[h:h + 1, :],
                                 in_=nst[gh % 4][:, :]).then_inc(sm[nsem], 16)
                    if gh == H - 1:
                        sc.wait_ge(sm["knst"], 16 * H)
                        sc.nop().then_inc(sm["kndone"], 1)

                sc.wait_ge(sm["tp_ev"], NT * ND)
                for k in range(2, 24):
                    pi, eg = divmod(k, NEG)
                    wT = PASSES[pi][1]
                    kk = k - 3
                    if 0 <= kk < 8:
                        for ec in range(4):
                            norm_store(kk * 4 + ec)
                    elif kk >= 16:
                        for ec in range(4):
                            norm_store(H + (kk - 16) * 4 + ec)
                    for d in range(ND):
                        j = (k - 2) * ND + d
                        if j >= 8:
                            sc.wait_ge(sm["pcv"], j - 7)
                        srcp = wT.ap().rearrange("(d p) e -> p d e", p=128)[
                            :, d:d + 1, eg * EGW:(eg + 1) * EGW]
                        sc.dma_start(out=stg[j % 8].rearrange("p e -> p () e"),
                                     in_=srcp).then_inc(sm[f"sg{j % 8}"], 16)
                for gh in range(H + 5 * 4, 2 * H):
                    norm_store(gh)
                sc.wait_ge(sm["qnst"], 16 * H)
                sc.nop().then_inc(sm["bar"], 1)

            @blk.vector
            def _(v):
                for i in range(NT + 1):
                    if i < NT:
                        v.wait_ge(sm["x0" if i % 2 == 0 else "x1"], 16 * (i // 2 + 1))
                        ins = None
                        for j in range(8):
                            ins = v.bn_stats(stats[i % 2][:, 6 * j:6 * j + 6],
                                             xt[i % 2][:, 512 * j:512 * (j + 1)])
                        ins.then_inc(sm["ln_bn"], 1)
                        v.wait_ge(sm["ln_bn"], i + 1)
                        v.bn_aggr(mv[i % 2][:, :], stats[i % 2][:, :]).then_inc(sm["ln_mv"], 1)
                        v.wait_ge(sm["ln_rstd"], 2 * i + 2)
                        if i >= 2:
                            v.wait_ge(sm["tp_pe"], 32 * (i - 1))
                        v.tensor_scalar(lnbf[i % 2][:, :], xt[i % 2][:, :],
                                        mv[i % 2][:, 0:1], rstd[i % 2][:, :],
                                        ALU.subtract, ALU.mult).then_inc(sm["ln_apply"], 1)
                    if i >= 1:
                        ti = i - 1
                        for d in range(ND):
                            j = ti * ND + d
                            v.wait_ge(sm["tp_pe"], j + 1)
                            v.tensor_copy(lnT[:, d * 1024 + ti * 128:d * 1024 + (ti + 1) * 128],
                                          tp_ps[j % 2][:, :]).then_inc(sm["tp_ev"], 1)
                def nevac(h, gh):
                    v.wait_ge(sm["n_mm"], gh + 1)
                    if gh >= 4:
                        if gh - 4 < H:
                            v.wait_ge(sm["knst"], 64 * min(gh // 4, 8))
                        else:
                            v.wait_ge(sm["qnst"], 64 * ((gh - H) // 4))
                    v.tensor_copy(nst[gh % 4][:, :], kn_ps[:, :]).then_inc(sm["n_ev"], 1)

                for pi, (pname, _w, _l) in enumerate(PASSES):
                    noff = 0 if pname == "k" else H
                    for eg in range(NEG):
                        k = pi * NEG + eg
                        for gi in range(8):
                            gidx = pi * NEG * 8 + eg * 8 + gi
                            v.wait_ge(sm["pj_mm"], gidx + 1)
                            if gi == 0 and k >= 2:
                                slab_guard(v, k)
                            if pname in ("k", "q"):
                                ec, th = divmod(gi, 2)
                                dstc = ec * 1024 + th * 512
                            else:
                                dstc = gi * 512
                            v.tensor_copy(slab[eg % 2][:, dstc:dstc + 512],
                                          pj_ps[gidx % 2][:, :]).then_inc(sm["pj_ev"], 1)
                            if pname in ("k", "q") and gi % 2 == 1:
                                ec = gi // 2
                                h = eg * 4 + ec
                                gh = noff + h
                                v.wait_ge(sm["pj_ev"], gidx + 1)
                                if gh >= 2:
                                    v.wait_ge(sm["n_mm"], gh - 1)
                                v.tensor_tensor(sqt[h % 2][:, 0:512],
                                                slab[eg % 2][:, ec * 1024:ec * 1024 + 512],
                                                slab[eg % 2][:, ec * 1024:ec * 1024 + 512],
                                                ALU.mult).then_inc(sm["sq_tt"], 1)
                                v.tensor_tensor(sqt[h % 2][:, 512:1024],
                                                slab[eg % 2][:, ec * 1024 + 512:ec * 1024 + 1024],
                                                slab[eg % 2][:, ec * 1024 + 512:ec * 1024 + 1024],
                                                ALU.mult).then_inc(sm["sq_tt"], 1)
                                if ec >= 1:
                                    nevac(eg * 4 + ec - 1, noff + eg * 4 + ec - 1)
                        if pname in ("k", "q"):
                            nevac(eg * 4 + 3, noff + eg * 4 + 3)
                v.nop().then_inc(sm["bar"], 1)


        CC1, CC2, CC3 = (16, 32, 48) if NO_CC else (1, 2, 3)
        if PHASES >= 2:
            with ExitStack() as p23:
                attnT = p23.enter_context(
                    nc.sbuf_tensor("attnT", [128, H * 1024], BF16))
                with ExitStack() as ctx:
                    def sb(name, shape, dt):
                        return ctx.enter_context(nc.sbuf_tensor(name, shape, dt))

                    def ps(name, shape, dt):
                        return ctx.enter_context(nc.psum_tensor(name, shape, dt))

                    khb = [sb(f"khb{i}", [128, 4 * 1024], BF16) for i in range(2)]
                    vhb = [sb(f"vhb{i}", [128, H * 128], BF16) for i in range(2)]
                    qhb = [sb(f"qhb{i}", [128, 1024], BF16) for i in range(2)]
                    qnb = [sb(f"qnb{i}", [128, 1024], BF16) for i in range(2)]
                    negm = [sb(f"negm{i}", [128, 1024], BF16) for i in range(2)]
                    ptb = [[sb(f"ptb{i}_{j}", [128, 1024], BF16) for j in range(2)] for i in range(2)]
                    denv = [sb(f"denv{i}", [128, 1024], F32) for i in range(2)]
                    denvb = [sb(f"denvb{i}", [128, 1024], BF16) for i in range(2)]
                    kn4 = sb("kn4", [H, 1024], BF16)
                    mk2 = sb("mk2", [H, 1], F32)
                    mk2b = sb("mk2b", [H, 1], F32)
                    mk2n = sb("mk2n", [128, H], F32)
                    r_sb = sb("r_sb", [1, 512], F32)
                    rbc_sb = sb("rbc_sb", [128, 512], F32)
                    s_ps = [ps(f"s_ps{i}", [128, 1024], F32) for i in range(2)]
                    acc_ps = [ps(f"acc_ps{i}", [128, 512], F32) for i in range(3)]
                    dr_ps = ps("dr_ps", [128, 512], F32)
                    den_ps = dr_ps[0:1, :]
                    rbc_ps = dr_ps

                    blk = ctx.enter_context(nc.Block("p2"))

                    @blk.sync
                    def _(s):
                        s.wait_ge(sm["bar"], 5)
                        s.wait_ge(sm["cc_s"], CC2)
                        kngv = kng.ap().rearrange("(r h) t -> h r t", r=GRP)
                        s.dma_start(out=kn4[:, :], in_=kngv[:, 0, :]).then_inc(sm["mk_ld"], 16)
                        s.wait_ge(sm["mk_dve"], 1)
                        s.dma_start(out=kn4[:, :], in_=kngv[:, 1, :]).then_inc(sm["mk_ld"], 16)
                        s.wait_ge(sm["mk_dve"], 3)
                        s.dma_start(out=kn4[:, :], in_=kngv[:, 2, :]).then_inc(sm["mk_ld"], 16)
                        s.wait_ge(sm["mk_dve"], 5)
                        s.dma_start(out=kn4[:, :], in_=kngv[:, 3, :]).then_inc(sm["mk_ld"], 16)
                        s.wait_ge(sm["mk_dve"], 7)
                        s.dma_start(out=mkd.ap().rearrange("a b -> b a"),
                                    in_=mk2[:, :]).then_inc(sm["mk_ld"], 16)
                        s.wait_ge(sm["mk_ld"], 80)
                        s.dma_start(out=mk2n.ap().rearrange("p h -> p () h"),
                                    in_=mkd.ap().partition_broadcast(128)
                                    ).then_inc(sm["mk_ld"], 16)
                        s.wait_ge(sm["qst"], 64 * 8)
                        s.wait_ge(sm["qnst"], 16 * H)
                        s.wait_ge(sm["cc_s"], CC3)
                        for h in range(H):
                            if h >= 2:
                                s.wait_ge(sm["sp"], 32 * (h - 1))
                                s.wait_ge(sm["av"], 32 * (h - 1))
                                s.wait_ge(sm["ng"], h - 1)
                            pf = sm["pfs0" if h % 2 == 0 else "pfs1"]
                            s.dma_start(
                                out=khb[h % 2].ap().rearrange("p (r t) -> p r t", t=T),
                                in_=kg.ap().rearrange("(r hh p) t -> p r hh t", r=GRP, hh=H)[:, :, h, :],
                            ).then_inc(pf, 16)
                            s.dma_start(out=qhb[h % 2][:, :],
                                        in_=qT_loc[h * 128:(h + 1) * 128, :]).then_inc(pf, 16)
                            s.dma_start(out=qnb[h % 2].ap().rearrange("p t -> p () t"),
                                        in_=qnloc[h:h + 1, :].partition_broadcast(128)
                                        ).then_inc(pf, 16)
                            s.dma_start(
                                out=vhb[h % 2].ap().rearrange("p (c e) -> p c e", e=128),
                                in_=vg.ap().rearrange("(c p) e -> p c e", p=128)[:, :, h * 128:(h + 1) * 128],
                            ).then_inc(pf, 16)
                        s.wait_ge(sm["pfs0"], 64 * 16)
                        s.wait_ge(sm["pfs1"], 64 * 16)
                        s.nop().then_inc(sm["bar"], 1)

                    def eidx(h, qg, p):
                        return h * 32 + 2 * p + qg

                    @blk.scalar
                    def _(sc):
                        sc.wait_ge(sm["bar"], 5)
                        for h in range(H):
                            for p in range(16):
                                for qg in range(2):
                                    g = eidx(h, qg, p)
                                    sc.wait_ge(sm["sp"], g + 1)
                                    if g >= 4:
                                        sc.wait_ge(sm["av"], g - 3)
                                    if h * 16 + p >= 2:
                                        sc.wait_ge(sm["dn" if qg == 0 else "pfs"],
                                                   h * 16 + p - 1)
                                    sc.activation(ptb[qg][p % 2][:, :], s_ps[qg][:, :],
                                                  AF.Exp, scale=SCALE).then_inc(sm["ex"], 1)
                        sc.nop().then_inc(sm["bar"], 1)

                    @blk.tensor
                    def _(t):
                        t.wait_ge(sm["bar"], 5)

                        def accb(h, qg):
                            return acc_ps[0] if qg == 0 else acc_ps[1 + h % 2]

                        def av_pair(h, qg, p):
                            g = eidx(h, qg, p)
                            t.wait_ge(sm["ex"], g + 1)
                            if p == 0 and h >= 1:
                                t.wait_ge(sm["ev2"], 2 * h - 1 if qg == 0 else max(2 * h - 2, 0))
                            lm = None
                            for c in (2 * p, 2 * p + 1):
                                lm = t.matmul(accb(h, qg)[:, :],
                                              vhb[h % 2][:, c * 128:(c + 1) * 128],
                                              ptb[qg][p % 2][:, (c % 2) * 512:(c % 2) * 512 + 512],
                                              start=(c == 0), stop=(c == 31))
                            lm.then_inc(sm["av"], 1)

                        def den_rbc(h, qg, part):
                            hq = 2 * h + qg
                            if part == 0:
                                t.wait_ge(sm["dn" if qg == 0 else "pfs"], 16 * (h + 1))
                                t.wait_ge(sm["mk_pe" if qg == 0 else "pfv"], h + 1)
                                if hq >= 1:
                                    t.wait_ge(sm["rb"], hq)
                                t.matmul(den_ps[:, :], onec_bf[:, :], denvb[qg][:, 0:512],
                                         start=True, stop=False)
                                t.matmul(den_ps[:, :], onec_bf[:, :], denvb[qg][:, 512:1024],
                                         start=False, stop=True).then_inc(sm["pd"], 1)
                            else:
                                t.wait_ge(sm["rc"], hq + 1)
                                if hq >= 1:
                                    t.wait_ge(sm["rb"], hq)
                                t.matmul(rbc_ps[:, :], oner_f[:, :], r_sb[:, :],
                                         start=True, stop=True).then_inc(sm["pr"], 1)

                        pend = []
                        for h in range(H):
                            t.wait_ge(sm["ng"], h + 1)
                            t.wait_ge(sm["pfs0" if h % 2 == 0 else "pfs1"],
                                      64 * (h // 2 + 1))
                            for p in range(16):
                                for qg in range(2):
                                    g = eidx(h, qg, p)
                                    if p >= 1:
                                        t.wait_ge(sm["ex"], eidx(h, qg, p - 1) + 1)
                                    elif h >= 1:
                                        t.wait_ge(sm["ex"], eidx(h - 1, qg, 15) + 1)
                                    lm = None
                                    for c in (2 * p, 2 * p + 1):
                                        t.matmul(s_ps[qg][:, (c % 2) * 512:(c % 2) * 512 + 512],
                                                 khb[h % 2][:, c * 128:(c + 1) * 128],
                                                 qhb[h % 2][:, qg * 512:(qg + 1) * 512],
                                                 start=True, stop=False)
                                        lm = t.matmul(s_ps[qg][:, (c % 2) * 512:(c % 2) * 512 + 512],
                                                      sc128b[:, :],
                                                      negm[h % 2][:, qg * 512:(qg + 1) * 512],
                                                      start=False, stop=True)
                                    lm.then_inc(sm["sp"], 1)
                                for _ in range(2):
                                    if pend:
                                        den_rbc(*pend.pop(0))
                                for qg in range(2):
                                    if p >= 1:
                                        av_pair(h, qg, p - 1)
                            for qg in range(2):
                                av_pair(h, qg, 15)
                            pend += [(h, 0, 0), (h, 0, 1), (h, 1, 0), (h, 1, 1)]
                        for job in pend:
                            den_rbc(*job)
                        t.nop().then_inc(sm["bar"], 1)

                    @blk.vector
                    def _(v):
                        v.wait_ge(sm["bar"], 5)
                        v.wait_ge(sm["mk_ld"], 16)
                        v.reduce_max(mk2[:, :], kn4[:, :], axis=AX.X).then_inc(sm["mk_dve"], 1)
                        for rr in range(1, 4):
                            v.wait_ge(sm["mk_ld"], 16 * (rr + 1))
                            v.reduce_max(mk2b[:, :], kn4[:, :], axis=AX.X).then_inc(sm["mk_dve"], 1)
                            v.wait_ge(sm["mk_dve"], 2 * rr)
                            v.tensor_tensor(mk2[:, :], mk2[:, :], mk2b[:, :],
                                            ALU.max).then_inc(sm["mk_dve"], 1)
                        v.wait_ge(sm["mk_ld"], 96)
                        v.tensor_scalar(mk2n[:, :], mk2n[:, :], -0.5, None,
                                        ALU.mult).then_inc(sm["mk_dve"], 1)

                        def fin(h, qg):
                            hq = 2 * h + qg
                            v.wait_ge(sm["pd"], hq + 1)
                            if hq >= 1:
                                v.wait_ge(sm["pr"], hq)
                            v.reciprocal(r_sb[:, :], den_ps[:, :]).then_inc(sm["rc"], 1)
                            v.wait_ge(sm["pr"], hq + 1)
                            v.tensor_copy(rbc_sb[:, :], rbc_ps[:, :]).then_inc(sm["rb"], 1)
                            v.wait_ge(sm["rb"], hq + 1)
                            c0 = h * 1024 + qg * 512
                            accq = acc_ps[0] if qg == 0 else acc_ps[1 + h % 2]
                            v.tensor_tensor(attnT[:, c0:c0 + 512],
                                            accq[:, :], rbc_sb[:, :],
                                            ALU.mult).then_inc(sm["ev2"], 1)

                        for h in range(H):
                            v.wait_ge(sm["pfs0" if h % 2 == 0 else "pfs1"],
                                      64 * (h // 2 + 1))
                            if h >= 2:
                                v.wait_ge(sm["sp"], 32 * (h - 1))
                            if h == 0:
                                v.wait_ge(sm["mk_dve"], 8)
                            v.tensor_scalar(negm[h % 2][:, :], qnb[h % 2][:, :],
                                            -0.5, mk2n[:, h:h + 1],
                                            ALU.mult, ALU.add).then_inc(sm["ng"], 1)
                            if h >= 1:
                                fin(h - 1, 0)
                                fin(h - 1, 1)
                            for p in range(16):
                                g = eidx(h, 0, p)
                                v.wait_ge(sm["ex"], g + 1)
                                if p == 0:
                                    if h >= 1:
                                        v.wait_ge(sm["pd"], 2 * h - 1)
                                    v.tensor_copy(denv[0][:, :], ptb[0][p % 2][:, :]
                                                  ).then_inc(sm["dn"], 1)
                                else:
                                    v.wait_ge(sm["dn"], h * 16 + p)
                                    lastd = v.tensor_tensor(denv[0][:, :], denv[0][:, :],
                                                            ptb[0][p % 2][:, :], ALU.add)
                                    lastd.then_inc(sm["dn"], 1)
                                    if p == 15:
                                        v.wait_ge(sm["dn"], h * 16 + 16)
                                        v.tensor_copy(denvb[0][:, :], denv[0][:, :]
                                                      ).then_inc(sm["mk_pe"], 1)
                        fin(H - 1, 0)
                        fin(H - 1, 1)
                        v.nop().then_inc(sm["bar"], 1)

                    @blk.gpsimd
                    def _(g):
                        g.wait_ge(sm["bar"], 5)
                        if PHASES >= 3:
                            for og in range(2):
                                k = 24 + og
                                g.wait_ge(sm["pj_mm"], pjg_cnt(2, 6 + og, 7))
                                srcp = woT.ap().rearrange("(d p) e -> p d e", p=128)[
                                    :, :, og * EGW:(og + 1) * EGW]
                                dstp = panel[k % 2].ap().rearrange("p (d e) -> p d e", e=EGW)
                                g.dma_start(out=dstp, in_=srcp).then_inc(
                                    sm["pan0" if k % 2 == 0 else "pan1"], 16)
                        for h in range(H):
                            for p in range(16):
                                gg = eidx(h, 1, p)
                                g.wait_ge(sm["ex"], gg + 1)
                                if p == 0:
                                    if h >= 1:
                                        g.wait_ge(sm["pd"], 2 * h)
                                    g.tensor_copy(denv[1][:, :], ptb[1][p % 2][:, :]
                                                  ).then_inc(sm["pfs"], 1)
                                else:
                                    g.wait_ge(sm["pfs"], h * 16 + p)
                                    lastd = g.tensor_tensor(denv[1][:, :], denv[1][:, :],
                                                            ptb[1][p % 2][:, :], ALU.add)
                                    lastd.then_inc(sm["pfs"], 1)
                                    if p == 15:
                                        g.wait_ge(sm["pfs"], h * 16 + 16)
                                        g.tensor_copy(denvb[1][:, :], denv[1][:, :]
                                                      ).then_inc(sm["pfv"], 1)
                        if DEBUG:
                            g.wait_ge(sm["ev2"], 2 * H)
                            g.dma_start(out=dbg["d_at"][:, :], in_=attnT[:, :]
                                        ).then_inc(sm["dbg_st"], 16)
                            g.wait_ge(sm["dbg_st"], 16 * 6)
                        g.nop().then_inc(sm["bar"], 1)


                if PHASES >= 3:
                    with ExitStack() as ctx:
                        def sb(name, shape, dt):
                            return ctx.enter_context(nc.sbuf_tensor(name, shape, dt))

                        def ps(name, shape, dt):
                            return ctx.enter_context(nc.psum_tensor(name, shape, dt))

                        xog = [sb(f"xog{i}", [128, 8 * 512], F32) for i in range(2)]
                        piece = [sb(f"piece{i}", [128, 512], F32) for i in range(2)]
                        o_ps = [ps(f"o_ps{i}", [128, 512], F32) for i in range(2)]

                        blk = ctx.enter_context(nc.Block("p3"))

                        @blk.sync
                        def _(s):
                            s.wait_ge(sm["bar"], 10)
                            for og in range(NEG):
                                if og >= 2:
                                    s.wait_ge(sm["o_ev"], 8 * (og - 1))
                                s.dma_start(
                                    out=xog[og % 2].ap().rearrange("p (ts e) -> p ts e", e=EGW),
                                    in_=x_in.ap().rearrange("(ts p) e -> p ts e", p=128)[
                                        :, :, og * EGW:(og + 1) * EGW],
                                ).then_inc(sm["x3a" if og % 2 == 0 else "x3b"], 16)
                                for ts_ in range(8):
                                    idx = og * 8 + ts_
                                    s.wait_ge(sm["o_ev"], idx + 1)
                                    dst = x2loc.ap().rearrange("(ts p) e -> p ts e", p=128)[
                                        :, ts_:ts_ + 1, og * EGW:(og + 1) * EGW]
                                    s.dma_start(out=dst,
                                                in_=piece[idx % 2].ap().rearrange(
                                                    "p (one e) -> p one e", one=1)
                                                ).then_inc(sm["x2a" if idx % 2 == 0 else "x2b"], 16)
                            s.wait_ge(sm["x2a"], 16 * 32)
                            s.wait_ge(sm["x2b"], 16 * 32)
                            s.wait_ge(sm["x3a"], 16 * 4)
                            s.wait_ge(sm["x3b"], 16 * 4)
                            s.nop().then_inc(sm["bar"], 1)

                        @blk.tensor
                        def _(t):
                            t.wait_ge(sm["bar"], 10)
                            for og in range(NEG):
                                pan_w(t, 24 + og)
                                for ts_ in range(8):
                                    gi = og * 8 + ts_
                                    if gi >= 2:
                                        t.wait_ge(sm["o_ev"], gi - 1)
                                    lm = None
                                    for e in range(ND):
                                        lm = t.matmul(
                                            o_ps[gi % 2][:, :],
                                            attnT[:, e * 1024 + ts_ * 128:e * 1024 + (ts_ + 1) * 128],
                                            panel[(24 + og) % 2][:, e * EGW:(e + 1) * EGW],
                                            start=(e == 0), stop=(e == ND - 1))
                                    lm.then_inc(sm["o_mm"], 1)
                            t.nop().then_inc(sm["bar"], 1)

                        @blk.vector
                        def _(v):
                            v.wait_ge(sm["bar"], 10)
                            for og in range(NEG):
                                v.wait_ge(sm["x3a" if og % 2 == 0 else "x3b"],
                                          16 * (og // 2 + 1))
                                for ts_ in range(8):
                                    idx = og * 8 + ts_
                                    v.wait_ge(sm["o_mm"], idx + 1)
                                    if idx >= 2:
                                        v.wait_ge(sm["x2a" if idx % 2 == 0 else "x2b"],
                                                  16 * (idx // 2))
                                    v.tensor_tensor(piece[idx % 2][:, :],
                                                    o_ps[idx % 2][:, :],
                                                    xog[og % 2][:, ts_ * 512:(ts_ + 1) * 512],
                                                    ALU.add).then_inc(sm["o_ev"], 1)
                            v.nop().then_inc(sm["bar"], 1)

                        @blk.scalar
                        def _(sc):
                            sc.wait_ge(sm["bar"], 10)
                            sc.nop().then_inc(sm["bar"], 1)

                        @blk.gpsimd
                        def _(g):
                            g.wait_ge(sm["bar"], 10)
                            for og in range(2, NEG):
                                k = 24 + og
                                g.wait_ge(sm["o_mm"], 8 * (og - 1))
                                srcp = woT.ap().rearrange("(d p) e -> p d e", p=128)[
                                    :, :, og * EGW:(og + 1) * EGW]
                                dstp = panel[k % 2].ap().rearrange("p (d e) -> p d e", e=EGW)
                                g.dma_start(out=dstp, in_=srcp).then_inc(
                                    sm["pan0" if k % 2 == 0 else "pan1"], 16)
                            if PHASES >= 4:
                                for og in range(2):
                                    k = 32 + og
                                    g.wait_ge(sm["o_mm"], 8 * (og + 7))
                                    srcp = wfT.ap().rearrange("(d p) e -> p d e", p=128)[
                                        :, :, og * EGW:(og + 1) * EGW]
                                    dstp = panel[k % 2].ap().rearrange("p (d e) -> p d e", e=EGW)
                                    g.dma_start(out=dstp, in_=srcp).then_inc(
                                        sm["pan0" if k % 2 == 0 else "pan1"], 16)
                            if DEBUG:
                                g.wait_ge(sm["x2a"], 16 * 32)
                                g.wait_ge(sm["x2b"], 16 * 32)
                                g.dma_start(out=dbg["d_x2"][:, :], in_=x2loc[:, :]
                                            ).then_inc(sm["dbg_st"], 16)
                            g.nop().then_inc(sm["bar"], 1)

        if PHASES >= 4:
            with ExitStack() as p4:
                lnT2 = p4.enter_context(nc.sbuf_tensor("lnT2", [128, ND * 1024], BF16))
                normb = p4.enter_context(nc.sbuf_tensor("normb", [128, ND * EGW], BF16))

                with ExitStack() as ctx:
                    def sb(name, shape, dt):
                        return ctx.enter_context(nc.sbuf_tensor(name, shape, dt))

                    def ps(name, shape, dt):
                        return ctx.enter_context(nc.psum_tensor(name, shape, dt))

                    xt2 = sb("xt2", [128, D], F32)
                    lnbf2 = [sb(f"lnbf2_{i}", [128, D], BF16) for i in range(2)]
                    stats2 = [sb(f"stats2_{i}", [128, 48], F32) for i in range(2)]
                    mv2 = [sb(f"mv2_{i}", [128, 2], F32) for i in range(2)]
                    lnv2 = [sb(f"lnv2_{i}", [128, 1], F32) for i in range(2)]
                    rstd2 = [sb(f"rstd2_{i}", [128, 1], F32) for i in range(2)]
                    tp2_ps = [ps(f"tp2_ps{i}", [128, 128], BF16) for i in range(2)]

                    blk = ctx.enter_context(nc.Block("p4a"))

                    @blk.sync
                    def _(s):
                        s.wait_ge(sm["bar"], 15)
                        for i in range(NT):
                            if i >= 1:
                                s.wait_ge(sm["ln_apply"], NT + i)
                            s.dma_start(out=xt2[:, :],
                                        in_=x2loc[128 * i:128 * (i + 1), :]).then_inc(sm["x4"], 16)
                        s.wait_ge(sm["x4"], 16 * NT)
                        s.nop().then_inc(sm["bar"], 1)

                    @blk.vector
                    def _(v):
                        v.wait_ge(sm["bar"], 15)
                        for i in range(NT + 1):
                            if i < NT:
                                v.wait_ge(sm["x4"], 16 * (i + 1))
                                ins = None
                                for j in range(8):
                                    ins = v.bn_stats(stats2[i % 2][:, 6 * j:6 * j + 6],
                                                     xt2[:, 512 * j:512 * (j + 1)])
                                ins.then_inc(sm["ln_bn"], 1)
                                v.wait_ge(sm["ln_bn"], NT + i + 1)
                                v.bn_aggr(mv2[i % 2][:, :], stats2[i % 2][:, :]).then_inc(sm["ln_mv"], 1)
                                v.wait_ge(sm["ln_rstd"], 2 * NT + 2 * i + 2)
                                if i >= 2:
                                    v.wait_ge(sm["tp_pe"], NT * ND + 32 * (i - 1))
                                v.tensor_scalar(lnbf2[i % 2][:, :], xt2[:, :],
                                                mv2[i % 2][:, 0:1], rstd2[i % 2][:, :],
                                                ALU.subtract, ALU.mult).then_inc(sm["ln_apply"], 1)
                            if i >= 1:
                                ti = i - 1
                                for d in range(ND):
                                    j = ti * ND + d
                                    v.wait_ge(sm["tp_pe"], NT * ND + j + 1)
                                    v.tensor_copy(lnT2[:, d * 1024 + ti * 128:d * 1024 + (ti + 1) * 128],
                                                  tp2_ps[j % 2][:, :]).then_inc(sm["tp_ev"], 1)
                        v.nop().then_inc(sm["bar"], 1)

                    @blk.scalar
                    def _(sc):
                        sc.wait_ge(sm["bar"], 15)
                        for i in range(NT):
                            sc.wait_ge(sm["ln_mv"], NT + i + 1)
                            sc.activation(lnv2[i % 2][:, :], mv2[i % 2][:, 1:2], AF.Ln,
                                          scale=1.0, bias=csb[:, 0:1]).then_inc(sm["ln_rstd"], 1)
                            sc.wait_ge(sm["ln_rstd"], 2 * NT + 2 * i + 1)
                            sc.activation(rstd2[i % 2][:, :], lnv2[i % 2][:, :], AF.Exp,
                                          scale=-0.5, bias=csb[:, 1:2]).then_inc(sm["ln_rstd"], 1)
                        sc.nop().then_inc(sm["bar"], 1)

                    @blk.tensor
                    def _(t):
                        t.wait_ge(sm["bar"], 15)
                        for i in range(NT):
                            t.wait_ge(sm["ln_apply"], NT + i + 1)
                            for d in range(ND):
                                j = i * ND + d
                                if j >= 2:
                                    t.wait_ge(sm["tp_ev"], NT * ND + j - 1)
                                t.transpose(tp2_ps[j % 2][:, :],
                                            lnbf2[i % 2][:, 128 * d:128 * (d + 1)],
                                            eyeb[:, :]).then_inc(sm["tp_pe"], 1)
                        t.nop().then_inc(sm["bar"], 1)

                    @blk.gpsimd
                    def _(g):
                        g.wait_ge(sm["bar"], 15)
                        srcn = normrep.ap().rearrange("(d p) e -> p d e", p=128)[:, :, 0:EGW]
                        dstn = normb.ap().rearrange("p (d e) -> p d e", e=EGW)
                        g.dma_start(out=dstn, in_=srcn).then_inc(sm["nrm"], 16)
                        g.nop().then_inc(sm["bar"], 1)

                with ExitStack() as ctx:
                    def sb(name, shape, dt):
                        return ctx.enter_context(nc.sbuf_tensor(name, shape, dt))

                    def ps(name, shape, dt):
                        return ctx.enter_context(nc.psum_tensor(name, shape, dt))

                    x2og = [sb(f"x2og{i}", [128, 8 * 512], F32) for i in range(2)]
                    biasog = [sb(f"biasog{i}", [128, 512], F32) for i in range(2)]
                    piece = [sb(f"fpiece{i}", [128, 512], F32) for i in range(2)]
                    f_ps = [ps(f"f_ps{i}", [128, 512], F32) for i in range(2)]

                    blk = ctx.enter_context(nc.Block("p4b"))

                    @blk.sync
                    def _(s):
                        s.wait_ge(sm["bar"], 20)
                        for og in range(NEG):
                            if og >= 2:
                                s.wait_ge(sm["f_ev"], 16 * (og - 1))
                            s.dma_start(
                                out=x2og[og % 2].ap().rearrange("p (ts e) -> p ts e", e=EGW),
                                in_=x2loc.ap().rearrange("(ts p) e -> p ts e", p=128)[
                                    :, :, og * EGW:(og + 1) * EGW],
                            ).then_inc(sm["xg4a" if og % 2 == 0 else "xg4b"], 16)
                            s.dma_start(out=biasog[og % 2][:, :],
                                        in_=bias_rep[:, og * EGW:(og + 1) * EGW]
                                        ).then_inc(sm["bl4a" if og % 2 == 0 else "bl4b"], 16)
                            for ts_ in range(8):
                                idx = og * 8 + ts_
                                s.wait_ge(sm["f_ev"], 2 * (idx + 1))
                                dst = out_ext.ap().rearrange("(ts p) e -> p ts e", p=128)[
                                    :, ts_:ts_ + 1, og * EGW:(og + 1) * EGW]
                                s.dma_start(out=dst,
                                            in_=piece[idx % 2].ap().rearrange(
                                                "p (one e) -> p one e", one=1)
                                            ).then_inc(sm["o4a" if idx % 2 == 0 else "o4b"], 16)
                        s.nop().then_inc(sm["bar"], 1)

                    @blk.gpsimd
                    def _(g):
                        g.wait_ge(sm["bar"], 20)
                        for og in range(NEG):
                            if og >= 2:
                                k = 32 + og
                                g.wait_ge(sm["f_mm"], 8 * (og - 1))
                                srcp = wfT.ap().rearrange("(d p) e -> p d e", p=128)[
                                    :, :, og * EGW:(og + 1) * EGW]
                                dstp = panel[k % 2].ap().rearrange("p (d e) -> p d e", e=EGW)
                                g.dma_start(out=dstp, in_=srcp).then_inc(
                                    sm["pan0" if k % 2 == 0 else "pan1"], 16)
                            if og >= 1:
                                g.wait_ge(sm["dq"], og)
                                srcn = normrep.ap().rearrange("(d p) e -> p d e", p=128)[
                                    :, :, og * EGW:(og + 1) * EGW]
                                dstn = normb.ap().rearrange("p (d e) -> p d e", e=EGW)
                                g.dma_start(out=dstn, in_=srcn).then_inc(sm["nrm"], 16)
                        g.wait_ge(sm["o4a"], 16 * 32)
                        g.wait_ge(sm["o4b"], 16 * 32)
                        g.wait_ge(sm["bar"], 24)
                        g.nop().then_inc(sm["bar"], 1)

                    @blk.vector
                    def _(v):
                        v.wait_ge(sm["bar"], 20)
                        for og in range(NEG):
                            k = 32 + og
                            pan_w(v, k)
                            v.wait_ge(sm["nrm"], 16 * (og + 1))
                            if og >= 2:
                                v.wait_ge(sm["f_mm"], 8 * (og - 1))
                            v.tensor_tensor(panel[k % 2][:, :], panel[k % 2][:, :],
                                            normb[:, :], ALU.mult).then_inc(sm["dq"], 1)
                            v.wait_ge(sm["xg4a" if og % 2 == 0 else "xg4b"],
                                      16 * (og // 2 + 1))
                            v.wait_ge(sm["bl4a" if og % 2 == 0 else "bl4b"],
                                      16 * (og // 2 + 1))
                            for ts_ in range(8):
                                idx = og * 8 + ts_
                                v.wait_ge(sm["f_mm"], idx + 1)
                                if idx >= 2:
                                    v.wait_ge(sm["o4a" if idx % 2 == 0 else "o4b"],
                                              16 * (idx // 2))
                                v.tensor_tensor(piece[idx % 2][:, :], f_ps[idx % 2][:, :],
                                                biasog[og % 2][:, :], ALU.add
                                                ).then_inc(sm["f_ev"], 1)
                                v.wait_ge(sm["f_ev"], 2 * idx + 1)
                                v.tensor_tensor(piece[idx % 2][:, :], piece[idx % 2][:, :],
                                                x2og[og % 2][:, ts_ * 512:(ts_ + 1) * 512],
                                                ALU.add).then_inc(sm["f_ev"], 1)
                        v.nop().then_inc(sm["bar"], 1)

                    @blk.tensor
                    def _(t):
                        t.wait_ge(sm["bar"], 20)
                        for og in range(NEG):
                            t.wait_ge(sm["dq"], og + 1)
                            k = 32 + og
                            for ts_ in range(8):
                                gi = og * 8 + ts_
                                if gi >= 2:
                                    t.wait_ge(sm["f_ev"], 2 * (gi - 1))
                                lm = None
                                for d in range(ND):
                                    lm = t.matmul(
                                        f_ps[gi % 2][:, :],
                                        lnT2[:, d * 1024 + ts_ * 128:d * 1024 + (ts_ + 1) * 128],
                                        panel[k % 2][:, d * EGW:(d + 1) * EGW],
                                        start=(d == 0), stop=(d == ND - 1))
                                lm.then_inc(sm["f_mm"], 1)
                            t.nop().then_inc(sm["bar"], 1) if og == NEG - 1 else None

                    @blk.scalar
                    def _(sc):
                        sc.wait_ge(sm["bar"], 20)
                        sc.nop().then_inc(sm["bar"], 1)
        if PHASES < 4:
            ndbg = 5 + (1 if PHASES >= 2 else 0) + (1 if PHASES >= 3 else 0)
            with ExitStack() as ctx:
                blk = ctx.enter_context(nc.Block("fin"))

                @blk.gpsimd
                def _(g):
                    if DEBUG:
                        g.wait_ge(sm["dbg_st"], 16 * ndbg)
                    g.wait_ge(sm["bar"], 5 * PHASES)

    return nc, dbg


def _host_layout(L0_attn_x, L0_attn_w_q, L0_attn_w_k, L0_attn_w_v, L0_attn_w_o,
                 L0_ffn_weight_q, L0_ffn_weight_norm, L0_ffn_bias):
    x = np.ascontiguousarray(np.asarray(L0_attn_x).astype(np.float32, copy=False).reshape(NC * T, D))
    wqT_ = np.ascontiguousarray(np.asarray(L0_attn_w_q).astype(np.float32, copy=False).T)
    wkT_ = np.ascontiguousarray(np.asarray(L0_attn_w_k).astype(np.float32, copy=False).T)
    wvT_ = np.ascontiguousarray(np.asarray(L0_attn_w_v).astype(np.float32, copy=False).T)
    woT_ = np.ascontiguousarray(np.asarray(L0_attn_w_o).astype(np.float32, copy=False).T)
    wfT_ = np.ascontiguousarray(np.asarray(L0_ffn_weight_q).astype(np.int8, copy=False).reshape(D, D).T)
    nrm = np.asarray(L0_ffn_weight_norm).astype(np.float16, copy=False).reshape(D, D // 16)
    normrep_ = np.ascontiguousarray(np.repeat(nrm.T, 16, axis=0))
    bias_rep_ = np.ascontiguousarray(
        np.tile(np.asarray(L0_ffn_bias).astype(np.float32, copy=False)[None, :], (128, 1)))
    base = {
        "wqT": wqT_, "wkT": wkT_, "wvT": wvT_, "woT": woT_,
        "wfT": wfT_, "normrep": normrep_, "bias_rep": bias_rep_,
        "ones_1x128_bf": np.ones((1, 128), ml_dtypes.bfloat16),
        "ones_128x1_bf": np.ones((128, 1), ml_dtypes.bfloat16),
        "ones_128x1_f": np.ones((128, 1), np.float32),
        "ones_1x128_f": np.ones((1, 128), np.float32),
        "eye_bf": np.eye(128).astype(ml_dtypes.bfloat16),
        "sc128": np.full((128, 128), 1.0 / 128.0, ml_dtypes.bfloat16),
        "eye_f": np.eye(128, dtype=np.float32),
        "consts": np.tile(np.array([[1e-5, -np.log(15.0), 0.0, 0.0]], np.float32), (128, 1)),
    }
    in_maps = []
    for c in range(NC):
        m = dict(base)
        m["x_in"] = np.ascontiguousarray(x[c * T:(c + 1) * T])
        in_maps.append(m)
    return in_maps


def kernel(**inputs):
    in_maps = _host_layout(**inputs)
    nc, _ = build_nc()
    res = run_bass_kernel_spmd(nc, in_maps, core_ids=list(range(NC)))
    out = np.concatenate([np.asarray(r["out"]) for r in res.results], axis=0)
    return out.reshape(2, S, D).astype(np.float32)


# version 07 (final)
# speedup vs baseline: 1.0831x; 1.0011x over previous
import numpy as np
import ml_dtypes
from contextlib import ExitStack

import concourse.bass as bass
import concourse.mybir as mybir
from concourse.bass_utils import run_bass_kernel_spmd

F32 = mybir.dt.float32
BF16 = mybir.dt.bfloat16
F16 = mybir.dt.float16
I8 = mybir.dt.int8
AF = mybir.ActivationFunctionType
ALU = mybir.AluOpType
AX = mybir.AxisListType

D = 4096
T = 1024
H = 32
S = 4096
NT = T // 128
ND = D // 128
NEG = 8
EGW = 512
NC = 8
GRP = 4

SCALE = float(1.0 / np.sqrt(128.0))
LN15 = float(np.log(15.0))

DEBUG = False
PHASES = 4
NO_CC = False


def build_nc():
    nc = bass.Bass("TRN2", target_bir_lowering=False)

    x_in = nc.dram_tensor("x_in", [T, D], F32, kind="ExternalInput")
    wqT = nc.dram_tensor("wqT", [D, D], F32, kind="ExternalInput")
    wkT = nc.dram_tensor("wkT", [D, D], F32, kind="ExternalInput")
    wvT = nc.dram_tensor("wvT", [D, D], F32, kind="ExternalInput")
    woT = nc.dram_tensor("woT", [D, D], F32, kind="ExternalInput")
    wfT = nc.dram_tensor("wfT", [D, D], I8, kind="ExternalInput")
    normrep = nc.dram_tensor("normrep", [D, D], F16, kind="ExternalInput")
    bias_rep = nc.dram_tensor("bias_rep", [128, D], F32, kind="ExternalInput")
    ones_1x128_bf = nc.dram_tensor("ones_1x128_bf", [1, 128], BF16, kind="ExternalInput")
    ones_128x1_bf = nc.dram_tensor("ones_128x1_bf", [128, 1], BF16, kind="ExternalInput")
    ones_128x1_f = nc.dram_tensor("ones_128x1_f", [128, 1], F32, kind="ExternalInput")
    ones_1x128_f = nc.dram_tensor("ones_1x128_f", [1, 128], F32, kind="ExternalInput")
    eye_bf = nc.dram_tensor("eye_bf", [128, 128], BF16, kind="ExternalInput")
    sc128 = nc.dram_tensor("sc128", [128, 128], BF16, kind="ExternalInput")
    eye_f = nc.dram_tensor("eye_f", [128, 128], F32, kind="ExternalInput")
    consts = nc.dram_tensor("consts", [128, 4], F32, kind="ExternalInput")

    out_ext = nc.dram_tensor("out", [T, D], F32, kind="ExternalOutput")

    qT_loc = nc.dram_tensor("qT_loc", [D, T], BF16, kind="Internal")
    kT_loc = nc.dram_tensor("kT_loc", [D, T], BF16, kind="Internal")
    v_loc = nc.dram_tensor("v_loc", [T, D], BF16, kind="Internal")
    knloc = nc.dram_tensor("knloc", [H, T], BF16, kind="Internal")
    qnloc = nc.dram_tensor("qnloc", [H, T], BF16, kind="Internal")
    kg = nc.dram_tensor("kg", [GRP * D, T], BF16, kind="Internal")
    vg = nc.dram_tensor("vg", [S, D], BF16, kind="Internal")
    kng = nc.dram_tensor("kng", [GRP * H, T], BF16, kind="Internal")
    mkd = nc.dram_tensor("mkd", [1, H], F32, kind="Internal")
    x2loc = nc.dram_tensor("x2loc", [T, D], F32, kind="Internal")

    dbg = {}
    if DEBUG:
        dbg["d_qT"] = nc.dram_tensor("d_qT", [D, T], BF16, kind="ExternalOutput")
        dbg["d_kg"] = nc.dram_tensor("d_kg", [GRP * D, T], BF16, kind="ExternalOutput")
        dbg["d_vg"] = nc.dram_tensor("d_vg", [S, D], BF16, kind="ExternalOutput")
        dbg["d_kng"] = nc.dram_tensor("d_kng", [GRP * H, T], BF16, kind="ExternalOutput")
        dbg["d_qn"] = nc.dram_tensor("d_qn", [H, T], BF16, kind="ExternalOutput")
        if PHASES >= 2:
            dbg["d_at"] = nc.dram_tensor("d_at", [128, H * T], BF16, kind="ExternalOutput")
        if PHASES >= 3:
            dbg["d_x2"] = nc.dram_tensor("d_x2", [T, D], F32, kind="ExternalOutput")

    RG = [[0, 1, 2, 3], [4, 5, 6, 7]]

    with ExitStack() as g_ctx:
        def g_sb(name, shape, dt):
            return g_ctx.enter_context(nc.sbuf_tensor(name, shape, dt))

        def g_sem(name):
            return g_ctx.enter_context(nc.semaphore(name))

        panel = [g_sb(f"panel{i}", [128, ND * EGW], BF16) for i in range(2)]
        onec_bf = g_sb("onec_bf", [128, 1], BF16)
        oner_bf = g_sb("oner_bf", [1, 128], BF16)
        onec_f = g_sb("onec_f", [128, 1], F32)
        oner_f = g_sb("oner_f", [1, 128], F32)
        eyeb = g_sb("eyeb", [128, 128], BF16)
        sc128b = g_sb("sc128b", [128, 128], BF16)
        eyef = g_sb("eyef", [128, 128], F32)
        csb = g_sb("csb", [128, 4], F32)

        names = ("c_ld bar x_ld ln_bn ln_mv ln_rstd ln_apply tp_pe tp_ev "
                 "pan0 pan1 x0 x1 pfs0 pfs1 pcv sg0 sg1 sg2 sg3 sg4 sg5 sg6 sg7 pj_mm pj_ev sq_tt n_mm n_ev knst qnst kst vst qst kdone vdone kndone cc_s "
                 "pfs pfv mk_ld mk_dve mk_pe ng sp ex dn av pd rc pr rb ev2 "
                 "x3a x3b x2a x2b o_mm o_ev nrm dq f_mm f_ev x4 xg4a xg4b bl4a bl4b o4a o4b dbg_st").split()
        sm = {n: g_sem(n) for n in names}

        PASSES = [("k", wkT, kT_loc), ("v", wvT, v_loc), ("q", wqT, qT_loc)]

        SWP = [0, 1] + list(range(24, 40))

        def pan_w(eng, k):
            if 2 <= k < 24:
                eng.wait_ge(sm["pcv"], 32 * (k - 1))
            else:
                i = SWP.index(k)
                par = [j for j in SWP if j % 2 == k % 2].index(k)
                eng.wait_ge(sm["pan0" if k % 2 == 0 else "pan1"], 16 * (par + 1))

        def pjg_cnt(pi, eg, gi):
            return pi * NEG * 8 + eg * 8 + gi + 1

        def slab_guard(eng, k):
            j = k - 1
            if k - 2 < 8:
                eng.wait_ge(sm["kst"], 64 * min(j + 1, 8))
            elif k - 2 < 16:
                eng.wait_ge(sm["vst"], 16 * (min(j, 15) - 7))
            else:
                eng.wait_ge(sm["qst"], 64 * (j - 15))

        with ExitStack() as ctx:
            def sb(name, shape, dt):
                return ctx.enter_context(nc.sbuf_tensor(name, shape, dt))

            def ps(name, shape, dt):
                return ctx.enter_context(nc.psum_tensor(name, shape, dt))

            xt = [sb(f"xt{i}", [128, D], F32) for i in range(2)]
            lnbf = [sb(f"lnbf{i}", [128, D], BF16) for i in range(2)]
            lnT = sb("lnT", [128, ND * 1024], BF16)
            stats = [sb(f"stats{i}", [128, 48], F32) for i in range(2)]
            mv = [sb(f"mv{i}", [128, 2], F32) for i in range(2)]
            lnv = [sb(f"lnv{i}", [128, 1], F32) for i in range(2)]
            rstd = [sb(f"rstd{i}", [128, 1], F32) for i in range(2)]
            slab = [sb(f"slab{i}", [128, 4096], BF16) for i in range(2)]
            sqt = [sb(f"sqt{i}", [128, 1024], BF16) for i in range(2)]
            nst = [sb(f"nst{i}", [1, 1024], BF16) for i in range(4)]
            stg = [lnbf[i // 4].ap().bitcast(F32)[:, (i % 4) * 512:(i % 4) * 512 + 512]
                   for i in range(8)]
            pj_ps = [ps(f"pj_ps{i}", [128, 512], F32) for i in range(2)]
            kn_ps = ps("kn_ps", [1, 1024], F32)
            tp_ps = [ps(f"tp_ps{i}", [128, 128], BF16) for i in range(2)]

            blk = ctx.enter_context(nc.Block("p1"))

            @blk.sync
            def _(s):
                s.dma_start(out=onec_bf[:, :], in_=ones_128x1_bf[:, :]).then_inc(sm["c_ld"], 16)
                s.dma_start(out=oner_bf[:, :], in_=ones_1x128_bf[:, :]).then_inc(sm["c_ld"], 16)
                s.dma_start(out=onec_f[:, :], in_=ones_128x1_f[:, :]).then_inc(sm["c_ld"], 16)
                s.dma_start(out=oner_f[:, :], in_=ones_1x128_f[:, :]).then_inc(sm["c_ld"], 16)
                s.dma_start(out=eyeb[:, :], in_=eye_bf[:, :]).then_inc(sm["c_ld"], 16)
                s.dma_start(out=sc128b[:, :], in_=sc128[:, :]).then_inc(sm["c_ld"], 16)
                s.dma_start(out=eyef[:, :], in_=eye_f[:, :]).then_inc(sm["c_ld"], 16)
                s.dma_start(out=csb[:, :], in_=consts[:, :]).then_inc(sm["c_ld"], 16)
                for i in range(NT):
                    if i >= 2:
                        s.wait_ge(sm["ln_apply"], i - 1)
                    s.dma_start(out=xt[i % 2][:, :],
                                in_=x_in[128 * i:128 * (i + 1), :]).then_inc(
                                    sm["x0" if i % 2 == 0 else "x1"], 16)
                for pi, (pname, _w, locT) in enumerate(PASSES):
                    stsem = {"k": "kst", "v": "vst", "q": "qst"}[pname]
                    for eg in range(NEG):
                        k = pi * NEG + eg
                        s.wait_ge(sm["pj_ev"], pjg_cnt(pi, eg, 7))
                        if pname in ("k", "q"):
                            for ec in range(4):
                                r0 = eg * 512 + ec * 128
                                s.dma_start(out=locT[r0:r0 + 128, :],
                                            in_=slab[eg % 2][:, ec * 1024:(ec + 1) * 1024]
                                            ).then_inc(sm[stsem], 16)
                        else:
                            dst = v_loc.ap().rearrange("(ts p) e -> p ts e", p=128)[
                                :, :, eg * EGW:(eg + 1) * EGW]
                            src = slab[eg % 2].ap().rearrange("p (ts e) -> p ts e", e=EGW)
                            s.dma_start(out=dst, in_=src).then_inc(sm[stsem], 16)
                    if pname == "k":
                        s.wait_ge(sm["kst"], 64 * 8)
                        s.nop().then_inc(sm["kdone"], 1)
                    elif pname == "v":
                        s.wait_ge(sm["vst"], 16 * 8)
                        s.nop().then_inc(sm["vdone"], 1)
                s.wait_ge(sm["qst"], 64 * 8)
                s.nop().then_inc(sm["bar"], 1)

            @blk.gpsimd
            def _(g):
                for k in range(24):
                    pi, eg = divmod(k, NEG)
                    wT = PASSES[pi][1]
                    if k < 2:
                        src = wT.ap().rearrange("(d p) e -> p d e", p=128)[
                            :, :, eg * EGW:(eg + 1) * EGW]
                        dst = panel[k % 2].ap().rearrange("p (d e) -> p d e", e=EGW)
                        g.dma_start(out=dst, in_=src).then_inc(
                            sm["pan0" if k % 2 == 0 else "pan1"], 16)
                    else:
                        g.wait_ge(sm["pj_mm"], pjg_cnt(*divmod(k - 2, NEG), 7))
                        for d in range(ND):
                            j = (k - 2) * ND + d
                            g.wait_ge(sm[f"sg{j % 8}"], 16 * (j // 8 + 1))
                            g.tensor_copy(panel[k % 2][:, d * EGW:(d + 1) * EGW],
                                          stg[j % 8]).then_inc(sm["pcv"], 1)
                    if k == 13:
                        g.wait_ge(sm["kdone"], 1)
                        g.wait_ge(sm["kndone"], 1)
                        if NO_CC:
                            g.dma_start(out=kg[0:D, :], in_=kT_loc[:, :]).then_inc(sm["cc_s"], 16)
                            g.dma_start(out=kng[0:H, :], in_=knloc[:, :]).then_inc(sm["cc_s"], 16)
                            g.wait_ge(sm["cc_s"], 32)
                        else:
                            g.collective_compute(
                                "AllGather", ALU.bypass, replica_groups=RG,
                                ins=[kT_loc.ap().opt()], outs=[kg.ap().opt()],
                            ).then_inc(sm["cc_s"], 1)
                            g.collective_compute(
                                "AllGather", ALU.bypass, replica_groups=RG,
                                ins=[knloc.ap().opt()], outs=[kng.ap().opt()],
                            ).then_inc(sm["cc_s"], 1)
                    if k == 21:
                        g.wait_ge(sm["vdone"], 1)
                        if NO_CC:
                            g.dma_start(out=vg[0:T, :], in_=v_loc[:, :]).then_inc(sm["cc_s"], 16)
                            g.wait_ge(sm["cc_s"], 48)
                        else:
                            g.collective_compute(
                                "AllGather", ALU.bypass, replica_groups=RG,
                                ins=[v_loc.ap().opt()], outs=[vg.ap().opt()],
                            ).then_inc(sm["cc_s"], 1)
                if DEBUG:
                    g.wait_ge(sm["cc_s"], 48 if NO_CC else 3)
                    g.wait_ge(sm["qst"], 64 * 8)
                    g.wait_ge(sm["qnst"], 16 * H)
                    g.dma_start(out=dbg["d_qT"][:, :], in_=qT_loc[:, :]).then_inc(sm["dbg_st"], 16)
                    g.dma_start(out=dbg["d_kg"][:, :], in_=kg[:, :]).then_inc(sm["dbg_st"], 16)
                    g.dma_start(out=dbg["d_vg"][:, :], in_=vg[:, :]).then_inc(sm["dbg_st"], 16)
                    g.dma_start(out=dbg["d_kng"][:, :], in_=kng[:, :]).then_inc(sm["dbg_st"], 16)
                    g.dma_start(out=dbg["d_qn"][:, :], in_=qnloc[:, :]).then_inc(sm["dbg_st"], 16)
                g.nop().then_inc(sm["bar"], 1)

            @blk.tensor
            def _(t):
                t.wait_ge(sm["c_ld"], 16 * 8)
                for i in range(NT):
                    t.wait_ge(sm["ln_apply"], i + 1)
                    for d in range(ND):
                        j = i * ND + d
                        if j >= 2:
                            t.wait_ge(sm["tp_ev"], j - 1)
                        t.transpose(tp_ps[j % 2][:, :],
                                    lnbf[i % 2][:, 128 * d:128 * (d + 1)],
                                    eyeb[:, :]).then_inc(sm["tp_pe"], 1)
                t.wait_ge(sm["tp_ev"], NT * ND)
                for pi, (pname, _w, _l) in enumerate(PASSES):
                    pend = []
                    for eg in range(NEG):
                        k = pi * NEG + eg
                        pan_w(t, k)
                        for gi in range(8):
                            gidx = pi * NEG * 8 + eg * 8 + gi
                            if gidx >= 2:
                                t.wait_ge(sm["pj_ev"], gidx - 1)
                            if pname in ("k", "q"):
                                ec, th = divmod(gi, 2)
                                lastm = None
                                for d in range(ND):
                                    lastm = t.matmul(
                                        pj_ps[gidx % 2][:, :],
                                        panel[k % 2][:, d * EGW + ec * 128:d * EGW + (ec + 1) * 128],
                                        lnT[:, d * 1024 + th * 512:d * 1024 + (th + 1) * 512],
                                        start=(d == 0), stop=(d == ND - 1))
                            else:
                                ts_ = gi
                                for d in range(ND):
                                    lastm = t.matmul(
                                        pj_ps[gidx % 2][:, :],
                                        lnT[:, d * 1024 + ts_ * 128:d * 1024 + (ts_ + 1) * 128],
                                        panel[k % 2][:, d * EGW:(d + 1) * EGW],
                                        start=(d == 0), stop=(d == ND - 1))
                            lastm.then_inc(sm["pj_mm"], 1)
                            if pname in ("k", "q") and pend:
                                h = pend.pop(0)
                                gh = (0 if pname == "k" else H) + h
                                t.wait_ge(sm["sq_tt"], (gh % H) * 2 + 2 + (0 if pname == "k" else 2 * H))
                                if gh >= 1:
                                    t.wait_ge(sm["n_ev"], gh)
                                lm = None
                                for th2 in range(2):
                                    lm = t.matmul(kn_ps[:, th2 * 512:(th2 + 1) * 512],
                                                  onec_bf[:, :],
                                                  sqt[h % 2][:, th2 * 512:(th2 + 1) * 512],
                                                  start=True, stop=True)
                                lm.then_inc(sm["n_mm"], 1)
                            if pname in ("k", "q") and gi % 2 == 1:
                                pend.append(eg * 4 + gi // 2)
                    for h in pend:
                        gh = (0 if pname == "k" else H) + h
                        t.wait_ge(sm["sq_tt"], (gh % H) * 2 + 2 + (0 if pname == "k" else 2 * H))
                        if gh >= 1:
                            t.wait_ge(sm["n_ev"], gh)
                        lm = None
                        for th2 in range(2):
                            lm = t.matmul(kn_ps[:, th2 * 512:(th2 + 1) * 512],
                                          onec_bf[:, :],
                                          sqt[h % 2][:, th2 * 512:(th2 + 1) * 512],
                                          start=True, stop=True)
                        lm.then_inc(sm["n_mm"], 1)
                    pend.clear()
                t.nop().then_inc(sm["bar"], 1)

            @blk.scalar
            def _(sc):
                sc.wait_ge(sm["c_ld"], 16 * 8)
                for i in range(NT):
                    sc.wait_ge(sm["ln_mv"], i + 1)
                    sc.activation(lnv[i % 2][:, :], mv[i % 2][:, 1:2], AF.Ln,
                                  scale=1.0, bias=csb[:, 0:1]).then_inc(sm["ln_rstd"], 1)
                    sc.wait_ge(sm["ln_rstd"], 2 * i + 1)
                    sc.activation(rstd[i % 2][:, :], lnv[i % 2][:, :], AF.Exp,
                                  scale=-0.5).then_inc(sm["ln_rstd"], 1)
                def norm_store(gh):
                    pnameh = "k" if gh < H else "q"
                    h = gh % H
                    nloc = knloc if pnameh == "k" else qnloc
                    nsem = "knst" if pnameh == "k" else "qnst"
                    sc.wait_ge(sm["n_ev"], gh + 1)
                    sc.dma_start(out=nloc[h:h + 1, :],
                                 in_=nst[gh % 4][:, :]).then_inc(sm[nsem], 16)
                    if gh == H - 1:
                        sc.wait_ge(sm["knst"], 16 * H)
                        sc.nop().then_inc(sm["kndone"], 1)

                sc.wait_ge(sm["tp_ev"], NT * ND)
                for k in range(2, 24):
                    pi, eg = divmod(k, NEG)
                    wT = PASSES[pi][1]
                    kk = k - 3
                    if 0 <= kk < 8:
                        for ec in range(4):
                            norm_store(kk * 4 + ec)
                    elif kk >= 16:
                        for ec in range(4):
                            norm_store(H + (kk - 16) * 4 + ec)
                    for d in range(ND):
                        j = (k - 2) * ND + d
                        if j >= 8:
                            sc.wait_ge(sm["pcv"], j - 7)
                        srcp = wT.ap().rearrange("(d p) e -> p d e", p=128)[
                            :, d:d + 1, eg * EGW:(eg + 1) * EGW]
                        sc.dma_start(out=stg[j % 8].rearrange("p e -> p () e"),
                                     in_=srcp).then_inc(sm[f"sg{j % 8}"], 16)
                for gh in range(H + 5 * 4, 2 * H):
                    norm_store(gh)
                sc.wait_ge(sm["qnst"], 16 * H)
                sc.nop().then_inc(sm["bar"], 1)

            @blk.vector
            def _(v):
                for i in range(NT + 1):
                    if i < NT:
                        v.wait_ge(sm["x0" if i % 2 == 0 else "x1"], 16 * (i // 2 + 1))
                        ins = None
                        for j in range(8):
                            ins = v.bn_stats(stats[i % 2][:, 6 * j:6 * j + 6],
                                             xt[i % 2][:, 512 * j:512 * (j + 1)])
                        ins.then_inc(sm["ln_bn"], 1)
                        v.wait_ge(sm["ln_bn"], i + 1)
                        v.bn_aggr(mv[i % 2][:, :], stats[i % 2][:, :]).then_inc(sm["ln_mv"], 1)
                        v.wait_ge(sm["ln_rstd"], 2 * i + 2)
                        if i >= 2:
                            v.wait_ge(sm["tp_pe"], 32 * (i - 1))
                        v.tensor_scalar(lnbf[i % 2][:, :], xt[i % 2][:, :],
                                        mv[i % 2][:, 0:1], rstd[i % 2][:, :],
                                        ALU.subtract, ALU.mult).then_inc(sm["ln_apply"], 1)
                    if i >= 1:
                        ti = i - 1
                        for d in range(ND):
                            j = ti * ND + d
                            v.wait_ge(sm["tp_pe"], j + 1)
                            v.tensor_copy(lnT[:, d * 1024 + ti * 128:d * 1024 + (ti + 1) * 128],
                                          tp_ps[j % 2][:, :]).then_inc(sm["tp_ev"], 1)
                def nevac(h, gh):
                    v.wait_ge(sm["n_mm"], gh + 1)
                    if gh >= 4:
                        if gh - 4 < H:
                            v.wait_ge(sm["knst"], 64 * min(gh // 4, 8))
                        else:
                            v.wait_ge(sm["qnst"], 64 * ((gh - H) // 4))
                    v.tensor_copy(nst[gh % 4][:, :], kn_ps[:, :]).then_inc(sm["n_ev"], 1)

                for pi, (pname, _w, _l) in enumerate(PASSES):
                    noff = 0 if pname == "k" else H
                    for eg in range(NEG):
                        k = pi * NEG + eg
                        for gi in range(8):
                            gidx = pi * NEG * 8 + eg * 8 + gi
                            v.wait_ge(sm["pj_mm"], gidx + 1)
                            if gi == 0 and k >= 2:
                                slab_guard(v, k)
                            if pname in ("k", "q"):
                                ec, th = divmod(gi, 2)
                                dstc = ec * 1024 + th * 512
                            else:
                                dstc = gi * 512
                            v.tensor_copy(slab[eg % 2][:, dstc:dstc + 512],
                                          pj_ps[gidx % 2][:, :]).then_inc(sm["pj_ev"], 1)
                            if pname in ("k", "q") and gi % 2 == 1:
                                ec = gi // 2
                                h = eg * 4 + ec
                                gh = noff + h
                                v.wait_ge(sm["pj_ev"], gidx + 1)
                                if gh >= 2:
                                    v.wait_ge(sm["n_mm"], gh - 1)
                                v.tensor_tensor(sqt[h % 2][:, 0:512],
                                                slab[eg % 2][:, ec * 1024:ec * 1024 + 512],
                                                slab[eg % 2][:, ec * 1024:ec * 1024 + 512],
                                                ALU.mult).then_inc(sm["sq_tt"], 1)
                                v.tensor_tensor(sqt[h % 2][:, 512:1024],
                                                slab[eg % 2][:, ec * 1024 + 512:ec * 1024 + 1024],
                                                slab[eg % 2][:, ec * 1024 + 512:ec * 1024 + 1024],
                                                ALU.mult).then_inc(sm["sq_tt"], 1)
                                if ec >= 1:
                                    nevac(eg * 4 + ec - 1, noff + eg * 4 + ec - 1)
                        if pname in ("k", "q"):
                            nevac(eg * 4 + 3, noff + eg * 4 + 3)
                v.nop().then_inc(sm["bar"], 1)


        CC1, CC2, CC3 = (16, 32, 48) if NO_CC else (1, 2, 3)
        if PHASES >= 2:
            with ExitStack() as p23:
                attnT = p23.enter_context(
                    nc.sbuf_tensor("attnT", [128, H * 1024], BF16))
                with ExitStack() as ctx:
                    def sb(name, shape, dt):
                        return ctx.enter_context(nc.sbuf_tensor(name, shape, dt))

                    def ps(name, shape, dt):
                        return ctx.enter_context(nc.psum_tensor(name, shape, dt))

                    khb = [sb(f"khb{i}", [128, 4 * 1024], BF16) for i in range(2)]
                    vhb = [sb(f"vhb{i}", [128, H * 128], BF16) for i in range(2)]
                    qhb = [sb(f"qhb{i}", [128, 1024], BF16) for i in range(2)]
                    qnb = [sb(f"qnb{i}", [128, 1024], BF16) for i in range(2)]
                    negm = [sb(f"negm{i}", [128, 1024], BF16) for i in range(2)]
                    ptb = [[sb(f"ptb{i}_{j}", [128, 1024], BF16) for j in range(2)] for i in range(2)]
                    denv = [sb(f"denv{i}", [128, 1024], F32) for i in range(2)]
                    denvb = [sb(f"denvb{i}", [128, 1024], BF16) for i in range(2)]
                    kn4 = sb("kn4", [H, 1024], BF16)
                    mk2 = sb("mk2", [H, 1], F32)
                    mk2b = sb("mk2b", [H, 1], F32)
                    mk2n = sb("mk2n", [128, H], F32)
                    r_sb = sb("r_sb", [1, 512], F32)
                    rbc_sb = sb("rbc_sb", [128, 512], F32)
                    s_ps = [ps(f"s_ps{i}", [128, 1024], F32) for i in range(2)]
                    acc_ps = [ps(f"acc_ps{i}", [128, 512], F32) for i in range(3)]
                    dr_ps = ps("dr_ps", [128, 512], F32)
                    den_ps = dr_ps[0:1, :]
                    rbc_ps = dr_ps

                    blk = ctx.enter_context(nc.Block("p2"))

                    @blk.sync
                    def _(s):
                        s.wait_ge(sm["bar"], 5)
                        s.wait_ge(sm["cc_s"], CC2)
                        kngv = kng.ap().rearrange("(r h) t -> h r t", r=GRP)
                        s.dma_start(out=kn4[:, :], in_=kngv[:, 0, :]).then_inc(sm["mk_ld"], 16)
                        s.wait_ge(sm["mk_dve"], 1)
                        s.dma_start(out=kn4[:, :], in_=kngv[:, 1, :]).then_inc(sm["mk_ld"], 16)
                        s.wait_ge(sm["mk_dve"], 3)
                        s.dma_start(out=kn4[:, :], in_=kngv[:, 2, :]).then_inc(sm["mk_ld"], 16)
                        s.wait_ge(sm["mk_dve"], 5)
                        s.dma_start(out=kn4[:, :], in_=kngv[:, 3, :]).then_inc(sm["mk_ld"], 16)
                        s.wait_ge(sm["mk_dve"], 7)
                        s.dma_start(out=mkd.ap().rearrange("a b -> b a"),
                                    in_=mk2[:, :]).then_inc(sm["mk_ld"], 16)
                        s.wait_ge(sm["mk_ld"], 80)
                        s.dma_start(out=mk2n.ap().rearrange("p h -> p () h"),
                                    in_=mkd.ap().partition_broadcast(128)
                                    ).then_inc(sm["mk_ld"], 16)
                        s.wait_ge(sm["qst"], 64 * 8)
                        s.wait_ge(sm["qnst"], 16 * H)
                        s.wait_ge(sm["cc_s"], CC3)
                        for h in range(H):
                            if h >= 2:
                                s.wait_ge(sm["sp"], 32 * (h - 1))
                                s.wait_ge(sm["av"], 32 * (h - 1))
                                s.wait_ge(sm["ng"], h - 1)
                            pf = sm["pfs0" if h % 2 == 0 else "pfs1"]
                            s.dma_start(
                                out=khb[h % 2].ap().rearrange("p (r t) -> p r t", t=T),
                                in_=kg.ap().rearrange("(r hh p) t -> p r hh t", r=GRP, hh=H)[:, :, h, :],
                            ).then_inc(pf, 16)
                            s.dma_start(out=qhb[h % 2][:, :],
                                        in_=qT_loc[h * 128:(h + 1) * 128, :]).then_inc(pf, 16)
                            s.dma_start(out=qnb[h % 2].ap().rearrange("p t -> p () t"),
                                        in_=qnloc[h:h + 1, :].partition_broadcast(128)
                                        ).then_inc(pf, 16)
                            s.dma_start(
                                out=vhb[h % 2].ap().rearrange("p (c e) -> p c e", e=128),
                                in_=vg.ap().rearrange("(c p) e -> p c e", p=128)[:, :, h * 128:(h + 1) * 128],
                            ).then_inc(pf, 16)
                        s.wait_ge(sm["pfs0"], 64 * 16)
                        s.wait_ge(sm["pfs1"], 64 * 16)
                        s.nop().then_inc(sm["bar"], 1)

                    def eidx(h, qg, p):
                        return h * 32 + 2 * p + qg

                    @blk.scalar
                    def _(sc):
                        sc.wait_ge(sm["bar"], 5)
                        for h in range(H):
                            for p in range(16):
                                for qg in range(2):
                                    g = eidx(h, qg, p)
                                    sc.wait_ge(sm["sp"], g + 1)
                                    if g >= 4:
                                        sc.wait_ge(sm["av"], g - 3)
                                    if h * 16 + p >= 2:
                                        sc.wait_ge(sm["dn" if qg == 0 else "pfs"],
                                                   h * 16 + p - 1)
                                    sc.activation(ptb[qg][p % 2][:, :], s_ps[qg][:, :],
                                                  AF.Exp, scale=SCALE).then_inc(sm["ex"], 1)
                        sc.nop().then_inc(sm["bar"], 1)

                    @blk.tensor
                    def _(t):
                        t.wait_ge(sm["bar"], 5)

                        def accb(h, qg):
                            return acc_ps[0] if qg == 0 else acc_ps[1 + h % 2]

                        def av_pair(h, qg, p):
                            g = eidx(h, qg, p)
                            t.wait_ge(sm["ex"], g + 1)
                            if p == 0 and h >= 1:
                                t.wait_ge(sm["ev2"], 2 * h - 1 if qg == 0 else max(2 * h - 2, 0))
                            lm = None
                            for c in (2 * p, 2 * p + 1):
                                lm = t.matmul(accb(h, qg)[:, :],
                                              vhb[h % 2][:, c * 128:(c + 1) * 128],
                                              ptb[qg][p % 2][:, (c % 2) * 512:(c % 2) * 512 + 512],
                                              start=(c == 0), stop=(c == 31))
                            lm.then_inc(sm["av"], 1)

                        def den_rbc(h, qg, part):
                            hq = 2 * h + qg
                            if part == 0:
                                t.wait_ge(sm["dn" if qg == 0 else "pfs"], 16 * (h + 1))
                                t.wait_ge(sm["mk_pe" if qg == 0 else "pfv"], h + 1)
                                if hq >= 1:
                                    t.wait_ge(sm["rb"], hq)
                                t.matmul(den_ps[:, :], onec_bf[:, :], denvb[qg][:, 0:512],
                                         start=True, stop=False)
                                t.matmul(den_ps[:, :], onec_bf[:, :], denvb[qg][:, 512:1024],
                                         start=False, stop=True).then_inc(sm["pd"], 1)
                            else:
                                t.wait_ge(sm["rc"], hq + 1)
                                if hq >= 1:
                                    t.wait_ge(sm["rb"], hq)
                                t.matmul(rbc_ps[:, :], oner_f[:, :], r_sb[:, :],
                                         start=True, stop=True).then_inc(sm["pr"], 1)

                        pend = []
                        for h in range(H):
                            t.wait_ge(sm["ng"], h + 1)
                            t.wait_ge(sm["pfs0" if h % 2 == 0 else "pfs1"],
                                      64 * (h // 2 + 1))
                            for p in range(16):
                                for qg in range(2):
                                    g = eidx(h, qg, p)
                                    if p >= 1:
                                        t.wait_ge(sm["ex"], eidx(h, qg, p - 1) + 1)
                                    elif h >= 1:
                                        t.wait_ge(sm["ex"], eidx(h - 1, qg, 15) + 1)
                                    lm = None
                                    for c in (2 * p, 2 * p + 1):
                                        t.matmul(s_ps[qg][:, (c % 2) * 512:(c % 2) * 512 + 512],
                                                 khb[h % 2][:, c * 128:(c + 1) * 128],
                                                 qhb[h % 2][:, qg * 512:(qg + 1) * 512],
                                                 start=True, stop=False)
                                        lm = t.matmul(s_ps[qg][:, (c % 2) * 512:(c % 2) * 512 + 512],
                                                      sc128b[:, :],
                                                      negm[h % 2][:, qg * 512:(qg + 1) * 512],
                                                      start=False, stop=True)
                                    lm.then_inc(sm["sp"], 1)
                                for _ in range(2):
                                    if pend:
                                        den_rbc(*pend.pop(0))
                                for qg in range(2):
                                    if p >= 1:
                                        av_pair(h, qg, p - 1)
                            for qg in range(2):
                                av_pair(h, qg, 15)
                            pend += [(h, 0, 0), (h, 0, 1), (h, 1, 0), (h, 1, 1)]
                        for job in pend:
                            den_rbc(*job)
                        t.nop().then_inc(sm["bar"], 1)

                    @blk.vector
                    def _(v):
                        v.wait_ge(sm["bar"], 5)
                        v.wait_ge(sm["mk_ld"], 16)
                        v.reduce_max(mk2[:, :], kn4[:, :], axis=AX.X).then_inc(sm["mk_dve"], 1)
                        for rr in range(1, 4):
                            v.wait_ge(sm["mk_ld"], 16 * (rr + 1))
                            v.reduce_max(mk2b[:, :], kn4[:, :], axis=AX.X).then_inc(sm["mk_dve"], 1)
                            v.wait_ge(sm["mk_dve"], 2 * rr)
                            v.tensor_tensor(mk2[:, :], mk2[:, :], mk2b[:, :],
                                            ALU.max).then_inc(sm["mk_dve"], 1)
                        v.wait_ge(sm["mk_ld"], 96)
                        v.tensor_scalar(mk2n[:, :], mk2n[:, :], -0.5, None,
                                        ALU.mult).then_inc(sm["mk_dve"], 1)

                        def fin(h, qg):
                            hq = 2 * h + qg
                            v.wait_ge(sm["pd"], hq + 1)
                            if hq >= 1:
                                v.wait_ge(sm["pr"], hq)
                            v.reciprocal(r_sb[:, :], den_ps[:, :]).then_inc(sm["rc"], 1)
                            v.wait_ge(sm["pr"], hq + 1)
                            v.tensor_copy(rbc_sb[:, :], rbc_ps[:, :]).then_inc(sm["rb"], 1)
                            v.wait_ge(sm["rb"], hq + 1)
                            c0 = h * 1024 + qg * 512
                            accq = acc_ps[0] if qg == 0 else acc_ps[1 + h % 2]
                            v.tensor_tensor(attnT[:, c0:c0 + 512],
                                            accq[:, :], rbc_sb[:, :],
                                            ALU.mult).then_inc(sm["ev2"], 1)

                        for h in range(H):
                            v.wait_ge(sm["pfs0" if h % 2 == 0 else "pfs1"],
                                      64 * (h // 2 + 1))
                            if h >= 2:
                                v.wait_ge(sm["sp"], 32 * (h - 1))
                            if h == 0:
                                v.wait_ge(sm["mk_dve"], 8)
                            v.tensor_scalar(negm[h % 2][:, :], qnb[h % 2][:, :],
                                            -0.5, mk2n[:, h:h + 1],
                                            ALU.mult, ALU.add).then_inc(sm["ng"], 1)
                            if h >= 1:
                                fin(h - 1, 0)
                                fin(h - 1, 1)
                            for p in range(16):
                                g = eidx(h, 0, p)
                                v.wait_ge(sm["ex"], g + 1)
                                if p == 0:
                                    if h >= 1:
                                        v.wait_ge(sm["pd"], 2 * h - 1)
                                    v.tensor_copy(denv[0][:, :], ptb[0][p % 2][:, :]
                                                  ).then_inc(sm["dn"], 1)
                                else:
                                    v.wait_ge(sm["dn"], h * 16 + p)
                                    lastd = v.tensor_tensor(denv[0][:, :], denv[0][:, :],
                                                            ptb[0][p % 2][:, :], ALU.add)
                                    lastd.then_inc(sm["dn"], 1)
                                    if p == 15:
                                        v.wait_ge(sm["dn"], h * 16 + 16)
                                        v.tensor_copy(denvb[0][:, :], denv[0][:, :]
                                                      ).then_inc(sm["mk_pe"], 1)
                        fin(H - 1, 0)
                        fin(H - 1, 1)
                        v.nop().then_inc(sm["bar"], 1)

                    @blk.gpsimd
                    def _(g):
                        g.wait_ge(sm["bar"], 5)
                        if PHASES >= 3:
                            for og in range(2):
                                k = 24 + og
                                g.wait_ge(sm["pj_mm"], pjg_cnt(2, 6 + og, 7))
                                srcp = woT.ap().rearrange("(d p) e -> p d e", p=128)[
                                    :, :, og * EGW:(og + 1) * EGW]
                                dstp = panel[k % 2].ap().rearrange("p (d e) -> p d e", e=EGW)
                                g.dma_start(out=dstp, in_=srcp).then_inc(
                                    sm["pan0" if k % 2 == 0 else "pan1"], 16)
                        for h in range(H):
                            for p in range(16):
                                gg = eidx(h, 1, p)
                                g.wait_ge(sm["ex"], gg + 1)
                                if p == 0:
                                    if h >= 1:
                                        g.wait_ge(sm["pd"], 2 * h)
                                    g.tensor_copy(denv[1][:, :], ptb[1][p % 2][:, :]
                                                  ).then_inc(sm["pfs"], 1)
                                else:
                                    g.wait_ge(sm["pfs"], h * 16 + p)
                                    lastd = g.tensor_tensor(denv[1][:, :], denv[1][:, :],
                                                            ptb[1][p % 2][:, :], ALU.add)
                                    lastd.then_inc(sm["pfs"], 1)
                                    if p == 15:
                                        g.wait_ge(sm["pfs"], h * 16 + 16)
                                        g.tensor_copy(denvb[1][:, :], denv[1][:, :]
                                                      ).then_inc(sm["pfv"], 1)
                        if DEBUG:
                            g.wait_ge(sm["ev2"], 2 * H)
                            g.dma_start(out=dbg["d_at"][:, :], in_=attnT[:, :]
                                        ).then_inc(sm["dbg_st"], 16)
                            g.wait_ge(sm["dbg_st"], 16 * 6)
                        g.nop().then_inc(sm["bar"], 1)


                if PHASES >= 3:
                    with ExitStack() as ctx:
                        def sb(name, shape, dt):
                            return ctx.enter_context(nc.sbuf_tensor(name, shape, dt))

                        def ps(name, shape, dt):
                            return ctx.enter_context(nc.psum_tensor(name, shape, dt))

                        xog = [sb(f"xog{i}", [128, 8 * 512], F32) for i in range(2)]
                        piece = [sb(f"piece{i}", [128, 512], F32) for i in range(2)]
                        o_ps = [ps(f"o_ps{i}", [128, 512], F32) for i in range(2)]

                        blk = ctx.enter_context(nc.Block("p3"))

                        @blk.sync
                        def _(s):
                            s.wait_ge(sm["bar"], 10)
                            for og in range(NEG):
                                if og >= 2:
                                    s.wait_ge(sm["o_ev"], 8 * (og - 1))
                                s.dma_start(
                                    out=xog[og % 2].ap().rearrange("p (ts e) -> p ts e", e=EGW),
                                    in_=x_in.ap().rearrange("(ts p) e -> p ts e", p=128)[
                                        :, :, og * EGW:(og + 1) * EGW],
                                ).then_inc(sm["x3a" if og % 2 == 0 else "x3b"], 16)
                                for ts_ in range(8):
                                    idx = og * 8 + ts_
                                    s.wait_ge(sm["o_ev"], idx + 1)
                                    dst = x2loc.ap().rearrange("(ts p) e -> p ts e", p=128)[
                                        :, ts_:ts_ + 1, og * EGW:(og + 1) * EGW]
                                    s.dma_start(out=dst,
                                                in_=piece[idx % 2].ap().rearrange(
                                                    "p (one e) -> p one e", one=1)
                                                ).then_inc(sm["x2a" if idx % 2 == 0 else "x2b"], 16)
                            s.wait_ge(sm["x2a"], 16 * 32)
                            s.wait_ge(sm["x2b"], 16 * 32)
                            s.wait_ge(sm["x3a"], 16 * 4)
                            s.wait_ge(sm["x3b"], 16 * 4)
                            s.nop().then_inc(sm["bar"], 1)

                        @blk.tensor
                        def _(t):
                            t.wait_ge(sm["bar"], 10)
                            for og in range(NEG):
                                pan_w(t, 24 + og)
                                for ts_ in range(8):
                                    gi = og * 8 + ts_
                                    if gi >= 2:
                                        t.wait_ge(sm["o_ev"], gi - 1)
                                    lm = None
                                    for e in range(ND):
                                        lm = t.matmul(
                                            o_ps[gi % 2][:, :],
                                            attnT[:, e * 1024 + ts_ * 128:e * 1024 + (ts_ + 1) * 128],
                                            panel[(24 + og) % 2][:, e * EGW:(e + 1) * EGW],
                                            start=(e == 0), stop=(e == ND - 1))
                                    lm.then_inc(sm["o_mm"], 1)
                            t.nop().then_inc(sm["bar"], 1)

                        @blk.vector
                        def _(v):
                            v.wait_ge(sm["bar"], 10)
                            for og in range(NEG):
                                v.wait_ge(sm["x3a" if og % 2 == 0 else "x3b"],
                                          16 * (og // 2 + 1))
                                for ts_ in range(8):
                                    idx = og * 8 + ts_
                                    v.wait_ge(sm["o_mm"], idx + 1)
                                    if idx >= 2:
                                        v.wait_ge(sm["x2a" if idx % 2 == 0 else "x2b"],
                                                  16 * (idx // 2))
                                    v.tensor_tensor(piece[idx % 2][:, :],
                                                    o_ps[idx % 2][:, :],
                                                    xog[og % 2][:, ts_ * 512:(ts_ + 1) * 512],
                                                    ALU.add).then_inc(sm["o_ev"], 1)
                            v.nop().then_inc(sm["bar"], 1)

                        @blk.scalar
                        def _(sc):
                            sc.wait_ge(sm["bar"], 10)
                            sc.nop().then_inc(sm["bar"], 1)

                        @blk.gpsimd
                        def _(g):
                            g.wait_ge(sm["bar"], 10)
                            for og in range(2, NEG):
                                k = 24 + og
                                g.wait_ge(sm["o_mm"], 8 * (og - 1))
                                srcp = woT.ap().rearrange("(d p) e -> p d e", p=128)[
                                    :, :, og * EGW:(og + 1) * EGW]
                                dstp = panel[k % 2].ap().rearrange("p (d e) -> p d e", e=EGW)
                                g.dma_start(out=dstp, in_=srcp).then_inc(
                                    sm["pan0" if k % 2 == 0 else "pan1"], 16)
                            if PHASES >= 4:
                                for og in range(2):
                                    k = 32 + og
                                    g.wait_ge(sm["o_mm"], 8 * (og + 7))
                                    srcp = wfT.ap().rearrange("(d p) e -> p d e", p=128)[
                                        :, :, og * EGW:(og + 1) * EGW]
                                    dstp = panel[k % 2].ap().rearrange("p (d e) -> p d e", e=EGW)
                                    g.dma_start(out=dstp, in_=srcp).then_inc(
                                        sm["pan0" if k % 2 == 0 else "pan1"], 16)
                            if DEBUG:
                                g.wait_ge(sm["x2a"], 16 * 32)
                                g.wait_ge(sm["x2b"], 16 * 32)
                                g.dma_start(out=dbg["d_x2"][:, :], in_=x2loc[:, :]
                                            ).then_inc(sm["dbg_st"], 16)
                            g.nop().then_inc(sm["bar"], 1)

        if PHASES >= 4:
            with ExitStack() as p4:
                lnT2 = p4.enter_context(nc.sbuf_tensor("lnT2", [128, ND * 1024], BF16))
                normb = p4.enter_context(nc.sbuf_tensor("normb", [128, ND * EGW], BF16))

                with ExitStack() as ctx:
                    def sb(name, shape, dt):
                        return ctx.enter_context(nc.sbuf_tensor(name, shape, dt))

                    def ps(name, shape, dt):
                        return ctx.enter_context(nc.psum_tensor(name, shape, dt))

                    xt2 = sb("xt2", [128, D], F32)
                    lnbf2 = [sb(f"lnbf2_{i}", [128, D], BF16) for i in range(2)]
                    stats2 = [sb(f"stats2_{i}", [128, 48], F32) for i in range(2)]
                    mv2 = [sb(f"mv2_{i}", [128, 2], F32) for i in range(2)]
                    lnv2 = [sb(f"lnv2_{i}", [128, 1], F32) for i in range(2)]
                    rstd2 = [sb(f"rstd2_{i}", [128, 1], F32) for i in range(2)]
                    tp2_ps = [ps(f"tp2_ps{i}", [128, 128], BF16) for i in range(2)]

                    blk = ctx.enter_context(nc.Block("p4a"))

                    @blk.sync
                    def _(s):
                        s.wait_ge(sm["bar"], 15)
                        for i in range(NT):
                            if i >= 1:
                                s.wait_ge(sm["ln_apply"], NT + i)
                            s.dma_start(out=xt2[:, :],
                                        in_=x2loc[128 * i:128 * (i + 1), :]).then_inc(sm["x4"], 16)
                        s.wait_ge(sm["x4"], 16 * NT)
                        s.nop().then_inc(sm["bar"], 1)

                    @blk.vector
                    def _(v):
                        v.wait_ge(sm["bar"], 15)
                        for i in range(NT + 1):
                            if i < NT:
                                v.wait_ge(sm["x4"], 16 * (i + 1))
                                ins = None
                                for j in range(8):
                                    ins = v.bn_stats(stats2[i % 2][:, 6 * j:6 * j + 6],
                                                     xt2[:, 512 * j:512 * (j + 1)])
                                ins.then_inc(sm["ln_bn"], 1)
                                v.wait_ge(sm["ln_bn"], NT + i + 1)
                                v.bn_aggr(mv2[i % 2][:, :], stats2[i % 2][:, :]).then_inc(sm["ln_mv"], 1)
                                v.wait_ge(sm["ln_rstd"], 2 * NT + 2 * i + 2)
                                if i >= 2:
                                    v.wait_ge(sm["tp_pe"], NT * ND + 32 * (i - 1))
                                v.tensor_scalar(lnbf2[i % 2][:, :], xt2[:, :],
                                                mv2[i % 2][:, 0:1], rstd2[i % 2][:, :],
                                                ALU.subtract, ALU.mult).then_inc(sm["ln_apply"], 1)
                            if i >= 1:
                                ti = i - 1
                                for d in range(ND):
                                    j = ti * ND + d
                                    v.wait_ge(sm["tp_pe"], NT * ND + j + 1)
                                    v.tensor_copy(lnT2[:, d * 1024 + ti * 128:d * 1024 + (ti + 1) * 128],
                                                  tp2_ps[j % 2][:, :]).then_inc(sm["tp_ev"], 1)
                        v.nop().then_inc(sm["bar"], 1)

                    @blk.scalar
                    def _(sc):
                        sc.wait_ge(sm["bar"], 15)
                        for i in range(NT):
                            sc.wait_ge(sm["ln_mv"], NT + i + 1)
                            sc.activation(lnv2[i % 2][:, :], mv2[i % 2][:, 1:2], AF.Ln,
                                          scale=1.0, bias=csb[:, 0:1]).then_inc(sm["ln_rstd"], 1)
                            sc.wait_ge(sm["ln_rstd"], 2 * NT + 2 * i + 1)
                            sc.activation(rstd2[i % 2][:, :], lnv2[i % 2][:, :], AF.Exp,
                                          scale=-0.5, bias=csb[:, 1:2]).then_inc(sm["ln_rstd"], 1)
                        sc.nop().then_inc(sm["bar"], 1)

                    @blk.tensor
                    def _(t):
                        t.wait_ge(sm["bar"], 15)
                        for i in range(NT):
                            t.wait_ge(sm["ln_apply"], NT + i + 1)
                            for d in range(ND):
                                j = i * ND + d
                                if j >= 2:
                                    t.wait_ge(sm["tp_ev"], NT * ND + j - 1)
                                t.transpose(tp2_ps[j % 2][:, :],
                                            lnbf2[i % 2][:, 128 * d:128 * (d + 1)],
                                            eyeb[:, :]).then_inc(sm["tp_pe"], 1)
                        t.nop().then_inc(sm["bar"], 1)

                    @blk.gpsimd
                    def _(g):
                        g.wait_ge(sm["bar"], 15)
                        for hf in range(2):
                            srcn = normrep.ap().rearrange("(d p) e -> p d e", p=128)[
                                :, 16 * hf:16 * (hf + 1), 0:EGW]
                            dstn = normb.ap().rearrange("p (d e) -> p d e", e=EGW)[
                                :, 16 * hf:16 * (hf + 1), :]
                            g.dma_start(out=dstn, in_=srcn).then_inc(
                                sm["nrm" if hf == 0 else "x_ld"], 16)
                        g.nop().then_inc(sm["bar"], 1)

                with ExitStack() as ctx:
                    def sb(name, shape, dt):
                        return ctx.enter_context(nc.sbuf_tensor(name, shape, dt))

                    def ps(name, shape, dt):
                        return ctx.enter_context(nc.psum_tensor(name, shape, dt))

                    x2og = [sb(f"x2og{i}", [128, 8 * 512], F32) for i in range(2)]
                    biasog = [sb(f"biasog{i}", [128, 512], F32) for i in range(2)]
                    piece = [sb(f"fpiece{i}", [128, 512], F32) for i in range(2)]
                    f_ps = [ps(f"f_ps{i}", [128, 512], F32) for i in range(2)]

                    blk = ctx.enter_context(nc.Block("p4b"))

                    @blk.sync
                    def _(s):
                        s.wait_ge(sm["bar"], 20)
                        for og in range(NEG):
                            if og >= 2:
                                s.wait_ge(sm["f_ev"], 16 * (og - 1))
                            s.dma_start(
                                out=x2og[og % 2].ap().rearrange("p (ts e) -> p ts e", e=EGW),
                                in_=x2loc.ap().rearrange("(ts p) e -> p ts e", p=128)[
                                    :, :, og * EGW:(og + 1) * EGW],
                            ).then_inc(sm["xg4a" if og % 2 == 0 else "xg4b"], 16)
                            s.dma_start(out=biasog[og % 2][:, :],
                                        in_=bias_rep[:, og * EGW:(og + 1) * EGW]
                                        ).then_inc(sm["bl4a" if og % 2 == 0 else "bl4b"], 16)
                            for ts_ in range(8):
                                idx = og * 8 + ts_
                                s.wait_ge(sm["f_ev"], 2 * (idx + 1))
                                dst = out_ext.ap().rearrange("(ts p) e -> p ts e", p=128)[
                                    :, ts_:ts_ + 1, og * EGW:(og + 1) * EGW]
                                s.dma_start(out=dst,
                                            in_=piece[idx % 2].ap().rearrange(
                                                "p (one e) -> p one e", one=1)
                                            ).then_inc(sm["o4a" if idx % 2 == 0 else "o4b"], 16)
                        s.nop().then_inc(sm["bar"], 1)

                    @blk.gpsimd
                    def _(g):
                        g.wait_ge(sm["bar"], 20)
                        for og in range(NEG):
                            if og >= 2:
                                k = 32 + og
                                g.wait_ge(sm["f_mm"], 8 * (og - 1))
                                srcp = wfT.ap().rearrange("(d p) e -> p d e", p=128)[
                                    :, :, og * EGW:(og + 1) * EGW]
                                dstp = panel[k % 2].ap().rearrange("p (d e) -> p d e", e=EGW)
                                g.dma_start(out=dstp, in_=srcp).then_inc(
                                    sm["pan0" if k % 2 == 0 else "pan1"], 16)
                            if og >= 1:
                                for hf in range(2):
                                    g.wait_ge(sm["dq"], 2 * (og - 1) + hf + 1)
                                    srcn = normrep.ap().rearrange("(d p) e -> p d e", p=128)[
                                        :, 16 * hf:16 * (hf + 1), og * EGW:(og + 1) * EGW]
                                    dstn = normb.ap().rearrange("p (d e) -> p d e", e=EGW)[
                                        :, 16 * hf:16 * (hf + 1), :]
                                    g.dma_start(out=dstn, in_=srcn).then_inc(
                                        sm["nrm" if hf == 0 else "x_ld"], 16)
                        g.wait_ge(sm["o4a"], 16 * 32)
                        g.wait_ge(sm["o4b"], 16 * 32)
                        g.wait_ge(sm["bar"], 24)
                        g.nop().then_inc(sm["bar"], 1)

                    @blk.vector
                    def _(v):
                        v.wait_ge(sm["bar"], 20)
                        for og in range(NEG):
                            k = 32 + og
                            pan_w(v, k)
                            if og >= 2:
                                v.wait_ge(sm["f_mm"], 8 * (og - 1))
                            for hf in range(2):
                                v.wait_ge(sm["nrm" if hf == 0 else "x_ld"],
                                          16 * (og + 1))
                                v.tensor_tensor(panel[k % 2][:, 8192 * hf:8192 * (hf + 1)],
                                                panel[k % 2][:, 8192 * hf:8192 * (hf + 1)],
                                                normb[:, 8192 * hf:8192 * (hf + 1)],
                                                ALU.mult).then_inc(sm["dq"], 1)
                            v.wait_ge(sm["xg4a" if og % 2 == 0 else "xg4b"],
                                      16 * (og // 2 + 1))
                            v.wait_ge(sm["bl4a" if og % 2 == 0 else "bl4b"],
                                      16 * (og // 2 + 1))
                            for ts_ in range(8):
                                idx = og * 8 + ts_
                                v.wait_ge(sm["f_mm"], idx + 1)
                                if idx >= 2:
                                    v.wait_ge(sm["o4a" if idx % 2 == 0 else "o4b"],
                                              16 * (idx // 2))
                                v.tensor_tensor(piece[idx % 2][:, :], f_ps[idx % 2][:, :],
                                                biasog[og % 2][:, :], ALU.add
                                                ).then_inc(sm["f_ev"], 1)
                                v.wait_ge(sm["f_ev"], 2 * idx + 1)
                                v.tensor_tensor(piece[idx % 2][:, :], piece[idx % 2][:, :],
                                                x2og[og % 2][:, ts_ * 512:(ts_ + 1) * 512],
                                                ALU.add).then_inc(sm["f_ev"], 1)
                        v.nop().then_inc(sm["bar"], 1)

                    @blk.tensor
                    def _(t):
                        t.wait_ge(sm["bar"], 20)
                        for og in range(NEG):
                            t.wait_ge(sm["dq"], 2 * og + 1)
                            k = 32 + og
                            for ts_ in range(8):
                                gi = og * 8 + ts_
                                if gi >= 2:
                                    t.wait_ge(sm["f_ev"], 2 * (gi - 1))
                                lm = None
                                for d in range(ND):
                                    if ts_ == 0 and d == 16:
                                        t.wait_ge(sm["dq"], 2 * (og + 1))
                                    lm = t.matmul(
                                        f_ps[gi % 2][:, :],
                                        lnT2[:, d * 1024 + ts_ * 128:d * 1024 + (ts_ + 1) * 128],
                                        panel[k % 2][:, d * EGW:(d + 1) * EGW],
                                        start=(d == 0), stop=(d == ND - 1))
                                lm.then_inc(sm["f_mm"], 1)
                            t.nop().then_inc(sm["bar"], 1) if og == NEG - 1 else None

                    @blk.scalar
                    def _(sc):
                        sc.wait_ge(sm["bar"], 20)
                        sc.nop().then_inc(sm["bar"], 1)
        if PHASES < 4:
            ndbg = 5 + (1 if PHASES >= 2 else 0) + (1 if PHASES >= 3 else 0)
            with ExitStack() as ctx:
                blk = ctx.enter_context(nc.Block("fin"))

                @blk.gpsimd
                def _(g):
                    if DEBUG:
                        g.wait_ge(sm["dbg_st"], 16 * ndbg)
                    g.wait_ge(sm["bar"], 5 * PHASES)

    return nc, dbg


def _host_layout(L0_attn_x, L0_attn_w_q, L0_attn_w_k, L0_attn_w_v, L0_attn_w_o,
                 L0_ffn_weight_q, L0_ffn_weight_norm, L0_ffn_bias):
    x = np.ascontiguousarray(np.asarray(L0_attn_x).astype(np.float32, copy=False).reshape(NC * T, D))
    wqT_ = np.ascontiguousarray(np.asarray(L0_attn_w_q).astype(np.float32, copy=False).T)
    wkT_ = np.ascontiguousarray(np.asarray(L0_attn_w_k).astype(np.float32, copy=False).T)
    wvT_ = np.ascontiguousarray(np.asarray(L0_attn_w_v).astype(np.float32, copy=False).T)
    woT_ = np.ascontiguousarray(np.asarray(L0_attn_w_o).astype(np.float32, copy=False).T)
    wfT_ = np.ascontiguousarray(np.asarray(L0_ffn_weight_q).astype(np.int8, copy=False).reshape(D, D).T)
    nrm = np.asarray(L0_ffn_weight_norm).astype(np.float16, copy=False).reshape(D, D // 16)
    normrep_ = np.ascontiguousarray(np.repeat(nrm.T, 16, axis=0))
    bias_rep_ = np.ascontiguousarray(
        np.tile(np.asarray(L0_ffn_bias).astype(np.float32, copy=False)[None, :], (128, 1)))
    base = {
        "wqT": wqT_, "wkT": wkT_, "wvT": wvT_, "woT": woT_,
        "wfT": wfT_, "normrep": normrep_, "bias_rep": bias_rep_,
        "ones_1x128_bf": np.ones((1, 128), ml_dtypes.bfloat16),
        "ones_128x1_bf": np.ones((128, 1), ml_dtypes.bfloat16),
        "ones_128x1_f": np.ones((128, 1), np.float32),
        "ones_1x128_f": np.ones((1, 128), np.float32),
        "eye_bf": np.eye(128).astype(ml_dtypes.bfloat16),
        "sc128": np.full((128, 128), 1.0 / 128.0, ml_dtypes.bfloat16),
        "eye_f": np.eye(128, dtype=np.float32),
        "consts": np.tile(np.array([[1e-5, -np.log(15.0), 0.0, 0.0]], np.float32), (128, 1)),
    }
    in_maps = []
    for c in range(NC):
        m = dict(base)
        m["x_in"] = np.ascontiguousarray(x[c * T:(c + 1) * T])
        in_maps.append(m)
    return in_maps


def kernel(**inputs):
    in_maps = _host_layout(**inputs)
    nc, _ = build_nc()
    res = run_bass_kernel_spmd(nc, in_maps, core_ids=list(range(NC)))
    out = np.concatenate([np.asarray(r["out"]) for r in res.results], axis=0)
    return out.reshape(2, S, D).astype(np.float32)
